# version 06
# baseline (speedup 1.0000x reference)
import numpy as np
import concourse.bass as bass
import concourse.mybir as mybir
from concourse.bass_utils import run_bass_kernel_spmd
import concourse.dve_ops as dve_ops
from concourse.dve_op import DveOp
from concourse.dve_spec import (
    Spec, Src0, Src1, C0, C1, C2, C3, One, Bin, AluOp, eq, lower as dve_lower,
)
from concourse.dve_table_gen import dve_ver_for

F32 = mybir.dt.float32
BF16 = mybir.dt.bfloat16
ALU = mybir.AluOpType
AX = mybir.AxisListType
AF = mybir.ActivationFunctionType

N = 2048
E = 2048
NT = N // 128
EC = E // 128
H = 16
QKVF = 2304
QC = QKVF // 128
I = 8192
OSTRIP = 512
NSTRIP = I // OSTRIP
HT = NT // 2
SM = 1.0 / float(np.sqrt(np.float32(128)))



def _cdt(a):
    return np.float64 if a.dtype == np.float64 else np.float32


def _ref_ik(in0, in1, s0, s1, imm2):
    f = _cdt(in0)
    a2 = in0.astype(f) * in0
    return (1.0 - (a2 >= s0) * imm2) * (1.0 - (a2 >= s1) * imm2)


def _ref_ex3(in0, in1, s0, s1, imm2):
    f = _cdt(in0)
    z = in0.astype(f) + in1
    b = (z >= s0).astype(f)
    z1 = z - b * s0
    b1 = (z1 >= s1).astype(f)
    return z1 - b1 * s1


def _ref_tie(in0, in1, s0, s1, imm2):
    f = _cdt(in0)
    z0 = in0.astype(f)
    b0 = (z0 >= 1.0).astype(f)
    fr = z0 - b0
    c1 = (fr > s1).astype(f)
    e = (fr == s1).astype(f)
    return in1 + ((c1 - fr) + e * b0)


def _ref_unf(in0, in1, s0, s1, imm2):
    f = _cdt(in0)
    a2 = in0.astype(f) * in0
    kk = (1.0 + (a2 >= s0)) * (1.0 + (a2 >= s1))
    return (in1 - imm2) * kk


def _mxq_specs():
    u2a = Src0 * Src0
    ik_body = (One - (u2a >= C0) * C2) * (One - (u2a >= C1) * C2)
    zs = Src0 + C3
    bb = zs >= C0
    zz1 = zs - bb * C0
    bb1 = zz1 >= C1
    ex3_body = zz1 - bb1 * C1
    b0 = Bin(AluOp.IS_GE, Src0, One)
    fr = Src0 - b0
    c1 = Bin(AluOp.IS_LT, C1, fr)
    e = eq(fr, C1)
    tie_body = Src1 + ((c1 - fr) + (e & b0))
    u2b = Src0 * Src0
    kk = (One + (u2b >= C0)) * (One + (u2b >= C1))
    unf_body = (Src1 - C2) * kk
    return {
        "MXQ_IK": (Spec(body=ik_body), _ref_ik),
        "MXQ_EX3": (Spec(body=ex3_body), _ref_ex3),
        "MXQ_TIE": (Spec(body=tie_body), _ref_tie),
        "MXQ_UNF": (Spec(body=unf_body), _ref_unf),
    }


_MXQ = {}


def _register_mxq():
    if _MXQ:
        return
    if any(op.name == "MXQ_IK" for op in dve_ops.OPS):
        for op in dve_ops.OPS:
            if op.name.startswith("MXQ_"):
                _MXQ[op.name] = op
        return
    ver = dve_ver_for("TRN2")
    replaceable = ["DVEHASH32L_SEED", "DVEHASH32L_BODY", "DVEHASH32L_DRAIN"]
    for name, (spec, ref) in _mxq_specs().items():
        compiled = dve_lower(spec, ver=ver, perf_en=False, name=name)
        op = DveOp(name, spec, subdim=False, uops_sha={ver: compiled.sha(ver)},
                   reference=ref)
        if max(dve_ops._SUB_OPCODE_FOR_NAME.values()) < 0x1F:
            row = max(dve_ops._SUB_OPCODE_FOR_NAME.values()) + 1
            dve_ops.OPS.append(op)
        else:
            victim = replaceable.pop()
            idx = next(i for i, o in enumerate(dve_ops.OPS) if o.name == victim)
            row = dve_ops._SUB_OPCODE_FOR_NAME.pop(victim)
            dve_ops.OPS[idx] = op
            dve_ops.CUSTOM_DVE_SPECS.pop(victim, None)
            dve_ops.CUSTOM_DVE_REFS.pop(victim, None)
        dve_ops._SUB_OPCODE_FOR_NAME[name] = row
        dve_ops.CUSTOM_DVE_SPECS[name] = spec
        dve_ops.CUSTOM_DVE_REFS[name] = op.reference
        _MXQ[name] = op



DMA_ENGS = {"S", "G"}
ALL_ENGS = ("S", "G", "T", "V", "A")


class Sched:
    DRAIN_DIST = 3

    def __init__(self):
        self.steps = []
        self.lw = {}
        self.rd = {}

    def step(self, eng, fn, reads=(), writes=(), ndma=0, tag="", flush=False):
        i = len(self.steps)
        st = {"eng": eng, "fn": fn, "deps": set(), "ndma": ndma, "tag": tag,
              "flush": flush}
        self.steps.append(st)
        for r in reads:
            for j in self.lw.get(r, ()):
                st["deps"].add(j)
            self.rd.setdefault(r, []).append(i)
        for w in writes:
            for j in self.lw.get(w, ()):
                st["deps"].add(j)
            readers = self.rd.get(w, ())
            for j in readers:
                if j != i:
                    st["deps"].add(j)
            if readers:
                self.lw[w] = [i]
            else:
                self.lw.setdefault(w, []).append(i)
            self.rd[w] = []
        return i

    def barrier(self, tag="bar"):
        toks = [f"_{tag}_{e}" for e in ALL_ENGS]
        for e, t in zip(ALL_ENGS, toks):
            self.step(e, lambda h: None, writes=[t], tag=f"{tag}.mark.{e}",
                      flush=(e in DMA_ENGS))
        for e in ALL_ENGS:
            self.step(e, lambda h: None, reads=list(toks), tag=f"{tag}.wait.{e}")

    def emit(self, nc, block, sems):
        steps = self.steps
        n = len(steps)
        ords = {}
        cnt = {e: 0 for e in ALL_ENGS}
        for i, st in enumerate(steps):
            ords[i] = cnt[st["eng"]]
            cnt[st["eng"]] += 1
        needed = [False] * n
        drain_before = [False] * n
        for i, st in enumerate(steps):
            for j in st["deps"]:
                if steps[j]["eng"] == st["eng"]:
                    if (ords[i] - ords[j] < self.DRAIN_DIST
                            and st["eng"] in ("V", "A", "G")):
                        drain_before[i] = True
                else:
                    needed[j] = True
        K = 6
        ev = [None] * n
        run = {e: 0 for e in ALL_ENGS}
        dma_ord = {e: 0 for e in DMA_ENGS}
        sem_tot = {(e, k): 0 for e in DMA_ENGS for k in range(K)}
        self_wait = [None] * n
        for i, st in enumerate(steps):
            e = st["eng"]
            if e in DMA_ENGS and st["ndma"]:
                k = dma_ord[e] % K
                dma_ord[e] += 1
                self_wait[i] = (k, sem_tot[(e, k)])
                sem_tot[(e, k)] += 16 * st["ndma"]
                ev[i] = (e, k, sem_tot[(e, k)])
            else:
                if needed[i]:
                    run[e] += 1
                ev[i] = (e, "C" if e in DMA_ENGS else None, run[e])
            if st.get("flush"):
                st["flush_vals"] = tuple(sem_tot[(e, k)] for k in range(K)) \
                    if e in DMA_ENGS else None
        totals = {e: [sem_tot[(e, k)] for k in range(K)] for e in DMA_ENGS}
        waits = [[] for _ in range(n)]
        for i, st in enumerate(steps):
            for j in st["deps"]:
                pe = steps[j]["eng"]
                pj, kj, vj = ev[j]
                if pe == st["eng"]:
                    if pe in DMA_ENGS and kj != "C":
                        waits[i].append((pe, kj, vj))
                    continue
                if pe in DMA_ENGS and kj != "C":
                    waits[i].append((pe, kj, vj))
                elif needed[j]:
                    waits[i].append((pe, kj, vj))
        seen = {}
        pruned = [[] for _ in range(n)]
        for i, st in enumerate(steps):
            e = st["eng"]
            best = {}
            for (pe, k2, v) in waits[i]:
                key = (pe, k2)
                best[key] = max(best.get(key, 0), v)
            for (pe, k2), v in sorted(best.items(), key=lambda kv: (kv[0][0], str(kv[0][1]))):
                gk = (e, pe, k2)
                if v == 0 or seen.get(gk, -1) >= v:
                    continue
                seen[gk] = v
                pruned[i].append((pe, k2, v))

        def semof(pe, k2):
            if k2 is None:
                return sems[pe]
            return sems[pe + ("C" if k2 == "C" else str(k2))]

        def runner(eng):
            def body(h):
                inc_since_wait = [False]
                ninc = [0]
                for i, st in enumerate(steps):
                    if st["eng"] != eng:
                        continue
                    if pruned[i]:
                        inc_since_wait[0] = False
                    for (pe, k2, v) in pruned[i]:
                        h.wait_ge(semof(pe, k2), v)
                    if drain_before[i]:
                        h.drain()
                    if st.get("flush") and st.get("flush_vals"):
                        for k2f, vf in enumerate(st["flush_vals"]):
                            if vf:
                                h.wait_ge(semof(eng, k2f), vf)
                        inc_since_wait[0] = False
                    if eng in DMA_ENGS and st["ndma"]:
                        k2, v0 = self_wait[i]
                        if v0:
                            h.wait_ge(semof(eng, k2), v0)
                        last = st["fn"](h, semof(eng, k2))
                    else:
                        last = st["fn"](h)
                        if needed[i]:
                            csem = sems[eng + "C"] if eng in DMA_ENGS else sems[eng]
                            if inc_since_wait[0]:
                                h.wait_ge(csem, ninc[0])
                            if eng == "T":
                                if last is not None:
                                    last.then_inc(csem, 1)
                                else:
                                    h.nop().then_inc(csem, 1)
                            else:
                                h.drain()
                                h.nop().then_inc(csem, 1)
                            ninc[0] += 1
                            inc_since_wait[0] = True
                if eng in DMA_ENGS:
                    for k2 in range(K):
                        if totals[eng][k2]:
                            h.wait_ge(semof(eng, k2), totals[eng][k2])
            return body

        block.sync(runner("S"))
        block.gpsimd(runner("G"))
        block.tensor(runner("T"))
        block.vector(runner("V"))
        block.scalar(runner("A"))




def emit_scale_chain(v, amax_ap, zt, ft, ft2, invs_ap, sca_ap, pre, js,
                     post=1.0):
    v.tensor_scalar_mul(zt, amax_ap, 2.0**pre)
    v.memset(invs_ap, 2.0**(pre - 1))
    for j in js:
        v.drain()
        v.tensor_single_scalar(ft, zt, 8.0 * 2.0**j, ALU.is_lt)
        v.tensor_scalar(out=ft2, in0=zt, scalar1=8.0 * 2.0**j,
                        scalar2=2.0**-j, op0=ALU.is_ge, op1=ALU.mult)
        v.drain()
        v.tensor_add(ft, ft, ft2)
        v.drain()
        v.tensor_mul(zt, zt, ft)
        v.tensor_mul(invs_ap, invs_ap, ft)
    v.drain()
    v.reciprocal(sca_ap, invs_ap)
    if post != 1.0:
        v.drain()
        v.tensor_scalar_mul(invs_ap, invs_ap, post)




CH_FULL = (64, 32, 16, 8, 4, 2, 1)
CH_STD = (16, 8, 4, 2, 1)
CH_INT = (32, 16, 8, 4, 2, 1)




def build_program():
    _register_mxq()
    nc = bass.Bass("TRN2", target_bir_lowering=False)

    x_h = nc.dram_tensor("x", [N, E], F32, kind="ExternalInput").ap()
    wqkv_h = nc.dram_tensor("w_qkv", [E, QKVF], F32, kind="ExternalInput").ap()
    bqkv_h = nc.dram_tensor("b_qkv", [QKVF], F32, kind="ExternalInput").ap()
    wfc_h = nc.dram_tensor("w_fc", [E, E], F32, kind="ExternalInput").ap()
    bfc_h = nc.dram_tensor("b_fc", [E], F32, kind="ExternalInput").ap()
    wg_h = nc.dram_tensor("w_gate", [I, E], F32, kind="ExternalInput").ap()
    wu_h = nc.dram_tensor("w_up", [I, E], F32, kind="ExternalInput").ap()
    wd_h = nc.dram_tensor("w_down", [E, I], F32, kind="ExternalInput").ap()
    ident_h = nc.dram_tensor("ident", [128, 128], BF16, kind="ExternalInput").ap()
    cmask_h = nc.dram_tensor("cmask", [128, 128], F32, kind="ExternalInput").ap()
    out_h = nc.dram_tensor("out", [N, E], F32, kind="ExternalOutput").ap()

    ares_h = nc.dram_tensor("ares_scr", [N, E], F32).ap()
    wgq_h = nc.dram_tensor("wgq_c", [I, E], BF16).ap()
    wuq_h = nc.dram_tensor("wuq_c", [I, E], BF16).ap()
    wdq_h = nc.dram_tensor("wdq_c", [E, I], BF16).ap()

    live = {}

    def salloc(name, shape, dtype, side=None):
        g = nc.sbuf_tensor(name, shape, dtype, side=side)
        t = g.__enter__()
        live[name] = g
        return t

    def palloc(name, shape, dtype=F32):
        g = nc.psum_tensor(name, shape, dtype)
        t = g.__enter__()
        live[name] = g
        return t

    def free(*names):
        for nm in names:
            live.pop(nm).__exit__(None, None, None)

    semg = []
    SEMS = {}
    for e in ("T", "V", "A"):
        g = nc.semaphore("sem" + e)
        SEMS[e] = g.__enter__()
        semg.append(g)
    for e in ("S", "G"):
        g = nc.semaphore(f"sem{e}C")
        SEMS[e + "C"] = g.__enter__()
        semg.append(g)
        for k in range(6):
            g = nc.semaphore(f"sem{e}{k}")
            SEMS[e + str(k)] = g.__enter__()
            semg.append(g)

    sc = Sched()

    ident = salloc("identt", [128, 128], BF16)
    cmask = salloc("cmaskt", [128, 128], F32)
    shift0 = salloc("shift0t", [128, 1], F32)
    sh16 = salloc("sh16", [128, 1], F32)
    sh4 = salloc("sh4", [128, 1], F32)
    bq_sb = salloc("bq_sb", [128, QC], F32)
    sm_all = salloc("sm_all", [128, NT], F32)
    sq_all = salloc("sq_all", [128, NT], F32)
    mn_all = salloc("mn_all", [128, NT], F32)
    va_all = salloc("va_all", [128, NT], F32)
    t_all = salloc("t_all", [128, NT], F32)
    sd_all = salloc("sd_all", [128, NT], F32)
    rs_all = salloc("rs_all", [128, NT], F32)
    xn_st = salloc("xn_st", [128, E], F32)
    trp_t = palloc("trp", [128, 2 * 1024], BF16)
    trp = trp_t

    def st_setup(s, sem):
        s.dma_start(out=ident[:, :], in_=ident_h[:, :]).then_inc(sem, 16)
        s.dma_start(out=cmask[:, :], in_=cmask_h[:, :]).then_inc(sem, 16)
        for c in range(QC):
            s.dma_start(out=bq_sb[:, c:c + 1],
                        in_=bqkv_h[c * 128:(c + 1) * 128]
                        .rearrange("(p o) -> p o", o=1)).then_inc(sem, 16)

    sc.step("S", st_setup, writes=["ident", "cmask", "bq_sb"], ndma=2 + QC)
    def v_sh(v):
        v.memset(shift0[:, :], 0.0)
        v.memset(sh16[:, :], 16.0)
        v.memset(sh4[:, :], 4.0)
    sc.step("V", v_sh, writes=["shifts"])

    xstg = salloc("xstg", [128, 2 * E], F32)
    junk = salloc("junk", [128, E], F32)
    xn_fm = salloc("xn_fm", [128, EC * N], BF16)
    xnbf = salloc("xnbf", [128, E], BF16)

    def xs(i):
        return xstg[:, (i % 2) * E:(i % 2 + 1) * E]

    def load_x(T, slot, tag):
        sc.step("S", (lambda s, sem, T=T, slot=slot:
                      s.dma_start(out=xs(slot), in_=x_h[T * 128:(T + 1) * 128, :])
                      .then_inc(sem, 16)),
                writes=[f"xstg{slot % 2}"], ndma=1, tag=tag)

    for T in range(NT):
        load_x(T, T, f"ph0a.x{T}")
        sc.step("A", (lambda a, T=T: a.activation(out=junk[:, :], in_=xs(T),
                                                  func=AF.Square,
                                                  accum_out=sq_all[:, T:T + 1])),
                reads=[f"xstg{T % 2}"], writes=["sq_all", "junk"], tag=f"ph0a.sq{T}")
        sc.step("V", (lambda v, T=T: v.reduce_sum(sm_all[:, T:T + 1], xs(T), axis=AX.X)),
                reads=[f"xstg{T % 2}"], writes=["sm_all"], tag=f"ph0a.sm{T}")

    def st_stats(v):
        v.tensor_scalar_mul(mn_all[:, :], sm_all[:, :], 1.0 / E)
        v.tensor_scalar_mul(va_all[:, :], sq_all[:, :], 1.0 / E)
        v.drain()
        v.tensor_mul(t_all[:, :], mn_all[:, :], mn_all[:, :])
        v.drain()
        v.tensor_sub(va_all[:, :], va_all[:, :], t_all[:, :])
        v.drain()
        v.tensor_scalar_add(va_all[:, :], va_all[:, :], 1e-5)

    sc.step("V", st_stats, reads=["sm_all", "sq_all"],
            writes=["mn_all", "va_all", "t_all"], tag="stats")
    sc.step("A", lambda a: a.sqrt(sd_all[:, :], va_all[:, :]), reads=["va_all"],
            writes=["sd_all"], tag="sqrt")
    sc.step("V", lambda v: v.reciprocal(rs_all[:, :], sd_all[:, :]), reads=["sd_all"],
            writes=["rs_all"], tag="rstd")

    def emit_xn(v, slot, T):
        v.tensor_scalar(out=xn_st[:, :], in0=xs(slot), scalar1=mn_all[:, T:T + 1],
                        scalar2=rs_all[:, T:T + 1], op0=ALU.subtract, op1=ALU.mult)

    def tr16(src, dst, dstw, coloff_fn, tag, src_res, dst3d=None):
        for b in range(4):
            bank = b % 2

            def trb(t, b=b, bank=bank):
                last = None
                for j in range(4):
                    c = 4 * b + j
                    last = t.transpose(trp[:, bank * 1024 + j * 128: bank * 1024 + (j + 1) * 128],
                                       src[:, c * 128:(c + 1) * 128], ident[:, :])
                return last
            sc.step("T", trb, reads=list(src_res) + ["ident"], writes=[f"trp{bank}"],
                    tag=f"{tag}.tr{b}")

            if dst3d is not None:
                def cpb(v, b=b, bank=bank):
                    ap3, off = dst3d
                    v.tensor_copy(out=ap3[:, 4 * b:4 * b + 4, off:off + 128],
                                  in_=trp[:, bank * 1024: bank * 1024 + 512]
                                  .rearrange("p (j f) -> p j f", f=128))
            else:
                def cpb(v, b=b, bank=bank):
                    for j in range(4):
                        c = 4 * b + j
                        o = coloff_fn(c)
                        v.tensor_copy(out=dst[:, o:o + 128],
                                      in_=trp[:, bank * 1024 + j * 128: bank * 1024 + (j + 1) * 128])
            sc.step("V", cpb, reads=[f"trp{bank}"], writes=[dstw], tag=f"{tag}.cp{b}")

    for T in range(NT):
        load_x(T, T, f"ph0b.x{T}")
        sc.step("V", (lambda v, T=T: emit_xn(v, T, T)),
                reads=[f"xstg{T % 2}", "mn_all", "rs_all"], writes=["xn_st"],
                tag=f"ph0b.xn{T}")
        sc.step("A", (lambda a: a.activation(out=xnbf[:, :], in_=xn_st[:, :],
                                             func=AF.Copy)),
                reads=["xn_st"], writes=["xnbf"],
                tag=f"ph0b.cast{T}")
        tr16(xnbf, xn_fm, "xn_fm", None, f"ph0b.t{T}", ["xnbf"],
             dst3d=(xn_fm[:, :].rearrange("p (c n) -> p c n", n=N), T * 128))

    q_fm = salloc("q_fm", [128, H * N], BF16, side="right")
    k_fm = salloc("k_fm", [128, N], BF16, side="right")
    vstage = salloc("vstage", [128, N], BF16, side="right")
    v_tok = salloc("v_tok", [128, N], BF16, side="right")
    wpan = salloc("wpan", [128, 2 * E], BF16)
    qk_ps = palloc("qk_ps", [128, 2 * 512])

    for po in range(QC):
        sc.step("G", (lambda g, sem, po=po:
                      g.dma_start(out=wpan[:, (po % 2) * E:(po % 2 + 1) * E]
                                  .rearrange("p (c f) -> p c f", f=128),
                                  in_=wqkv_h[:, po * 128:(po + 1) * 128]
                                  .rearrange("(c p) f -> p c f", p=128))
                      .then_inc(sem, 16)),
                writes=[f"wpan{po % 2}"], ndma=1, tag=f"ph1.w{po}")
        for sp in range(4):
            idx = po * 4 + sp

            def mm16(t, po=po, sp=sp, idx=idx):
                last = None
                for kc in range(EC):
                    last = t.matmul(
                        qk_ps[:, (idx % 2) * 512:(idx % 2 + 1) * 512],
                        wpan[:, (po % 2) * E + kc * 128:(po % 2) * E + (kc + 1) * 128],
                        xn_fm[:, kc * N + sp * 512: kc * N + (sp + 1) * 512],
                        start=(kc == 0), stop=(kc == EC - 1))
                return last
            sc.step("T", mm16, reads=[f"wpan{po % 2}", "xn_fm"],
                    writes=[f"qk_ps{idx % 2}"], tag=f"ph1.mm{po}.{sp}")

            def evac(a, po=po, sp=sp, idx=idx):
                src = qk_ps[:, (idx % 2) * 512:(idx % 2 + 1) * 512]
                if po < 16:
                    dst = q_fm[:, po * N + sp * 512: po * N + (sp + 1) * 512]
                elif po == 16:
                    dst = k_fm[:, sp * 512:(sp + 1) * 512]
                else:
                    dst = vstage[:, sp * 512:(sp + 1) * 512]
                a.activation(out=dst, in_=src, func=AF.Identity, bias=bq_sb[:, po:po + 1])
            wres = "q_fm" if po < 16 else ("k_fm" if po == 16 else "vstage")
            sc.step("A", evac, reads=[f"qk_ps{idx % 2}", "bq_sb"], writes=[wres],
                    tag=f"ph1.ev{po}.{sp}")

    tr16(vstage, v_tok, "v_tok", (lambda c: c * 128), "ph1v", ["vstage"])

    sc.barrier("b12")
    free("qk_ps", "wpan", "xnbf", "xn_fm", "junk")

    attn_fm = salloc("attn_fm", [128, EC * N], BF16)
    probs_sb = salloc("probs_sb", [128, 2 * N], F32)
    probs_bf = salloc("probs_bf", [128, 2 * N], BF16)
    probsT = salloc("probsT", [128, 2 * N], BF16)
    rm2 = salloc("rm2", [128, 2], F32)
    nb2 = salloc("nb2", [128, 2], F32)
    rsum2 = salloc("rsum2", [128, 2], F32)
    rinv2 = salloc("rinv2", [128, 2], F32)
    sc_ps = palloc("sc_ps", [128, N])
    av_ps = palloc("av_ps", [128, 128])

    for T in range(NT):
        L = (T + 1) * 128
        nb4 = (T + 4) // 4
        for h in range(H):
            hs = h % 2

            def mm_sc(t, T=T, h=h, L=L, nb4=nb4):
                last = None
                for c in range(nb4):
                    w = min(512, L - c * 512)
                    last = t.matmul(sc_ps[:, c * 512:c * 512 + w],
                                    q_fm[:, h * N + T * 128: h * N + (T + 1) * 128],
                                    k_fm[:, c * 512:c * 512 + w],
                                    start=True, stop=True)
                return last
            sc.step("T", mm_sc, reads=["q_fm", "k_fm"], writes=["sc_ps"],
                    tag=f"ph2.sc{T}.{h}")

            def pre(v, T=T, L=L, hs=hs):
                v.tensor_add(sc_ps[:, T * 128:(T + 1) * 128],
                             sc_ps[:, T * 128:(T + 1) * 128], cmask[:, :])
                v.drain()
                v.reduce_max(rm2[:, hs:hs + 1], sc_ps[:, 0:L], axis=AX.X)
                v.drain()
                v.tensor_scalar_mul(nb2[:, hs:hs + 1], rm2[:, hs:hs + 1], -SM)
            sc.step("V", pre, reads=["sc_ps", "cmask"],
                    writes=["sc_ps", f"rm{hs}"], tag=f"ph2.pre{T}.{h}")

            def expo(a, L=L, hs=hs):
                a.activation(out=probs_sb[:, hs * N: hs * N + L], in_=sc_ps[:, 0:L],
                             func=AF.Exp, scale=SM, bias=nb2[:, hs:hs + 1],
                             accum_out=rsum2[:, hs:hs + 1])
            sc.step("A", expo, reads=["sc_ps", f"rm{hs}"],
                    writes=[f"probs{hs}", f"rsum{hs}"], tag=f"ph2.exp{T}.{h}")

            def cast(v, L=L, hs=hs):
                v.reciprocal(rinv2[:, hs:hs + 1], rsum2[:, hs:hs + 1])
                v.drain()
                v.tensor_single_scalar(probs_bf[:, hs * N: hs * N + L],
                                       probs_sb[:, hs * N: hs * N + L],
                                       rinv2[:, hs:hs + 1], ALU.mult)
            sc.step("V", cast, reads=[f"probs{hs}", f"rsum{hs}"],
                    writes=[f"probsbf{hs}"], tag=f"ph2.cast{T}.{h}")

            for b in range(nb4):
                ntr = min(4, (T + 1) - 4 * b)

                bank = b % 2

                def trb(t, b=b, ntr=ntr, bank=bank, hs=hs):
                    last = None
                    for j in range(ntr):
                        kc = 4 * b + j
                        last = t.transpose(trp[:, bank * 1024 + j * 128: bank * 1024 + (j + 1) * 128],
                                           probs_bf[:, hs * N + kc * 128: hs * N + (kc + 1) * 128],
                                           ident[:, :])
                    return last
                sc.step("T", trb, reads=[f"probsbf{hs}", "ident"], writes=[f"trp{bank}"],
                        tag=f"ph2.tr{T}.{h}.{b}")

                def cpb(v, b=b, ntr=ntr, bank=bank, hs=hs):
                    v.tensor_copy(out=probsT[:, hs * N + 4 * b * 128: hs * N + (4 * b + ntr) * 128],
                                  in_=trp[:, bank * 1024: bank * 1024 + ntr * 128])
                sc.step("V", cpb, reads=[f"trp{bank}"], writes=[f"probsT{hs}"],
                        tag=f"ph2.cp{T}.{h}.{b}")

            def mm_av(t, T=T, hs=hs):
                last = None
                for kc in range(T + 1):
                    last = t.matmul(av_ps[:, :], v_tok[:, kc * 128:(kc + 1) * 128],
                                    probsT[:, hs * N + kc * 128: hs * N + (kc + 1) * 128],
                                    start=(kc == 0), stop=(kc == T))
                return last
            sc.step("T", mm_av, reads=["v_tok", f"probsT{hs}"], writes=["av_ps"],
                    tag=f"ph2.av{T}.{h}")

            def ev_av(a, T=T, h=h):
                a.activation(out=attn_fm[:, h * N + T * 128: h * N + (T + 1) * 128],
                             in_=av_ps[:, :], func=AF.Copy)
            sc.step("A", ev_av, reads=["av_ps"], writes=["attn_fm"],
                    tag=f"ph2.ev{T}.{h}")

    sc.barrier("b23")
    free("av_ps", "sc_ps", "rinv2", "rsum2", "nb2", "rm2", "probsT", "probs_bf",
         "probs_sb", "v_tok", "vstage", "k_fm", "q_fm")

    wfc_bf = salloc("wfc_bf", [128, EC * E], BF16, side="right")
    bfc_row = salloc("bfc_row", [1, E], F32)
    ones1 = salloc("ones1", [1, 128], F32)
    bfc_bc = salloc("bfc_bc", [128, E], F32)
    ares_st = salloc("ares_st", [128, 2 * E], F32)
    fc_ps = palloc("fc_ps", [128, 2 * 512])

    sc.step("G", (lambda g, sem: g.dma_start(out=bfc_row[:, :],
                                        in_=bfc_h[:].rearrange("(o f) -> o f", o=1))
                  .then_inc(sem, 16)), writes=["bfc_row"], ndma=1, tag="ph3.b")
    sc.step("V", lambda v: v.memset(ones1[:, :], 1.0), writes=["ones1"], tag="ph3.o")
    for es in range(4):
        sc.step("T", (lambda t, es=es:
                      t.matmul(fc_ps[:, (es % 2) * 512:(es % 2 + 1) * 512],
                               ones1[:, :], bfc_row[:, es * 512:(es + 1) * 512],
                               start=True, stop=True)),
                reads=["ones1", "bfc_row"], writes=[f"fc_ps{es % 2}"],
                tag=f"ph3.bm{es}")
        sc.step("A", (lambda a, es=es:
                      a.activation(out=bfc_bc[:, es * 512:(es + 1) * 512],
                                   in_=fc_ps[:, (es % 2) * 512:(es % 2 + 1) * 512],
                                   func=AF.Copy)),
                reads=[f"fc_ps{es % 2}"], writes=["bfc_bc"], tag=f"ph3.be{es}")

    for kc in range(EC):
        sc.step("G", (lambda g, sem, kc=kc:
                      g.dma_start(out=wfc_bf[:, kc * E:(kc + 1) * E],
                                  in_=wfc_h[kc * 128:(kc + 1) * 128, :])
                      .then_inc(sem, 16)),
                writes=["wfc_bf"], ndma=1, tag=f"ph3.w{kc}")

    for T in range(NT):
        load_x(T, T, f"ph3.x{T}")
        for es in range(4):
            idx = T * 4 + es

            def mmfc(t, T=T, es=es, idx=idx):
                last = None
                for kc in range(EC):
                    last = t.matmul(
                        fc_ps[:, (idx % 2) * 512:(idx % 2 + 1) * 512],
                        attn_fm[:, kc * N + T * 128: kc * N + (T + 1) * 128],
                        wfc_bf[:, kc * E + es * 512: kc * E + (es + 1) * 512],
                        start=(kc == 0), stop=(kc == EC - 1))
                return last
            sc.step("T", mmfc, reads=["attn_fm", "wfc_bf"], writes=[f"fc_ps{idx % 2}"],
                    tag=f"ph3.mm{T}.{es}")

            def evfc(v, T=T, es=es, idx=idx):
                dst = ares_st[:, (T % 2) * E + es * 512:(T % 2) * E + (es + 1) * 512]
                v.scalar_tensor_tensor(
                    out=dst, in0=fc_ps[:, (idx % 2) * 512:(idx % 2 + 1) * 512],
                    scalar=1.0, in1=bfc_bc[:, es * 512:(es + 1) * 512],
                    op0=ALU.mult, op1=ALU.add)
                v.drain()
                v.tensor_add(dst, dst,
                             xstg[:, (T % 2) * E + es * 512:(T % 2) * E + (es + 1) * 512])
            sc.step("V", evfc, reads=[f"fc_ps{idx % 2}", "bfc_bc", f"xstg{T % 2}"],
                    writes=[f"ares{T % 2}"], tag=f"ph3.ev{T}.{es}")

        sc.step("G", (lambda g, sem, T=T:
                      g.dma_start(out=ares_h[T * 128:(T + 1) * 128, :],
                                  in_=ares_st[:, (T % 2) * E:(T % 2 + 1) * E])
                      .then_inc(sem, 16)),
                reads=[f"ares{T % 2}"], writes=[f"aresh{T}"], ndma=1, tag=f"ph3.s{T}")

    sc.barrier("b34")
    free("fc_ps", "ares_st", "bfc_bc", "ones1", "bfc_row", "wfc_bf", "attn_fm",
         "xstg")

    xq_fm = salloc("xq_fm", [128, EC * (N // 2)], BF16)
    res = salloc("res", [128, HT * E], BF16)
    wgT = salloc("wgT", [128, EC * OSTRIP], BF16)
    wuT = salloc("wuT", [128, EC * OSTRIP], BF16)
    wdT = salloc("wdT", [128, 4 * E], BF16)
    ring = salloc("ring", [128, 4 * E], F32)
    qbf = salloc("qbf", [128, 2 * E], BF16)
    interall = salloc("interall", [128, HT * 512], F32)
    sg_t = salloc("sg_t", [128, 512], F32)
    qs0A = salloc("qs0A", [128, 1024], F32)
    qs0B = salloc("qs0B", [128, 1024], F32)
    qs0C = salloc("qs0C", [128, 1024], F32)
    qs1A = salloc("qs1A", [128, 1024], F32)
    qs1B = salloc("qs1B", [128, 1024], F32)
    qs1C = salloc("qs1C", [128, 1024], F32)
    QSETS = [(qs0A, qs0B, qs0C), (qs1A, qs1B, qs1C)]
    iv4 = salloc("iv4", [128, 512], F32)
    sc4 = salloc("sc4", [128, 512], F32)
    am4 = qs1C[:, 0:512]
    zt4 = qs0A[:, 0:512]
    ft4 = qs0B[:, 0:512]
    ft4b = qs0C[:, 0:512]
    xqbf = qbf[:, 0:E]
    intq = qbf[:, 2 * 512:3 * 512]
    intT = qbf[:, 3 * 512:4 * 512]
    _QP = [0]

    def q_body(src_full, src_res, dst_full, dst_res, fmt, iv_off, tag,
               extra_dst_res=()):
        F = src_full.shape[-1]
        tinv = 8.0 if fmt == "act" else 2.0
        mn = 7.5 if fmt == "act" else 6.0
        shT = sh16 if fmt == "act" else sh4
        nh = (F + 1023) // 1024
        for hh in range(nh):
            Fh = min(1024, F - hh * 1024)
            nb = Fh // 32
            p = _QP[0] % 2
            _QP[0] += 1
            qA, qB, qC = QSETS[p]
            qr = f"qs{p}"
            A, B, Cb = qA[:, 0:Fh], qB[:, 0:Fh], qC[:, 0:Fh]
            srch = src_full[:, hh * 1024: hh * 1024 + Fh]
            dsth = dst_full[:, hh * 1024: hh * 1024 + Fh]
            ivc = iv4[:, iv_off + hh * 32: iv_off + hh * 32 + nb]
            scc = sc4[:, iv_off + hh * 32: iv_off + hh * 32 + nb]
            ivb = ivc.rearrange("p (b o) -> p b o", o=1).broadcast_to([128, nb, 32])
            scb = scc.rearrange("p (b o) -> p b o", o=1).broadcast_to([128, nb, 32])
            s3 = srch.rearrange("p (b k) -> p b k", k=32)

            t2 = tinv * tinv

            def v1(v, A=A, s3=s3, ivb=ivb):
                v.tensor_tensor(out=A.rearrange("p (b k) -> p b k", k=32),
                                in0=s3, in1=ivb, op=ALU.mult)
            sc.step("V", v1, reads=list(src_res) + ["iv4"], writes=[qr],
                    tag=f"{tag}.v1")

            def vik(v, A=A, B=B, t2=t2):
                v._custom_dve(_MXQ["MXQ_IK"], out=B, in0=A, s0=4.0 * t2,
                              s1=16.0 * t2, imm2=0.5)
            sc.step("V", vik, reads=[qr], writes=[qr], tag=f"{tag}.vik")

            def g1b(g, A=A, B=B, Cb=Cb):
                g.tensor_mul(Cb, A, B)
            sc.step("G", g1b, reads=[qr], writes=[qr], tag=f"{tag}.g1")

            def v2(v, A=A, B=B, Cb=Cb, fmt=fmt, shT=shT, t2=t2):
                if fmt == "act":
                    v._custom_dve(_MXQ["MXQ_EX3"], out=B, in0=Cb, in1=shT[:, :],
                                  s0=16.0, s1=8.0)
                    v.drain()
                    v._custom_dve(_MXQ["MXQ_EX3"], out=B, in0=B, in1=shift0[:, :],
                                  s0=4.0, s1=2.0)
                else:
                    v._custom_dve(_MXQ["MXQ_EX3"], out=B, in0=Cb, in1=shT[:, :],
                                  s0=4.0, s1=2.0)
                v.drain()
                v._custom_dve(_MXQ["MXQ_TIE"], out=B, in0=B, in1=Cb, s1=0.5)
                v.drain()
                v._custom_dve(_MXQ["MXQ_UNF"], out=Cb, in0=A, in1=B,
                              s0=4.0 * t2, s1=16.0 * t2, imm2=0.0)
            sc.step("V", v2, reads=[qr, "shifts"], writes=[qr], tag=f"{tag}.v2")

            def g2(g, A=A, Cb=Cb):
                g.tensor_scalar(out=A, in0=Cb, scalar1=1.0 / tinv, scalar2=mn,
                                op0=ALU.mult, op1=ALU.min)
                g.drain()
                g.tensor_scalar(out=A, in0=A, scalar1=-mn, scalar2=None,
                                op0=ALU.max)
            sc.step("G", g2, reads=[qr], writes=[qr], tag=f"{tag}.g2")

            def v3(v, A=A, dsth=dsth, scb=scb):
                v.tensor_tensor(out=dsth.rearrange("p (b k) -> p b k", k=32),
                                in0=A.rearrange("p (b k) -> p b k", k=32),
                                in1=scb, op=ALU.mult)
            sc.step("V", v3, reads=[qr, "sc4"],
                    writes=[qr, dst_res] + list(extra_dst_res), tag=f"{tag}.v3")
    g_ps = palloc("g_ps", [128, 2 * 512])
    u_ps = palloc("u_ps", [128, 2 * 512])
    dn_ps = palloc("dn_ps", [128, 2 * 512])

    def rg(i):
        return ring[:, i * E:(i + 1) * E]


    for hf in range(2):
        T0 = hf * HT
        for i in range(HT):
            T = T0 + i
            sc.step("S", (lambda s, sem, T=T, i=i:
                          s.dma_start(out=rg(i % 2), in_=x_h[T * 128:(T + 1) * 128, :])
                          .then_inc(sem, 16)),
                    writes=[f"ring{i % 2}"], ndma=1, tag=f"p4a{hf}.x{i}")

            def am(v, T=T, i=i):
                v.tensor_scalar(out=xn_st[:, :], in0=rg(i % 2),
                                scalar1=mn_all[:, T:T + 1], scalar2=rs_all[:, T:T + 1],
                                op0=ALU.subtract, op1=ALU.mult)
                v.drain()
                v.tensor_reduce(out=am4[:, i * 64:(i + 1) * 64],
                                in_=xn_st[:, :].rearrange("p (b k) -> p b k", k=32),
                                axis=AX.X, op=ALU.max, apply_absolute_value=True)
            sc.step("V", am, reads=[f"ring{i % 2}", "mn_all", "rs_all"],
                    writes=["xn_st", "qs1"], tag=f"p4a{hf}.am{i}")

        sc.step("V", (lambda v: emit_scale_chain(v, am4[:, :], zt4, ft4, ft4b, iv4[:, :], sc4[:, :],
                                                 24, CH_STD, post=8.0)),
                reads=["qs1"], writes=["iv4", "sc4", "qs0"],
                tag=f"p4a{hf}.ch")

        for i in range(HT):
            T = T0 + i
            sc.step("S", (lambda s, sem, T=T, i=i:
                          s.dma_start(out=rg(i % 2), in_=x_h[T * 128:(T + 1) * 128, :])
                          .then_inc(sem, 16)),
                    writes=[f"ring{i % 2}"], ndma=1, tag=f"p4b{hf}.x{i}")

            def qx(v, T=T, i=i):
                v.tensor_scalar(out=xn_st[:, :], in0=rg(i % 2),
                                scalar1=mn_all[:, T:T + 1], scalar2=rs_all[:, T:T + 1],
                                op0=ALU.subtract, op1=ALU.mult)
            sc.step("V", qx, reads=[f"ring{i % 2}", "mn_all", "rs_all"],
                    writes=["xn_st"], tag=f"p4b{hf}.ln{i}")
            q_body(xn_st[:, :], ["xn_st"], xqbf, "qbf0", "act", i * 64,
                   f"p4b{hf}.q{i}")
            tr16(xqbf, xq_fm, "xq_fm", None, f"p4b{hf}.t{i}", ["qbf0"],
                 dst3d=(xq_fm[:, :].rearrange("p (c n) -> p c n", n=N // 2), i * 128))
            sc.step("V", (lambda v, i=i: v.memset(res[:, i * E:(i + 1) * E], 0.0)),
                    writes=[f"res{i}"], tag=f"p4b{hf}.z{i}")

        for s_i in range(NSTRIP):
            o0 = s_i * OSTRIP
            if hf == 0:
                for m in range(2):
                    srcw = wg_h if m == 0 else wu_h
                    dstc = wgq_h if m == 0 else wuq_h
                    dstTn = "wgT" if m == 0 else "wuT"
                    dstT = wgT if m == 0 else wuT
                    for it in range(4):
                        sc.step("S", (lambda s, sem, srcw=srcw, o0=o0, it=it:
                                      s.dma_start(out=rg(it),
                                                  in_=srcw[o0 + it * 128:o0 + (it + 1) * 128, :])
                                      .then_inc(sem, 16)),
                                writes=[f"ring{it}"], ndma=1, tag=f"w{m}l{s_i}.{it}")
                        sc.step("V", (lambda v, it=it:
                                      v.tensor_reduce(out=am4[:, it * 64:(it + 1) * 64],
                                                      in_=rg(it).rearrange("p (b k) -> p b k", k=32),
                                                      axis=AX.X, op=ALU.max,
                                                      apply_absolute_value=True)),
                                reads=[f"ring{it}"], writes=["qs1"], tag=f"w{m}a{s_i}.{it}")
                    sc.step("V", (lambda v: emit_scale_chain(v, am4[:, 0:256],
                                                             zt4[:, 0:256], ft4[:, 0:256],
                                                             ft4b[:, 0:256], iv4[:, 0:256],
                                                             sc4[:, 0:256], 24, CH_STD,
                                                             post=2.0)),
                            reads=["qs1"], writes=["iv4", "sc4", "qs0"],
                            tag=f"w{m}c{s_i}")
                    for it in range(4):
                        q_body(rg(it), [f"ring{it}"],
                               qbf[:, (it % 2) * E:(it % 2 + 1) * E],
                               f"qbf{it % 2}", "wgt", it * 64,
                               f"w{m}q{s_i}.{it}")
                        sc.step("G", (lambda g, sem, dstc=dstc, o0=o0, it=it:
                                      g.dma_start(out=dstc[o0 + it * 128:o0 + (it + 1) * 128, :],
                                                  in_=qbf[:, (it % 2) * E:(it % 2 + 1) * E])
                                      .then_inc(sem, 16)),
                                reads=[f"qbf{it % 2}"], writes=[f"c{m}.{s_i}.{it}"],
                                ndma=1, tag=f"w{m}cw{s_i}.{it}")
                        tr16(qbf[:, (it % 2) * E:(it % 2 + 1) * E], dstT, dstTn,
                             None, f"w{m}t{s_i}.{it}", [f"qbf{it % 2}"],
                             dst3d=(dstT[:, :].rearrange("p (c n) -> p c n", n=OSTRIP),
                                    it * 128))
                for j in range(16):
                    sc.step("S", (lambda s, sem, o0=o0, j=j:
                                  s.dma_start(out=ring[:, j * 512:(j + 1) * 512],
                                              in_=wd_h[j * 128:(j + 1) * 128, o0:o0 + OSTRIP])
                                  .then_inc(sem, 16)),
                            writes=[f"ring{j // 4}"], ndma=1, tag=f"wdl{s_i}.{j}")
                    sc.step("V", (lambda v, j=j:
                                  v.tensor_reduce(out=am4[:, j * 16:(j + 1) * 16],
                                                  in_=ring[:, j * 512:(j + 1) * 512]
                                                  .rearrange("p (b k) -> p b k", k=32),
                                                  axis=AX.X, op=ALU.max,
                                                  apply_absolute_value=True)),
                            reads=[f"ring{j // 4}"], writes=["qs1"], tag=f"wda{s_i}.{j}")
                sc.step("V", (lambda v: emit_scale_chain(v, am4[:, 0:256], zt4[:, 0:256],
                                                         ft4[:, 0:256], ft4b[:, 0:256],
                                                         iv4[:, 0:256], sc4[:, 0:256],
                                                         24, CH_STD, post=2.0)),
                        reads=["qs1"], writes=["iv4", "sc4", "qs0"],
                        tag=f"wdc{s_i}")
                for j in range(16):
                    q_body(ring[:, j * 512:(j + 1) * 512], [f"ring{j // 4}"],
                           qbf[:, (j % 8) * 512:(j % 8 + 1) * 512],
                           f"qbf{(j % 8) // 4}", "wgt", j * 16,
                           f"wdq{s_i}.{j}", extra_dst_res=[f"wdq{s_i}.{j}"])
                    sc.step("G", (lambda g, sem, o0=o0, j=j:
                                  g.dma_start(out=wdq_h[j * 128:(j + 1) * 128, o0:o0 + OSTRIP],
                                              in_=qbf[:, (j % 8) * 512:(j % 8 + 1) * 512])
                                  .then_inc(sem, 16)),
                            reads=[f"wdq{s_i}.{j}", f"qbf{(j % 8) // 4}"],
                            writes=[f"cd.{s_i}.{j}"], ndma=1, tag=f"wdcw{s_i}.{j}")

                    bank = j % 2

                    def trd(t, j=j, bank=bank):
                        last = None
                        for oc in range(4):
                            last = t.transpose(
                                trp[:, bank * 1024 + oc * 128: bank * 1024 + (oc + 1) * 128],
                                qbf[:, (j % 8) * 512 + oc * 128:(j % 8) * 512 + (oc + 1) * 128],
                                ident[:, :])
                        return last
                    sc.step("T", trd, reads=[f"wdq{s_i}.{j}", f"qbf{(j % 8) // 4}",
                                             "ident"], writes=[f"trp{bank}"],
                            tag=f"wdt{s_i}.{j}")

                    def cpd(v, j=j, bank=bank):
                        v.tensor_copy(out=wdT[:, :].rearrange("p (oc e) -> p oc e", e=E)
                                      [:, :, j * 128:(j + 1) * 128],
                                      in_=trp[:, bank * 1024: bank * 1024 + 512]
                                      .rearrange("p (oc f) -> p oc f", f=128))
                    sc.step("V", cpd, reads=[f"trp{bank}"], writes=["wdT"], tag=f"wdcp{s_i}.{j}")
            else:
                for m in range(2):
                    srcc = wgq_h if m == 0 else wuq_h
                    dstTn = "wgT" if m == 0 else "wuT"
                    dstT = wgT if m == 0 else wuT
                    for it in range(4):
                        sc.step("S", (lambda s, sem, srcc=srcc, o0=o0, it=it:
                                      s.dma_start(out=qbf[:, (it % 2) * E:(it % 2 + 1) * E],
                                                  in_=srcc[o0 + it * 128:o0 + (it + 1) * 128, :])
                                      .then_inc(sem, 16)),
                                reads=[f"c{m}.{s_i}.{it}"],
                                writes=[f"qbf{it % 2}"],
                                ndma=1, tag=f"w{m}lb{s_i}.{it}")
                        tr16(qbf[:, (it % 2) * E:(it % 2 + 1) * E], dstT, dstTn,
                             None, f"w{m}t{s_i}b.{it}", [f"qbf{it % 2}"],
                             dst3d=(dstT[:, :].rearrange("p (c n) -> p c n", n=OSTRIP),
                                    it * 128))
                for j in range(16):
                    sc.step("S", (lambda s, sem, o0=o0, j=j:
                                  s.dma_start(out=qbf[:, (j % 8) * 512:(j % 8 + 1) * 512],
                                              in_=wdq_h[j * 128:(j + 1) * 128, o0:o0 + OSTRIP])
                                  .then_inc(sem, 16)),
                            reads=[f"cd.{s_i}.{j}"],
                            writes=[f"qbf{(j % 8) // 4}", f"wdq{s_i}b.{j}"],
                            ndma=1, tag=f"wdlb{s_i}.{j}")

                    bank = j % 2

                    def trd(t, j=j, bank=bank):
                        last = None
                        for oc in range(4):
                            last = t.transpose(
                                trp[:, bank * 1024 + oc * 128: bank * 1024 + (oc + 1) * 128],
                                qbf[:, (j % 8) * 512 + oc * 128:(j % 8) * 512 + (oc + 1) * 128],
                                ident[:, :])
                        return last
                    sc.step("T", trd, reads=[f"wdq{s_i}b.{j}", f"qbf{(j % 8) // 4}",
                                             "ident"], writes=[f"trp{bank}"],
                            tag=f"wdtb{s_i}.{j}")

                    def cpd(v, j=j, bank=bank):
                        v.tensor_copy(out=wdT[:, :].rearrange("p (oc e) -> p oc e", e=E)
                                      [:, :, j * 128:(j + 1) * 128],
                                      in_=trp[:, bank * 1024: bank * 1024 + 512]
                                      .rearrange("p (oc f) -> p oc f", f=128))
                    sc.step("V", cpd, reads=[f"trp{bank}"], writes=["wdT"], tag=f"wdcpb{s_i}.{j}")

            for tt in range(HT):
                sl = tt % 2

                def mmg(t, tt=tt, sl=sl):
                    last = None
                    for hc in range(EC):
                        last = t.matmul(
                            g_ps[:, sl * 512:(sl + 1) * 512],
                            xq_fm[:, hc * (N // 2) + tt * 128: hc * (N // 2) + (tt + 1) * 128],
                            wgT[:, hc * OSTRIP:(hc + 1) * OSTRIP],
                            start=(hc == 0), stop=(hc == EC - 1))
                    return last
                sc.step("T", mmg, reads=["xq_fm", "wgT"], writes=[f"g_ps{sl}"],
                        tag=f"ffg{hf}.{s_i}.{tt}")
                sc.step("A", (lambda a, sl=sl:
                              a.activation(out=sg_t[:, :],
                                           in_=g_ps[:, sl * 512:(sl + 1) * 512],
                                           func=AF.Silu)),
                        reads=[f"g_ps{sl}"], writes=["sg_t"], tag=f"ffs{hf}.{s_i}.{tt}")

                def mmu(t, tt=tt, sl=sl):
                    last = None
                    for hc in range(EC):
                        last = t.matmul(
                            u_ps[:, sl * 512:(sl + 1) * 512],
                            xq_fm[:, hc * (N // 2) + tt * 128: hc * (N // 2) + (tt + 1) * 128],
                            wuT[:, hc * OSTRIP:(hc + 1) * OSTRIP],
                            start=(hc == 0), stop=(hc == EC - 1))
                    return last
                sc.step("T", mmu, reads=["xq_fm", "wuT"], writes=[f"u_ps{sl}"],
                        tag=f"ffu{hf}.{s_i}.{tt}")

                def mul(v, tt=tt, sl=sl):
                    v.tensor_mul(interall[:, tt * 512:(tt + 1) * 512],
                                 sg_t[:, :],
                                 u_ps[:, sl * 512:(sl + 1) * 512])
                    v.drain()
                    v.tensor_reduce(out=am4[:, 256 + tt * 16: 256 + (tt + 1) * 16],
                                    in_=interall[:, tt * 512:(tt + 1) * 512]
                                    .rearrange("p (b k) -> p b k", k=32),
                                    axis=AX.X, op=ALU.max, apply_absolute_value=True)
                sc.step("V", mul, reads=["sg_t", f"u_ps{sl}"], writes=["interall", "qs1"],
                        tag=f"ffm{hf}.{s_i}.{tt}")

            sc.step("V", (lambda v: emit_scale_chain(v, am4[:, 256:384], zt4[:, 0:128],
                                                     ft4[:, 0:128], ft4b[:, 0:128],
                                                     iv4[:, 256:384], sc4[:, 256:384],
                                                     43, CH_INT, post=8.0)),
                    reads=["qs1"], writes=["iv4", "sc4", "qs0"],
                    tag=f"ffc{hf}.{s_i}")

            for tt in range(HT):
                q_body(interall[:, tt * 512:(tt + 1) * 512], ["interall"],
                       intq, "qbf0", "act", 256 + tt * 16,
                       f"ffq{hf}.{s_i}.{tt}")

                bank = tt % 2

                def tri(t, bank=bank):
                    last = None
                    for oc in range(4):
                        last = t.transpose(trp[:, bank * 1024 + oc * 128: bank * 1024 + (oc + 1) * 128],
                                           qbf[:, 2 * 512 + oc * 128:2 * 512 + (oc + 1) * 128],
                                           ident[:, :])
                    return last
                sc.step("T", tri, reads=["qbf0", "ident"], writes=[f"trp{bank}"],
                        tag=f"fft{hf}.{s_i}.{tt}")
                sc.step("V", (lambda v, bank=bank:
                              v.tensor_copy(out=intT, in_=trp[:, bank * 1024: bank * 1024 + 512])),
                        reads=[f"trp{bank}"], writes=["qbf0"], tag=f"ffcp{hf}.{s_i}.{tt}")

                for es in range(4):
                    def mmd(t, es=es):
                        last = None
                        for oc in range(4):
                            last = t.matmul(
                                dn_ps[:, (es % 2) * 512:(es % 2 + 1) * 512],
                                qbf[:, 3 * 512 + oc * 128:3 * 512 + (oc + 1) * 128],
                                wdT[:, oc * E + es * 512: oc * E + (es + 1) * 512],
                                start=(oc == 0), stop=(oc == 3))
                        return last
                    sc.step("T", mmd, reads=["qbf0", "wdT"], writes=[f"dn_ps{es % 2}"],
                            tag=f"ffd{hf}.{s_i}.{tt}.{es}")

                    def acc(v, tt=tt, es=es):
                        dst = res[:, tt * E + es * 512: tt * E + (es + 1) * 512]
                        v.tensor_add(dst, dst,
                                     dn_ps[:, (es % 2) * 512:(es % 2 + 1) * 512])
                    sc.step("V", acc, reads=[f"dn_ps{es % 2}"], writes=[f"res{tt}"],
                            tag=f"ffa{hf}.{s_i}.{tt}.{es}")

        for tt in range(HT):
            T = T0 + tt
            sc.step("G", (lambda g, sem, T=T: g.dma_start(out=rg(0),
                                                     in_=ares_h[T * 128:(T + 1) * 128, :])
                          .then_inc(sem, 16)),
                    reads=[f"aresh{T}"], writes=["ring0"], ndma=1, tag=f"fin{hf}.l{tt}")
            sc.step("V", (lambda v, tt=tt: v.tensor_add(rg(1), rg(0),
                                                        res[:, tt * E:(tt + 1) * E])),
                    reads=["ring0", f"res{tt}"], writes=["ring1"], tag=f"fin{hf}.a{tt}")
            sc.step("G", (lambda g, sem, T=T: g.dma_start(out=out_h[T * 128:(T + 1) * 128, :],
                                                     in_=rg(1))
                          .then_inc(sem, 16)),
                    reads=["ring1"], writes=[], ndma=1, tag=f"fin{hf}.s{tt}")
        if hf == 0:
            sc.barrier("bhalf")

    blk = nc.Block()
    block = blk.__enter__()
    sc.emit(nc, block, SEMS)
    blk.__exit__(None, None, None)
    for g in semg:
        g.__exit__(None, None, None)
    for nm in reversed(list(live)):
        live.pop(nm).__exit__(None, None, None)
    mybir.codegen_inst_isa_subclasses(nc)
    return nc


_NC_CACHE = []


def kernel(L0_attn_x, L0_attn_w_qkv, L0_attn_b_qkv, L0_attn_w_fc, L0_attn_b_fc,
           L0_ffn_w_gate, L0_ffn_w_up, L0_ffn_w_down):
    import ml_dtypes
    if not _NC_CACHE:
        _NC_CACHE.append(build_program())
    nc = _NC_CACHE[0]
    x = np.ascontiguousarray(np.asarray(L0_attn_x, dtype=np.float32))
    wqkv = np.ascontiguousarray(np.asarray(L0_attn_w_qkv, dtype=np.float32))
    bqkv = np.ascontiguousarray(np.asarray(L0_attn_b_qkv, dtype=np.float32))
    wfc = np.ascontiguousarray(np.asarray(L0_attn_w_fc, dtype=np.float32))
    bfc = np.ascontiguousarray(np.asarray(L0_attn_b_fc, dtype=np.float32))
    wg = np.ascontiguousarray(np.asarray(L0_ffn_w_gate, dtype=np.float32))
    wu = np.ascontiguousarray(np.asarray(L0_ffn_w_up, dtype=np.float32))
    wd = np.ascontiguousarray(np.asarray(L0_ffn_w_down, dtype=np.float32))
    ident = np.eye(128, dtype=ml_dtypes.bfloat16)
    cmask = np.triu(np.full((128, 128), -1.0e30, np.float32), k=1)
    in_maps = []
    for b in range(8):
        in_maps.append({
            "x": np.ascontiguousarray(x[b]),
            "w_qkv": wqkv, "b_qkv": bqkv, "w_fc": wfc, "b_fc": bfc,
            "w_gate": wg, "w_up": wu, "w_down": wd,
            "ident": ident, "cmask": cmask,
        })
    res = run_bass_kernel_spmd(nc, in_maps, core_ids=list(range(8)))
    out = np.stack([np.asarray(r["out"], dtype=np.float32) for r in res.results], axis=0)
    return out


# version 07
# speedup vs baseline: 1.8163x; 1.8163x over previous
import numpy as np
import concourse.bass as bass
import concourse.mybir as mybir
from concourse.bass_utils import run_bass_kernel_spmd
import concourse.dve_ops as dve_ops
from concourse.dve_op import DveOp
from concourse.dve_spec import (
    Spec, Src0, Src1, C0, C1, C2, C3, One, Bin, AluOp, eq, lower as dve_lower,
)
from concourse.dve_table_gen import dve_ver_for

F32 = mybir.dt.float32
BF16 = mybir.dt.bfloat16
ALU = mybir.AluOpType
AX = mybir.AxisListType
AF = mybir.ActivationFunctionType

N = 2048
E = 2048
NT = N // 128
EC = E // 128
H = 16
QKVF = 2304
QC = QKVF // 128
I = 8192
OSTRIP = 512
NSTRIP = I // OSTRIP
HT = NT // 2
SM = 1.0 / float(np.sqrt(np.float32(128)))



def _cdt(a):
    return np.float64 if a.dtype == np.float64 else np.float32


def _ref_ik(in0, in1, s0, s1, imm2):
    f = _cdt(in0)
    a2 = in0.astype(f) * in0
    return (1.0 - (a2 >= s0) * imm2) * (1.0 - (a2 >= s1) * imm2)


def _ref_ex3(in0, in1, s0, s1, imm2):
    f = _cdt(in0)
    z = in0.astype(f) + in1
    b = (z >= s0).astype(f)
    z1 = z - b * s0
    b1 = (z1 >= s1).astype(f)
    return z1 - b1 * s1


def _ref_tie(in0, in1, s0, s1, imm2):
    f = _cdt(in0)
    z0 = in0.astype(f)
    b0 = (z0 >= 1.0).astype(f)
    fr = z0 - b0
    c1 = (fr > s1).astype(f)
    e = (fr == s1).astype(f)
    return in1 + ((c1 - fr) + e * b0)


def _ref_unf(in0, in1, s0, s1, imm2):
    f = _cdt(in0)
    a2 = in0.astype(f) * in0
    kk = (1.0 + (a2 >= s0)) * (1.0 + (a2 >= s1))
    return (in1 - imm2) * kk


def _mxq_specs():
    u2a = Src0 * Src0
    ik_body = (One - (u2a >= C0) * C2) * (One - (u2a >= C1) * C2)
    zs = Src0 + C3
    bb = zs >= C0
    zz1 = zs - bb * C0
    bb1 = zz1 >= C1
    ex3_body = zz1 - bb1 * C1
    b0 = Bin(AluOp.IS_GE, Src0, One)
    fr = Src0 - b0
    c1 = Bin(AluOp.IS_LT, C1, fr)
    e = eq(fr, C1)
    tie_body = Src1 + ((c1 - fr) + (e & b0))
    u2b = Src0 * Src0
    kk = (One + (u2b >= C0)) * (One + (u2b >= C1))
    unf_body = (Src1 - C2) * kk
    return {
        "MXQ_IK": (Spec(body=ik_body), _ref_ik),
        "MXQ_EX3": (Spec(body=ex3_body), _ref_ex3),
        "MXQ_TIE": (Spec(body=tie_body), _ref_tie),
        "MXQ_UNF": (Spec(body=unf_body), _ref_unf),
    }


_MXQ = {}


def _register_mxq():
    if _MXQ:
        return
    if any(op.name == "MXQ_IK" for op in dve_ops.OPS):
        for op in dve_ops.OPS:
            if op.name.startswith("MXQ_"):
                _MXQ[op.name] = op
        return
    ver = dve_ver_for("TRN2")
    replaceable = ["DVEHASH32L_SEED", "DVEHASH32L_BODY", "DVEHASH32L_DRAIN"]
    for name, (spec, ref) in _mxq_specs().items():
        compiled = dve_lower(spec, ver=ver, perf_en=False, name=name)
        op = DveOp(name, spec, subdim=False, uops_sha={ver: compiled.sha(ver)},
                   reference=ref)
        if max(dve_ops._SUB_OPCODE_FOR_NAME.values()) < 0x1F:
            row = max(dve_ops._SUB_OPCODE_FOR_NAME.values()) + 1
            dve_ops.OPS.append(op)
        else:
            victim = replaceable.pop()
            idx = next(i for i, o in enumerate(dve_ops.OPS) if o.name == victim)
            row = dve_ops._SUB_OPCODE_FOR_NAME.pop(victim)
            dve_ops.OPS[idx] = op
            dve_ops.CUSTOM_DVE_SPECS.pop(victim, None)
            dve_ops.CUSTOM_DVE_REFS.pop(victim, None)
        dve_ops._SUB_OPCODE_FOR_NAME[name] = row
        dve_ops.CUSTOM_DVE_SPECS[name] = spec
        dve_ops.CUSTOM_DVE_REFS[name] = op.reference
        _MXQ[name] = op



DMA_ENGS = {"S", "G"}
ALL_ENGS = ("S", "G", "T", "V", "A")


class Sched:
    DRAIN_DIST = 3

    def __init__(self):
        self.steps = []
        self.lw = {}
        self.rd = {}

    def step(self, eng, fn, reads=(), writes=(), ndma=0, tag="", flush=False):
        i = len(self.steps)
        st = {"eng": eng, "fn": fn, "deps": set(), "ndma": ndma, "tag": tag,
              "flush": flush}
        self.steps.append(st)
        for r in reads:
            for j in self.lw.get(r, ()):
                st["deps"].add(j)
            self.rd.setdefault(r, []).append(i)
        for w in writes:
            for j in self.lw.get(w, ()):
                st["deps"].add(j)
            readers = self.rd.get(w, ())
            for j in readers:
                if j != i:
                    st["deps"].add(j)
            if readers:
                self.lw[w] = [i]
            else:
                self.lw.setdefault(w, []).append(i)
            self.rd[w] = []
        return i

    def barrier(self, tag="bar"):
        toks = [f"_{tag}_{e}" for e in ALL_ENGS]
        for e, t in zip(ALL_ENGS, toks):
            self.step(e, lambda h: None, writes=[t], tag=f"{tag}.mark.{e}",
                      flush=(e in DMA_ENGS))
        for e in ALL_ENGS:
            self.step(e, lambda h: None, reads=list(toks), tag=f"{tag}.wait.{e}")

    def emit(self, nc, block, sems):
        steps = self.steps
        n = len(steps)
        ords = {}
        cnt = {e: 0 for e in ALL_ENGS}
        for i, st in enumerate(steps):
            ords[i] = cnt[st["eng"]]
            cnt[st["eng"]] += 1
        needed = [False] * n
        drain_before = [False] * n
        for i, st in enumerate(steps):
            for j in st["deps"]:
                if steps[j]["eng"] == st["eng"]:
                    if (ords[i] - ords[j] < self.DRAIN_DIST
                            and st["eng"] in ("V", "A", "G")):
                        drain_before[i] = True
                else:
                    needed[j] = True
        K = 6
        ev = [None] * n
        run = {e: 0 for e in ALL_ENGS}
        dma_ord = {e: 0 for e in DMA_ENGS}
        sem_tot = {(e, k): 0 for e in DMA_ENGS for k in range(K)}
        self_wait = [None] * n
        for i, st in enumerate(steps):
            e = st["eng"]
            if e in DMA_ENGS and st["ndma"]:
                k = dma_ord[e] % K
                dma_ord[e] += 1
                self_wait[i] = (k, sem_tot[(e, k)])
                sem_tot[(e, k)] += 16 * st["ndma"]
                ev[i] = (e, k, sem_tot[(e, k)])
            else:
                if needed[i]:
                    run[e] += 1
                ev[i] = (e, "C" if e in DMA_ENGS else None, run[e])
            if st.get("flush"):
                st["flush_vals"] = tuple(sem_tot[(e, k)] for k in range(K)) \
                    if e in DMA_ENGS else None
        totals = {e: [sem_tot[(e, k)] for k in range(K)] for e in DMA_ENGS}
        waits = [[] for _ in range(n)]
        for i, st in enumerate(steps):
            for j in st["deps"]:
                pe = steps[j]["eng"]
                pj, kj, vj = ev[j]
                if pe == st["eng"]:
                    if pe in DMA_ENGS and kj != "C":
                        waits[i].append((pe, kj, vj))
                    continue
                if pe in DMA_ENGS and kj != "C":
                    waits[i].append((pe, kj, vj))
                elif needed[j]:
                    waits[i].append((pe, kj, vj))
        seen = {}
        pruned = [[] for _ in range(n)]
        for i, st in enumerate(steps):
            e = st["eng"]
            best = {}
            for (pe, k2, v) in waits[i]:
                key = (pe, k2)
                best[key] = max(best.get(key, 0), v)
            for (pe, k2), v in sorted(best.items(), key=lambda kv: (kv[0][0], str(kv[0][1]))):
                gk = (e, pe, k2)
                if v == 0 or seen.get(gk, -1) >= v:
                    continue
                seen[gk] = v
                pruned[i].append((pe, k2, v))

        def semof(pe, k2):
            if k2 is None:
                return sems[pe]
            return sems[pe + ("C" if k2 == "C" else str(k2))]

        def runner(eng):
            def body(h):
                inc_since_wait = [False]
                ninc = [0]
                for i, st in enumerate(steps):
                    if st["eng"] != eng:
                        continue
                    if pruned[i]:
                        inc_since_wait[0] = False
                    for (pe, k2, v) in pruned[i]:
                        h.wait_ge(semof(pe, k2), v)
                    if drain_before[i]:
                        h.drain()
                    if st.get("flush") and st.get("flush_vals"):
                        for k2f, vf in enumerate(st["flush_vals"]):
                            if vf:
                                h.wait_ge(semof(eng, k2f), vf)
                        inc_since_wait[0] = False
                    if eng in DMA_ENGS and st["ndma"]:
                        k2, v0 = self_wait[i]
                        if v0:
                            h.wait_ge(semof(eng, k2), v0)
                        last = st["fn"](h, semof(eng, k2))
                    else:
                        last = st["fn"](h)
                        if needed[i]:
                            csem = sems[eng + "C"] if eng in DMA_ENGS else sems[eng]
                            if inc_since_wait[0]:
                                h.wait_ge(csem, ninc[0])
                            if eng == "T":
                                if last is not None:
                                    last.then_inc(csem, 1)
                                else:
                                    h.nop().then_inc(csem, 1)
                            else:
                                h.drain()
                                h.nop().then_inc(csem, 1)
                            ninc[0] += 1
                            inc_since_wait[0] = True
                if eng in DMA_ENGS:
                    for k2 in range(K):
                        if totals[eng][k2]:
                            h.wait_ge(semof(eng, k2), totals[eng][k2])
            return body

        block.sync(runner("S"))
        block.gpsimd(runner("G"))
        block.tensor(runner("T"))
        block.vector(runner("V"))
        block.scalar(runner("A"))




def emit_scale_chain(v, amax_ap, zt, ft, ft2, invs_ap, sca_ap, pre, js,
                     post=1.0):
    v.tensor_scalar_mul(zt, amax_ap, 2.0**pre)
    v.memset(invs_ap, 2.0**(pre - 1))
    for j in js:
        v.drain()
        v.tensor_single_scalar(ft, zt, 8.0 * 2.0**j, ALU.is_lt)
        v.tensor_scalar(out=ft2, in0=zt, scalar1=8.0 * 2.0**j,
                        scalar2=2.0**-j, op0=ALU.is_ge, op1=ALU.mult)
        v.drain()
        v.tensor_add(ft, ft, ft2)
        v.drain()
        v.tensor_mul(zt, zt, ft)
        v.tensor_mul(invs_ap, invs_ap, ft)
    v.drain()
    if post != 1.0:
        v.tensor_scalar_mul(invs_ap, invs_ap, post)
        v.drain()
    v.reciprocal(sca_ap, invs_ap)




CH_FULL = (64, 32, 16, 8, 4, 2, 1)
CH_STD = (16, 8, 4, 2, 1)
CH_INT = (32, 16, 8, 4, 2, 1)




def build_program():
    _register_mxq()
    nc = bass.Bass("TRN2", target_bir_lowering=False)

    x_h = nc.dram_tensor("x", [N, E], F32, kind="ExternalInput").ap()
    wqkv_h = nc.dram_tensor("w_qkv", [E, QKVF], F32, kind="ExternalInput").ap()
    bqkv_h = nc.dram_tensor("b_qkv", [QKVF], F32, kind="ExternalInput").ap()
    wfc_h = nc.dram_tensor("w_fc", [E, E], F32, kind="ExternalInput").ap()
    bfc_h = nc.dram_tensor("b_fc", [E], F32, kind="ExternalInput").ap()
    wg_h = nc.dram_tensor("w_gate", [I, E], F32, kind="ExternalInput").ap()
    wu_h = nc.dram_tensor("w_up", [I, E], F32, kind="ExternalInput").ap()
    wd_h = nc.dram_tensor("w_down", [E, I], F32, kind="ExternalInput").ap()
    ident_h = nc.dram_tensor("ident", [128, 128], BF16, kind="ExternalInput").ap()
    cmask_h = nc.dram_tensor("cmask", [128, 128], F32, kind="ExternalInput").ap()
    out_h = nc.dram_tensor("out", [N, E], F32, kind="ExternalOutput").ap()

    ares_h = nc.dram_tensor("ares_scr", [N, E], F32).ap()
    wgq_h = nc.dram_tensor("wgq_c", [I, E], BF16).ap()
    wuq_h = nc.dram_tensor("wuq_c", [I, E], BF16).ap()
    wdq_h = nc.dram_tensor("wdq_c", [E, I], BF16).ap()

    live = {}

    def salloc(name, shape, dtype, side=None):
        g = nc.sbuf_tensor(name, shape, dtype, side=side)
        t = g.__enter__()
        live[name] = g
        return t

    def palloc(name, shape, dtype=F32):
        g = nc.psum_tensor(name, shape, dtype)
        t = g.__enter__()
        live[name] = g
        return t

    def free(*names):
        for nm in names:
            live.pop(nm).__exit__(None, None, None)

    semg = []
    SEMS = {}
    for e in ("T", "V", "A"):
        g = nc.semaphore("sem" + e)
        SEMS[e] = g.__enter__()
        semg.append(g)
    for e in ("S", "G"):
        g = nc.semaphore(f"sem{e}C")
        SEMS[e + "C"] = g.__enter__()
        semg.append(g)
        for k in range(6):
            g = nc.semaphore(f"sem{e}{k}")
            SEMS[e + str(k)] = g.__enter__()
            semg.append(g)

    sc = Sched()

    ident = salloc("identt", [128, 128], BF16)
    cmask = salloc("cmaskt", [128, 128], F32)
    shift0 = salloc("shift0t", [128, 1], F32)
    sh16 = salloc("sh16", [128, 1], F32)
    sh4 = salloc("sh4", [128, 1], F32)
    bq_sb = salloc("bq_sb", [128, QC], F32)
    sm_all = salloc("sm_all", [128, NT], F32)
    sq_all = salloc("sq_all", [128, NT], F32)
    mn_all = salloc("mn_all", [128, NT], F32)
    va_all = salloc("va_all", [128, NT], F32)
    t_all = salloc("t_all", [128, NT], F32)
    sd_all = salloc("sd_all", [128, NT], F32)
    rs_all = salloc("rs_all", [128, NT], F32)
    xn_st = salloc("xn_st", [128, E], F32)
    trp_t = palloc("trp", [128, 2 * 1024], BF16)
    trp = trp_t

    def st_setup(s, sem):
        s.dma_start(out=ident[:, :], in_=ident_h[:, :]).then_inc(sem, 16)
        s.dma_start(out=cmask[:, :], in_=cmask_h[:, :]).then_inc(sem, 16)
        for c in range(QC):
            s.dma_start(out=bq_sb[:, c:c + 1],
                        in_=bqkv_h[c * 128:(c + 1) * 128]
                        .rearrange("(p o) -> p o", o=1)).then_inc(sem, 16)

    sc.step("S", st_setup, writes=["ident", "cmask", "bq_sb"], ndma=2 + QC)
    def v_sh(v):
        v.memset(shift0[:, :], 0.0)
        v.memset(sh16[:, :], 16.0)
        v.memset(sh4[:, :], 4.0)
    sc.step("V", v_sh, writes=["shifts"])

    xstg = salloc("xstg", [128, 2 * E], F32)
    junk = salloc("junk", [128, E], F32)
    xn_fm = salloc("xn_fm", [128, EC * N], BF16)
    xnbf = salloc("xnbf", [128, E], BF16)

    def xs(i):
        return xstg[:, (i % 2) * E:(i % 2 + 1) * E]

    def load_x(T, slot, tag):
        sc.step("S", (lambda s, sem, T=T, slot=slot:
                      s.dma_start(out=xs(slot), in_=x_h[T * 128:(T + 1) * 128, :])
                      .then_inc(sem, 16)),
                writes=[f"xstg{slot % 2}"], ndma=1, tag=tag)

    for T in range(NT):
        load_x(T, T, f"ph0a.x{T}")
        sc.step("A", (lambda a, T=T: a.activation(out=junk[:, :], in_=xs(T),
                                                  func=AF.Square,
                                                  accum_out=sq_all[:, T:T + 1])),
                reads=[f"xstg{T % 2}"], writes=["sq_all", "junk"], tag=f"ph0a.sq{T}")
        sc.step("V", (lambda v, T=T: v.reduce_sum(sm_all[:, T:T + 1], xs(T), axis=AX.X)),
                reads=[f"xstg{T % 2}"], writes=["sm_all"], tag=f"ph0a.sm{T}")

    def st_stats(v):
        v.tensor_scalar_mul(mn_all[:, :], sm_all[:, :], 1.0 / E)
        v.tensor_scalar_mul(va_all[:, :], sq_all[:, :], 1.0 / E)
        v.drain()
        v.tensor_mul(t_all[:, :], mn_all[:, :], mn_all[:, :])
        v.drain()
        v.tensor_sub(va_all[:, :], va_all[:, :], t_all[:, :])
        v.drain()
        v.tensor_scalar_add(va_all[:, :], va_all[:, :], 1e-5)

    sc.step("V", st_stats, reads=["sm_all", "sq_all"],
            writes=["mn_all", "va_all", "t_all"], tag="stats")
    sc.step("A", lambda a: a.sqrt(sd_all[:, :], va_all[:, :]), reads=["va_all"],
            writes=["sd_all"], tag="sqrt")
    sc.step("V", lambda v: v.reciprocal(rs_all[:, :], sd_all[:, :]), reads=["sd_all"],
            writes=["rs_all"], tag="rstd")

    def emit_xn(v, slot, T):
        v.tensor_scalar(out=xn_st[:, :], in0=xs(slot), scalar1=mn_all[:, T:T + 1],
                        scalar2=rs_all[:, T:T + 1], op0=ALU.subtract, op1=ALU.mult)

    def tr16(src, dst, dstw, coloff_fn, tag, src_res, dst3d=None):
        for b in range(4):
            bank = b % 2

            def trb(t, b=b, bank=bank):
                last = None
                for j in range(4):
                    c = 4 * b + j
                    last = t.transpose(trp[:, bank * 1024 + j * 128: bank * 1024 + (j + 1) * 128],
                                       src[:, c * 128:(c + 1) * 128], ident[:, :])
                return last
            sc.step("T", trb, reads=list(src_res) + ["ident"], writes=[f"trp{bank}"],
                    tag=f"{tag}.tr{b}")

            if dst3d is not None:
                def cpb(v, b=b, bank=bank):
                    ap3, off = dst3d
                    v.tensor_copy(out=ap3[:, 4 * b:4 * b + 4, off:off + 128],
                                  in_=trp[:, bank * 1024: bank * 1024 + 512]
                                  .rearrange("p (j f) -> p j f", f=128))
            else:
                def cpb(v, b=b, bank=bank):
                    for j in range(4):
                        c = 4 * b + j
                        o = coloff_fn(c)
                        v.tensor_copy(out=dst[:, o:o + 128],
                                      in_=trp[:, bank * 1024 + j * 128: bank * 1024 + (j + 1) * 128])
            sc.step("V", cpb, reads=[f"trp{bank}"], writes=[dstw], tag=f"{tag}.cp{b}")

    for T in range(NT):
        load_x(T, T, f"ph0b.x{T}")
        sc.step("V", (lambda v, T=T: emit_xn(v, T, T)),
                reads=[f"xstg{T % 2}", "mn_all", "rs_all"], writes=["xn_st"],
                tag=f"ph0b.xn{T}")
        sc.step("A", (lambda a: a.activation(out=xnbf[:, :], in_=xn_st[:, :],
                                             func=AF.Copy)),
                reads=["xn_st"], writes=["xnbf"],
                tag=f"ph0b.cast{T}")
        tr16(xnbf, xn_fm, "xn_fm", None, f"ph0b.t{T}", ["xnbf"],
             dst3d=(xn_fm[:, :].rearrange("p (c n) -> p c n", n=N), T * 128))

    q_fm = salloc("q_fm", [128, H * N], BF16, side="right")
    k_fm = salloc("k_fm", [128, N], BF16, side="right")
    vstage = salloc("vstage", [128, N], BF16, side="right")
    v_tok = salloc("v_tok", [128, N], BF16, side="right")
    wpan = salloc("wpan", [128, 2 * E], BF16)
    qk_ps = palloc("qk_ps", [128, 2 * 512])

    for po in range(QC):
        sc.step("G", (lambda g, sem, po=po:
                      g.dma_start(out=wpan[:, (po % 2) * E:(po % 2 + 1) * E]
                                  .rearrange("p (c f) -> p c f", f=128),
                                  in_=wqkv_h[:, po * 128:(po + 1) * 128]
                                  .rearrange("(c p) f -> p c f", p=128))
                      .then_inc(sem, 16)),
                writes=[f"wpan{po % 2}"], ndma=1, tag=f"ph1.w{po}")
        for sp in range(4):
            idx = po * 4 + sp

            def mm16(t, po=po, sp=sp, idx=idx):
                last = None
                for kc in range(EC):
                    last = t.matmul(
                        qk_ps[:, (idx % 2) * 512:(idx % 2 + 1) * 512],
                        wpan[:, (po % 2) * E + kc * 128:(po % 2) * E + (kc + 1) * 128],
                        xn_fm[:, kc * N + sp * 512: kc * N + (sp + 1) * 512],
                        start=(kc == 0), stop=(kc == EC - 1))
                return last
            sc.step("T", mm16, reads=[f"wpan{po % 2}", "xn_fm"],
                    writes=[f"qk_ps{idx % 2}"], tag=f"ph1.mm{po}.{sp}")

            def evac(a, po=po, sp=sp, idx=idx):
                src = qk_ps[:, (idx % 2) * 512:(idx % 2 + 1) * 512]
                if po < 16:
                    dst = q_fm[:, po * N + sp * 512: po * N + (sp + 1) * 512]
                elif po == 16:
                    dst = k_fm[:, sp * 512:(sp + 1) * 512]
                else:
                    dst = vstage[:, sp * 512:(sp + 1) * 512]
                a.activation(out=dst, in_=src, func=AF.Identity, bias=bq_sb[:, po:po + 1])
            wres = "q_fm" if po < 16 else ("k_fm" if po == 16 else "vstage")
            sc.step("A", evac, reads=[f"qk_ps{idx % 2}", "bq_sb"], writes=[wres],
                    tag=f"ph1.ev{po}.{sp}")

    tr16(vstage, v_tok, "v_tok", (lambda c: c * 128), "ph1v", ["vstage"])

    sc.barrier("b12")
    free("qk_ps", "wpan", "xnbf", "xn_fm", "junk")

    attn_fm = salloc("attn_fm", [128, EC * N], BF16)
    probs_sb = salloc("probs_sb", [128, 2 * N], F32)
    probs_bf = salloc("probs_bf", [128, 2 * N], BF16)
    probsT = salloc("probsT", [128, 2 * N], BF16)
    rm2 = salloc("rm2", [128, 2], F32)
    nb2 = salloc("nb2", [128, 2], F32)
    rsum2 = salloc("rsum2", [128, 2], F32)
    rinv2 = salloc("rinv2", [128, 2], F32)
    sc_ps = palloc("sc_ps", [128, N])
    av_ps = palloc("av_ps", [128, 128])

    for T in range(NT):
        L = (T + 1) * 128
        nb4 = (T + 4) // 4
        for h in range(H):
            hs = h % 2

            def mm_sc(t, T=T, h=h, L=L, nb4=nb4):
                last = None
                for c in range(nb4):
                    w = min(512, L - c * 512)
                    last = t.matmul(sc_ps[:, c * 512:c * 512 + w],
                                    q_fm[:, h * N + T * 128: h * N + (T + 1) * 128],
                                    k_fm[:, c * 512:c * 512 + w],
                                    start=True, stop=True)
                return last
            sc.step("T", mm_sc, reads=["q_fm", "k_fm"], writes=["sc_ps"],
                    tag=f"ph2.sc{T}.{h}")

            def pre(v, T=T, L=L, hs=hs):
                v.tensor_add(sc_ps[:, T * 128:(T + 1) * 128],
                             sc_ps[:, T * 128:(T + 1) * 128], cmask[:, :])
                v.drain()
                v.reduce_max(rm2[:, hs:hs + 1], sc_ps[:, 0:L], axis=AX.X)
                v.drain()
                v.tensor_scalar_mul(nb2[:, hs:hs + 1], rm2[:, hs:hs + 1], -SM)
            sc.step("V", pre, reads=["sc_ps", "cmask"],
                    writes=["sc_ps", f"rm{hs}"], tag=f"ph2.pre{T}.{h}")

            def expo(a, L=L, hs=hs):
                a.activation(out=probs_sb[:, hs * N: hs * N + L], in_=sc_ps[:, 0:L],
                             func=AF.Exp, scale=SM, bias=nb2[:, hs:hs + 1],
                             accum_out=rsum2[:, hs:hs + 1])
            sc.step("A", expo, reads=["sc_ps", f"rm{hs}"],
                    writes=[f"probs{hs}", f"rsum{hs}"], tag=f"ph2.exp{T}.{h}")

            def cast(v, L=L, hs=hs):
                v.reciprocal(rinv2[:, hs:hs + 1], rsum2[:, hs:hs + 1])
                v.drain()
                v.tensor_single_scalar(probs_bf[:, hs * N: hs * N + L],
                                       probs_sb[:, hs * N: hs * N + L],
                                       rinv2[:, hs:hs + 1], ALU.mult)
            sc.step("V", cast, reads=[f"probs{hs}", f"rsum{hs}"],
                    writes=[f"probsbf{hs}"], tag=f"ph2.cast{T}.{h}")

            for b in range(nb4):
                ntr = min(4, (T + 1) - 4 * b)

                bank = b % 2

                def trb(t, b=b, ntr=ntr, bank=bank, hs=hs):
                    last = None
                    for j in range(ntr):
                        kc = 4 * b + j
                        last = t.transpose(trp[:, bank * 1024 + j * 128: bank * 1024 + (j + 1) * 128],
                                           probs_bf[:, hs * N + kc * 128: hs * N + (kc + 1) * 128],
                                           ident[:, :])
                    return last
                sc.step("T", trb, reads=[f"probsbf{hs}", "ident"], writes=[f"trp{bank}"],
                        tag=f"ph2.tr{T}.{h}.{b}")

                def cpb(v, b=b, ntr=ntr, bank=bank, hs=hs):
                    v.tensor_copy(out=probsT[:, hs * N + 4 * b * 128: hs * N + (4 * b + ntr) * 128],
                                  in_=trp[:, bank * 1024: bank * 1024 + ntr * 128])
                sc.step("V", cpb, reads=[f"trp{bank}"], writes=[f"probsT{hs}"],
                        tag=f"ph2.cp{T}.{h}.{b}")

            def mm_av(t, T=T, hs=hs):
                last = None
                for kc in range(T + 1):
                    last = t.matmul(av_ps[:, :], v_tok[:, kc * 128:(kc + 1) * 128],
                                    probsT[:, hs * N + kc * 128: hs * N + (kc + 1) * 128],
                                    start=(kc == 0), stop=(kc == T))
                return last
            sc.step("T", mm_av, reads=["v_tok", f"probsT{hs}"], writes=["av_ps"],
                    tag=f"ph2.av{T}.{h}")

            def ev_av(a, T=T, h=h):
                a.activation(out=attn_fm[:, h * N + T * 128: h * N + (T + 1) * 128],
                             in_=av_ps[:, :], func=AF.Copy)
            sc.step("A", ev_av, reads=["av_ps"], writes=["attn_fm"],
                    tag=f"ph2.ev{T}.{h}")

    sc.barrier("b23")
    free("av_ps", "sc_ps", "rinv2", "rsum2", "nb2", "rm2", "probsT", "probs_bf",
         "probs_sb", "v_tok", "vstage", "k_fm", "q_fm")

    wfc_bf = salloc("wfc_bf", [128, EC * E], BF16, side="right")
    bfc_row = salloc("bfc_row", [1, E], F32)
    ones1 = salloc("ones1", [1, 128], F32)
    bfc_bc = salloc("bfc_bc", [128, E], F32)
    ares_st = salloc("ares_st", [128, 2 * E], F32)
    fc_ps = palloc("fc_ps", [128, 2 * 512])

    sc.step("G", (lambda g, sem: g.dma_start(out=bfc_row[:, :],
                                        in_=bfc_h[:].rearrange("(o f) -> o f", o=1))
                  .then_inc(sem, 16)), writes=["bfc_row"], ndma=1, tag="ph3.b")
    sc.step("V", lambda v: v.memset(ones1[:, :], 1.0), writes=["ones1"], tag="ph3.o")
    for es in range(4):
        sc.step("T", (lambda t, es=es:
                      t.matmul(fc_ps[:, (es % 2) * 512:(es % 2 + 1) * 512],
                               ones1[:, :], bfc_row[:, es * 512:(es + 1) * 512],
                               start=True, stop=True)),
                reads=["ones1", "bfc_row"], writes=[f"fc_ps{es % 2}"],
                tag=f"ph3.bm{es}")
        sc.step("A", (lambda a, es=es:
                      a.activation(out=bfc_bc[:, es * 512:(es + 1) * 512],
                                   in_=fc_ps[:, (es % 2) * 512:(es % 2 + 1) * 512],
                                   func=AF.Copy)),
                reads=[f"fc_ps{es % 2}"], writes=["bfc_bc"], tag=f"ph3.be{es}")

    for kc in range(EC):
        sc.step("G", (lambda g, sem, kc=kc:
                      g.dma_start(out=wfc_bf[:, kc * E:(kc + 1) * E],
                                  in_=wfc_h[kc * 128:(kc + 1) * 128, :])
                      .then_inc(sem, 16)),
                writes=["wfc_bf"], ndma=1, tag=f"ph3.w{kc}")

    for T in range(NT):
        load_x(T, T, f"ph3.x{T}")
        for es in range(4):
            idx = T * 4 + es

            def mmfc(t, T=T, es=es, idx=idx):
                last = None
                for kc in range(EC):
                    last = t.matmul(
                        fc_ps[:, (idx % 2) * 512:(idx % 2 + 1) * 512],
                        attn_fm[:, kc * N + T * 128: kc * N + (T + 1) * 128],
                        wfc_bf[:, kc * E + es * 512: kc * E + (es + 1) * 512],
                        start=(kc == 0), stop=(kc == EC - 1))
                return last
            sc.step("T", mmfc, reads=["attn_fm", "wfc_bf"], writes=[f"fc_ps{idx % 2}"],
                    tag=f"ph3.mm{T}.{es}")

            def evfc(v, T=T, es=es, idx=idx):
                dst = ares_st[:, (T % 2) * E + es * 512:(T % 2) * E + (es + 1) * 512]
                v.scalar_tensor_tensor(
                    out=dst, in0=fc_ps[:, (idx % 2) * 512:(idx % 2 + 1) * 512],
                    scalar=1.0, in1=bfc_bc[:, es * 512:(es + 1) * 512],
                    op0=ALU.mult, op1=ALU.add)
                v.drain()
                v.tensor_add(dst, dst,
                             xstg[:, (T % 2) * E + es * 512:(T % 2) * E + (es + 1) * 512])
            sc.step("V", evfc, reads=[f"fc_ps{idx % 2}", "bfc_bc", f"xstg{T % 2}"],
                    writes=[f"ares{T % 2}"], tag=f"ph3.ev{T}.{es}")

        sc.step("G", (lambda g, sem, T=T:
                      g.dma_start(out=ares_h[T * 128:(T + 1) * 128, :],
                                  in_=ares_st[:, (T % 2) * E:(T % 2 + 1) * E])
                      .then_inc(sem, 16)),
                reads=[f"ares{T % 2}"], writes=[f"aresh{T}"], ndma=1, tag=f"ph3.s{T}")

    sc.barrier("b34")
    free("fc_ps", "ares_st", "bfc_bc", "ones1", "bfc_row", "wfc_bf", "attn_fm",
         "xstg")

    xq_fm = salloc("xq_fm", [128, EC * (N // 2)], BF16)
    res = salloc("res", [128, HT * E], BF16)
    wgT = salloc("wgT", [128, EC * OSTRIP], BF16)
    wuT = salloc("wuT", [128, EC * OSTRIP], BF16)
    wdT = salloc("wdT", [128, 4 * E], BF16)
    ring = salloc("ring", [128, 4 * E], F32)
    qbf = salloc("qbf", [128, 2 * E], BF16)
    interall = salloc("interall", [128, HT * 512], F32)
    sg_t = salloc("sg_t", [128, 512], F32)
    qs0A = salloc("qs0A", [128, 1024], F32)
    qs0B = salloc("qs0B", [128, 1024], F32)
    qs0C = salloc("qs0C", [128, 1024], F32)
    qs1A = salloc("qs1A", [128, 1024], F32)
    qs1B = salloc("qs1B", [128, 1024], F32)
    qs1C = salloc("qs1C", [128, 1024], F32)
    QSETS = [(qs0A, qs0B, qs0C), (qs1A, qs1B, qs1C)]
    iv4 = salloc("iv4", [128, 512], F32)
    sc4 = salloc("sc4", [128, 512], F32)
    am4 = qs1C[:, 0:512]
    zt4 = qs0A[:, 0:512]
    ft4 = qs0B[:, 0:512]
    ft4b = qs0C[:, 0:512]
    xqbf = qbf[:, 0:E]
    intq = qbf[:, 2 * 512:3 * 512]
    intT = qbf[:, 3 * 512:4 * 512]
    _QP = [0]

    def q_body(src_full, src_res, dst_full, dst_res, fmt, iv_off, tag,
               extra_dst_res=()):
        F = src_full.shape[-1]
        tinv = 8.0 if fmt == "act" else 2.0
        mn = 7.5 if fmt == "act" else 6.0
        shT = sh16 if fmt == "act" else sh4
        nh = (F + 1023) // 1024
        for hh in range(nh):
            Fh = min(1024, F - hh * 1024)
            nb = Fh // 32
            p = _QP[0] % 2
            _QP[0] += 1
            qA, qB, qC = QSETS[p]
            qr = f"qs{p}"
            A, B, Cb = qA[:, 0:Fh], qB[:, 0:Fh], qC[:, 0:Fh]
            srch = src_full[:, hh * 1024: hh * 1024 + Fh]
            dsth = dst_full[:, hh * 1024: hh * 1024 + Fh]
            ivc = iv4[:, iv_off + hh * 32: iv_off + hh * 32 + nb]
            scc = sc4[:, iv_off + hh * 32: iv_off + hh * 32 + nb]
            ivb = ivc.rearrange("p (b o) -> p b o", o=1).broadcast_to([128, nb, 32])
            scb = scc.rearrange("p (b o) -> p b o", o=1).broadcast_to([128, nb, 32])
            s3 = srch.rearrange("p (b k) -> p b k", k=32)

            t2 = tinv * tinv

            def v1(v, A=A, s3=s3, ivb=ivb):
                v.tensor_tensor(out=A.rearrange("p (b k) -> p b k", k=32),
                                in0=s3, in1=ivb, op=ALU.mult)
            sc.step("V", v1, reads=list(src_res) + ["iv4"], writes=[qr],
                    tag=f"{tag}.v1")

            def vik(v, A=A, B=B, t2=t2):
                v._custom_dve(_MXQ["MXQ_IK"], out=B, in0=A, s0=4.0 * t2,
                              s1=16.0 * t2, imm2=0.5)
            sc.step("V", vik, reads=[qr], writes=[qr], tag=f"{tag}.vik")

            def g1b(g, A=A, B=B, Cb=Cb):
                g.tensor_mul(Cb, A, B)
            sc.step("G", g1b, reads=[qr], writes=[qr], tag=f"{tag}.g1")

            def v2(v, A=A, B=B, Cb=Cb, fmt=fmt, shT=shT, t2=t2):
                if fmt == "act":
                    v._custom_dve(_MXQ["MXQ_EX3"], out=B, in0=Cb, in1=shT[:, :],
                                  s0=16.0, s1=8.0)
                    v.drain()
                    v._custom_dve(_MXQ["MXQ_EX3"], out=B, in0=B, in1=shift0[:, :],
                                  s0=4.0, s1=2.0)
                else:
                    v._custom_dve(_MXQ["MXQ_EX3"], out=B, in0=Cb, in1=shT[:, :],
                                  s0=4.0, s1=2.0)
                v.drain()
                v._custom_dve(_MXQ["MXQ_TIE"], out=B, in0=B, in1=Cb, s1=0.5)
                v.drain()
                v._custom_dve(_MXQ["MXQ_UNF"], out=Cb, in0=A, in1=B,
                              s0=4.0 * t2, s1=16.0 * t2, imm2=0.0)
            sc.step("V", v2, reads=[qr, "shifts"], writes=[qr], tag=f"{tag}.v2")

            def v3(v, A=A, Cb=Cb, dsth=dsth, scb=scb):
                v.tensor_single_scalar(A, Cb, mn * tinv, ALU.min)
                v.drain()
                v.scalar_tensor_tensor(out=dsth.rearrange("p (b k) -> p b k", k=32),
                                       in0=A.rearrange("p (b k) -> p b k", k=32),
                                       scalar=-mn * tinv, in1=scb, op0=ALU.max,
                                       op1=ALU.mult)
            sc.step("V", v3, reads=[qr, "sc4"],
                    writes=[qr, dst_res] + list(extra_dst_res), tag=f"{tag}.v3")
    g_ps = palloc("g_ps", [128, 2 * 512])
    u_ps = palloc("u_ps", [128, 2 * 512])
    dn_ps = palloc("dn_ps", [128, 2 * 512])

    def rg(i):
        return ring[:, i * E:(i + 1) * E]


    for hf in range(2):
        T0 = hf * HT
        for i in range(HT):
            T = T0 + i
            sc.step("S", (lambda s, sem, T=T, i=i:
                          s.dma_start(out=rg(i % 2), in_=x_h[T * 128:(T + 1) * 128, :])
                          .then_inc(sem, 16)),
                    writes=[f"ring{i % 2}"], ndma=1, tag=f"p4a{hf}.x{i}")

            def am(v, T=T, i=i):
                v.tensor_scalar(out=xn_st[:, :], in0=rg(i % 2),
                                scalar1=mn_all[:, T:T + 1], scalar2=rs_all[:, T:T + 1],
                                op0=ALU.subtract, op1=ALU.mult)
                v.drain()
                v.tensor_reduce(out=am4[:, i * 64:(i + 1) * 64],
                                in_=xn_st[:, :].rearrange("p (b k) -> p b k", k=32),
                                axis=AX.X, op=ALU.max, apply_absolute_value=True)
            sc.step("V", am, reads=[f"ring{i % 2}", "mn_all", "rs_all"],
                    writes=["xn_st", "qs1"], tag=f"p4a{hf}.am{i}")

        sc.step("V", (lambda v: emit_scale_chain(v, am4[:, :], zt4, ft4, ft4b, iv4[:, :], sc4[:, :],
                                                 24, CH_STD, post=8.0)),
                reads=["qs1"], writes=["iv4", "sc4", "qs0"],
                tag=f"p4a{hf}.ch")

        for i in range(HT):
            T = T0 + i
            sc.step("S", (lambda s, sem, T=T, i=i:
                          s.dma_start(out=rg(i % 2), in_=x_h[T * 128:(T + 1) * 128, :])
                          .then_inc(sem, 16)),
                    writes=[f"ring{i % 2}"], ndma=1, tag=f"p4b{hf}.x{i}")

            def qx(v, T=T, i=i):
                v.tensor_scalar(out=xn_st[:, :], in0=rg(i % 2),
                                scalar1=mn_all[:, T:T + 1], scalar2=rs_all[:, T:T + 1],
                                op0=ALU.subtract, op1=ALU.mult)
            sc.step("V", qx, reads=[f"ring{i % 2}", "mn_all", "rs_all"],
                    writes=["xn_st"], tag=f"p4b{hf}.ln{i}")
            q_body(xn_st[:, :], ["xn_st"], xqbf, "qbf0", "act", i * 64,
                   f"p4b{hf}.q{i}")
            tr16(xqbf, xq_fm, "xq_fm", None, f"p4b{hf}.t{i}", ["qbf0"],
                 dst3d=(xq_fm[:, :].rearrange("p (c n) -> p c n", n=N // 2), i * 128))
            sc.step("V", (lambda v, i=i: v.memset(res[:, i * E:(i + 1) * E], 0.0)),
                    writes=[f"res{i}"], tag=f"p4b{hf}.z{i}")

        for s_i in range(NSTRIP):
            o0 = s_i * OSTRIP
            if hf == 0:
                for m in range(2):
                    srcw = wg_h if m == 0 else wu_h
                    dstc = wgq_h if m == 0 else wuq_h
                    dstTn = "wgT" if m == 0 else "wuT"
                    dstT = wgT if m == 0 else wuT
                    for it in range(4):
                        sc.step("S", (lambda s, sem, srcw=srcw, o0=o0, it=it:
                                      s.dma_start(out=rg(it),
                                                  in_=srcw[o0 + it * 128:o0 + (it + 1) * 128, :])
                                      .then_inc(sem, 16)),
                                writes=[f"ring{it}"], ndma=1, tag=f"w{m}l{s_i}.{it}")
                        sc.step("V", (lambda v, it=it:
                                      v.tensor_reduce(out=am4[:, it * 64:(it + 1) * 64],
                                                      in_=rg(it).rearrange("p (b k) -> p b k", k=32),
                                                      axis=AX.X, op=ALU.max,
                                                      apply_absolute_value=True)),
                                reads=[f"ring{it}"], writes=["qs1"], tag=f"w{m}a{s_i}.{it}")
                    sc.step("V", (lambda v: emit_scale_chain(v, am4[:, 0:256],
                                                             zt4[:, 0:256], ft4[:, 0:256],
                                                             ft4b[:, 0:256], iv4[:, 0:256],
                                                             sc4[:, 0:256], 24, CH_STD,
                                                             post=2.0)),
                            reads=["qs1"], writes=["iv4", "sc4", "qs0"],
                            tag=f"w{m}c{s_i}")
                    for it in range(4):
                        q_body(rg(it), [f"ring{it}"],
                               qbf[:, (it % 2) * E:(it % 2 + 1) * E],
                               f"qbf{it % 2}", "wgt", it * 64,
                               f"w{m}q{s_i}.{it}")
                        sc.step("G", (lambda g, sem, dstc=dstc, o0=o0, it=it:
                                      g.dma_start(out=dstc[o0 + it * 128:o0 + (it + 1) * 128, :],
                                                  in_=qbf[:, (it % 2) * E:(it % 2 + 1) * E])
                                      .then_inc(sem, 16)),
                                reads=[f"qbf{it % 2}"], writes=[f"c{m}.{s_i}.{it}"],
                                ndma=1, tag=f"w{m}cw{s_i}.{it}")
                        tr16(qbf[:, (it % 2) * E:(it % 2 + 1) * E], dstT, dstTn,
                             None, f"w{m}t{s_i}.{it}", [f"qbf{it % 2}"],
                             dst3d=(dstT[:, :].rearrange("p (c n) -> p c n", n=OSTRIP),
                                    it * 128))
                for j in range(16):
                    sc.step("S", (lambda s, sem, o0=o0, j=j:
                                  s.dma_start(out=ring[:, j * 512:(j + 1) * 512],
                                              in_=wd_h[j * 128:(j + 1) * 128, o0:o0 + OSTRIP])
                                  .then_inc(sem, 16)),
                            writes=[f"ring{j // 4}"], ndma=1, tag=f"wdl{s_i}.{j}")
                    sc.step("V", (lambda v, j=j:
                                  v.tensor_reduce(out=am4[:, j * 16:(j + 1) * 16],
                                                  in_=ring[:, j * 512:(j + 1) * 512]
                                                  .rearrange("p (b k) -> p b k", k=32),
                                                  axis=AX.X, op=ALU.max,
                                                  apply_absolute_value=True)),
                            reads=[f"ring{j // 4}"], writes=["qs1"], tag=f"wda{s_i}.{j}")
                sc.step("V", (lambda v: emit_scale_chain(v, am4[:, 0:256], zt4[:, 0:256],
                                                         ft4[:, 0:256], ft4b[:, 0:256],
                                                         iv4[:, 0:256], sc4[:, 0:256],
                                                         24, CH_STD, post=2.0)),
                        reads=["qs1"], writes=["iv4", "sc4", "qs0"],
                        tag=f"wdc{s_i}")
                for j in range(16):
                    q_body(ring[:, j * 512:(j + 1) * 512], [f"ring{j // 4}"],
                           qbf[:, (j % 8) * 512:(j % 8 + 1) * 512],
                           f"qbf{(j % 8) // 4}", "wgt", j * 16,
                           f"wdq{s_i}.{j}", extra_dst_res=[f"wdq{s_i}.{j}"])
                    sc.step("G", (lambda g, sem, o0=o0, j=j:
                                  g.dma_start(out=wdq_h[j * 128:(j + 1) * 128, o0:o0 + OSTRIP],
                                              in_=qbf[:, (j % 8) * 512:(j % 8 + 1) * 512])
                                  .then_inc(sem, 16)),
                            reads=[f"wdq{s_i}.{j}", f"qbf{(j % 8) // 4}"],
                            writes=[f"cd.{s_i}.{j}"], ndma=1, tag=f"wdcw{s_i}.{j}")

                    bank = j % 2

                    def trd(t, j=j, bank=bank):
                        last = None
                        for oc in range(4):
                            last = t.transpose(
                                trp[:, bank * 1024 + oc * 128: bank * 1024 + (oc + 1) * 128],
                                qbf[:, (j % 8) * 512 + oc * 128:(j % 8) * 512 + (oc + 1) * 128],
                                ident[:, :])
                        return last
                    sc.step("T", trd, reads=[f"wdq{s_i}.{j}", f"qbf{(j % 8) // 4}",
                                             "ident"], writes=[f"trp{bank}"],
                            tag=f"wdt{s_i}.{j}")

                    def cpd(v, j=j, bank=bank):
                        v.tensor_copy(out=wdT[:, :].rearrange("p (oc e) -> p oc e", e=E)
                                      [:, :, j * 128:(j + 1) * 128],
                                      in_=trp[:, bank * 1024: bank * 1024 + 512]
                                      .rearrange("p (oc f) -> p oc f", f=128))
                    sc.step("V", cpd, reads=[f"trp{bank}"], writes=["wdT"], tag=f"wdcp{s_i}.{j}")
            else:
                for m in range(2):
                    srcc = wgq_h if m == 0 else wuq_h
                    dstTn = "wgT" if m == 0 else "wuT"
                    dstT = wgT if m == 0 else wuT
                    for it in range(4):
                        sc.step("S", (lambda s, sem, srcc=srcc, o0=o0, it=it:
                                      s.dma_start(out=qbf[:, (it % 2) * E:(it % 2 + 1) * E],
                                                  in_=srcc[o0 + it * 128:o0 + (it + 1) * 128, :])
                                      .then_inc(sem, 16)),
                                reads=[f"c{m}.{s_i}.{it}"],
                                writes=[f"qbf{it % 2}"],
                                ndma=1, tag=f"w{m}lb{s_i}.{it}")
                        tr16(qbf[:, (it % 2) * E:(it % 2 + 1) * E], dstT, dstTn,
                             None, f"w{m}t{s_i}b.{it}", [f"qbf{it % 2}"],
                             dst3d=(dstT[:, :].rearrange("p (c n) -> p c n", n=OSTRIP),
                                    it * 128))
                for j in range(16):
                    sc.step("S", (lambda s, sem, o0=o0, j=j:
                                  s.dma_start(out=qbf[:, (j % 8) * 512:(j % 8 + 1) * 512],
                                              in_=wdq_h[j * 128:(j + 1) * 128, o0:o0 + OSTRIP])
                                  .then_inc(sem, 16)),
                            reads=[f"cd.{s_i}.{j}"],
                            writes=[f"qbf{(j % 8) // 4}", f"wdq{s_i}b.{j}"],
                            ndma=1, tag=f"wdlb{s_i}.{j}")

                    bank = j % 2

                    def trd(t, j=j, bank=bank):
                        last = None
                        for oc in range(4):
                            last = t.transpose(
                                trp[:, bank * 1024 + oc * 128: bank * 1024 + (oc + 1) * 128],
                                qbf[:, (j % 8) * 512 + oc * 128:(j % 8) * 512 + (oc + 1) * 128],
                                ident[:, :])
                        return last
                    sc.step("T", trd, reads=[f"wdq{s_i}b.{j}", f"qbf{(j % 8) // 4}",
                                             "ident"], writes=[f"trp{bank}"],
                            tag=f"wdtb{s_i}.{j}")

                    def cpd(v, j=j, bank=bank):
                        v.tensor_copy(out=wdT[:, :].rearrange("p (oc e) -> p oc e", e=E)
                                      [:, :, j * 128:(j + 1) * 128],
                                      in_=trp[:, bank * 1024: bank * 1024 + 512]
                                      .rearrange("p (oc f) -> p oc f", f=128))
                    sc.step("V", cpd, reads=[f"trp{bank}"], writes=["wdT"], tag=f"wdcpb{s_i}.{j}")

            for tt in range(HT):
                sl = tt % 2

                def mmg(t, tt=tt, sl=sl):
                    last = None
                    for hc in range(EC):
                        last = t.matmul(
                            g_ps[:, sl * 512:(sl + 1) * 512],
                            xq_fm[:, hc * (N // 2) + tt * 128: hc * (N // 2) + (tt + 1) * 128],
                            wgT[:, hc * OSTRIP:(hc + 1) * OSTRIP],
                            start=(hc == 0), stop=(hc == EC - 1))
                    return last
                sc.step("T", mmg, reads=["xq_fm", "wgT"], writes=[f"g_ps{sl}"],
                        tag=f"ffg{hf}.{s_i}.{tt}")
                sc.step("A", (lambda a, sl=sl:
                              a.activation(out=sg_t[:, :],
                                           in_=g_ps[:, sl * 512:(sl + 1) * 512],
                                           func=AF.Silu)),
                        reads=[f"g_ps{sl}"], writes=["sg_t"], tag=f"ffs{hf}.{s_i}.{tt}")

                def mmu(t, tt=tt, sl=sl):
                    last = None
                    for hc in range(EC):
                        last = t.matmul(
                            u_ps[:, sl * 512:(sl + 1) * 512],
                            xq_fm[:, hc * (N // 2) + tt * 128: hc * (N // 2) + (tt + 1) * 128],
                            wuT[:, hc * OSTRIP:(hc + 1) * OSTRIP],
                            start=(hc == 0), stop=(hc == EC - 1))
                    return last
                sc.step("T", mmu, reads=["xq_fm", "wuT"], writes=[f"u_ps{sl}"],
                        tag=f"ffu{hf}.{s_i}.{tt}")

                def mul(v, tt=tt, sl=sl):
                    v.tensor_mul(interall[:, tt * 512:(tt + 1) * 512],
                                 sg_t[:, :],
                                 u_ps[:, sl * 512:(sl + 1) * 512])
                    v.drain()
                    v.tensor_reduce(out=am4[:, 256 + tt * 16: 256 + (tt + 1) * 16],
                                    in_=interall[:, tt * 512:(tt + 1) * 512]
                                    .rearrange("p (b k) -> p b k", k=32),
                                    axis=AX.X, op=ALU.max, apply_absolute_value=True)
                sc.step("V", mul, reads=["sg_t", f"u_ps{sl}"], writes=["interall", "qs1"],
                        tag=f"ffm{hf}.{s_i}.{tt}")

            sc.step("V", (lambda v: emit_scale_chain(v, am4[:, 256:384], zt4[:, 0:128],
                                                     ft4[:, 0:128], ft4b[:, 0:128],
                                                     iv4[:, 256:384], sc4[:, 256:384],
                                                     43, CH_INT, post=8.0)),
                    reads=["qs1"], writes=["iv4", "sc4", "qs0"],
                    tag=f"ffc{hf}.{s_i}")

            for tt in range(HT):
                q_body(interall[:, tt * 512:(tt + 1) * 512], ["interall"],
                       intq, "qbf0", "act", 256 + tt * 16,
                       f"ffq{hf}.{s_i}.{tt}")

                bank = tt % 2

                def tri(t, bank=bank):
                    last = None
                    for oc in range(4):
                        last = t.transpose(trp[:, bank * 1024 + oc * 128: bank * 1024 + (oc + 1) * 128],
                                           qbf[:, 2 * 512 + oc * 128:2 * 512 + (oc + 1) * 128],
                                           ident[:, :])
                    return last
                sc.step("T", tri, reads=["qbf0", "ident"], writes=[f"trp{bank}"],
                        tag=f"fft{hf}.{s_i}.{tt}")
                sc.step("V", (lambda v, bank=bank:
                              v.tensor_copy(out=intT, in_=trp[:, bank * 1024: bank * 1024 + 512])),
                        reads=[f"trp{bank}"], writes=["qbf0"], tag=f"ffcp{hf}.{s_i}.{tt}")

                for es in range(4):
                    def mmd(t, es=es):
                        last = None
                        for oc in range(4):
                            last = t.matmul(
                                dn_ps[:, (es % 2) * 512:(es % 2 + 1) * 512],
                                qbf[:, 3 * 512 + oc * 128:3 * 512 + (oc + 1) * 128],
                                wdT[:, oc * E + es * 512: oc * E + (es + 1) * 512],
                                start=(oc == 0), stop=(oc == 3))
                        return last
                    sc.step("T", mmd, reads=["qbf0", "wdT"], writes=[f"dn_ps{es % 2}"],
                            tag=f"ffd{hf}.{s_i}.{tt}.{es}")

                    def acc(v, tt=tt, es=es):
                        dst = res[:, tt * E + es * 512: tt * E + (es + 1) * 512]
                        v.tensor_add(dst, dst,
                                     dn_ps[:, (es % 2) * 512:(es % 2 + 1) * 512])
                    sc.step("V", acc, reads=[f"dn_ps{es % 2}"], writes=[f"res{tt}"],
                            tag=f"ffa{hf}.{s_i}.{tt}.{es}")

        for tt in range(HT):
            T = T0 + tt
            sc.step("G", (lambda g, sem, T=T: g.dma_start(out=rg(0),
                                                     in_=ares_h[T * 128:(T + 1) * 128, :])
                          .then_inc(sem, 16)),
                    reads=[f"aresh{T}"], writes=["ring0"], ndma=1, tag=f"fin{hf}.l{tt}")
            sc.step("V", (lambda v, tt=tt: v.tensor_add(rg(1), rg(0),
                                                        res[:, tt * E:(tt + 1) * E])),
                    reads=["ring0", f"res{tt}"], writes=["ring1"], tag=f"fin{hf}.a{tt}")
            sc.step("G", (lambda g, sem, T=T: g.dma_start(out=out_h[T * 128:(T + 1) * 128, :],
                                                     in_=rg(1))
                          .then_inc(sem, 16)),
                    reads=["ring1"], writes=[], ndma=1, tag=f"fin{hf}.s{tt}")
        if hf == 0:
            sc.barrier("bhalf")

    blk = nc.Block()
    block = blk.__enter__()
    sc.emit(nc, block, SEMS)
    blk.__exit__(None, None, None)
    for g in semg:
        g.__exit__(None, None, None)
    for nm in reversed(list(live)):
        live.pop(nm).__exit__(None, None, None)
    mybir.codegen_inst_isa_subclasses(nc)
    return nc


_NC_CACHE = []


def kernel(L0_attn_x, L0_attn_w_qkv, L0_attn_b_qkv, L0_attn_w_fc, L0_attn_b_fc,
           L0_ffn_w_gate, L0_ffn_w_up, L0_ffn_w_down):
    import ml_dtypes
    if not _NC_CACHE:
        _NC_CACHE.append(build_program())
    nc = _NC_CACHE[0]
    x = np.ascontiguousarray(np.asarray(L0_attn_x, dtype=np.float32))
    wqkv = np.ascontiguousarray(np.asarray(L0_attn_w_qkv, dtype=np.float32))
    bqkv = np.ascontiguousarray(np.asarray(L0_attn_b_qkv, dtype=np.float32))
    wfc = np.ascontiguousarray(np.asarray(L0_attn_w_fc, dtype=np.float32))
    bfc = np.ascontiguousarray(np.asarray(L0_attn_b_fc, dtype=np.float32))
    wg = np.ascontiguousarray(np.asarray(L0_ffn_w_gate, dtype=np.float32))
    wu = np.ascontiguousarray(np.asarray(L0_ffn_w_up, dtype=np.float32))
    wd = np.ascontiguousarray(np.asarray(L0_ffn_w_down, dtype=np.float32))
    ident = np.eye(128, dtype=ml_dtypes.bfloat16)
    cmask = np.triu(np.full((128, 128), -1.0e30, np.float32), k=1)
    in_maps = []
    for b in range(8):
        in_maps.append({
            "x": np.ascontiguousarray(x[b]),
            "w_qkv": wqkv, "b_qkv": bqkv, "w_fc": wfc, "b_fc": bfc,
            "w_gate": wg, "w_up": wu, "w_down": wd,
            "ident": ident, "cmask": cmask,
        })
    res = run_bass_kernel_spmd(nc, in_maps, core_ids=list(range(8)))
    out = np.stack([np.asarray(r["out"], dtype=np.float32) for r in res.results], axis=0)
    return out


# version 08
# speedup vs baseline: 1.9648x; 1.0818x over previous
import numpy as np
import concourse.bass as bass
import concourse.mybir as mybir
from concourse.bass_utils import run_bass_kernel_spmd
import concourse.dve_ops as dve_ops
from concourse.dve_op import DveOp
from concourse.dve_spec import (
    Spec, Src0, Src1, C0, C1, C2, C3, One, Bin, AluOp, eq, lower as dve_lower,
)
from concourse.dve_table_gen import dve_ver_for

F32 = mybir.dt.float32
BF16 = mybir.dt.bfloat16
ALU = mybir.AluOpType
AX = mybir.AxisListType
AF = mybir.ActivationFunctionType

N = 2048
E = 2048
NT = N // 128
EC = E // 128
H = 16
QKVF = 2304
QC = QKVF // 128
I = 8192
OSTRIP = 512
NSTRIP = I // OSTRIP
HT = NT // 2
SM = 1.0 / float(np.sqrt(np.float32(128)))



def _cdt(a):
    return np.float64 if a.dtype == np.float64 else np.float32


def _ref_ik(in0, in1, s0, s1, imm2):
    f = _cdt(in0)
    a2 = in0.astype(f) * in0
    return (1.0 - (a2 >= s0) * imm2) * (1.0 - (a2 >= s1) * imm2)


def _ref_ex3(in0, in1, s0, s1, imm2):
    f = _cdt(in0)
    z = in0.astype(f) + in1
    b = (z >= s0).astype(f)
    z1 = z - b * s0
    b1 = (z1 >= s1).astype(f)
    return z1 - b1 * s1


def _ref_tie(in0, in1, s0, s1, imm2):
    f = _cdt(in0)
    z0 = in0.astype(f)
    b0 = (z0 >= 1.0).astype(f)
    fr = z0 - b0
    c1 = (fr > s1).astype(f)
    e = (fr == s1).astype(f)
    return in1 + ((c1 - fr) + e * b0)


def _ref_unf(in0, in1, s0, s1, imm2):
    f = _cdt(in0)
    a2 = in0.astype(f) * in0
    kk = (1.0 + (a2 >= s0)) * (1.0 + (a2 >= s1))
    return np.minimum(in1 * kk, imm2)


def _mxq_specs():
    u2a = Src0 * Src0
    ik_body = (One - (u2a >= C0) * C2) * (One - (u2a >= C1) * C2)
    zs = Src0 + C3
    bb = zs >= C0
    zz1 = zs - bb * C0
    bb1 = zz1 >= C1
    ex3_body = zz1 - bb1 * C1
    b0 = Bin(AluOp.IS_GE, Src0, One)
    fr = Src0 - b0
    c1 = Bin(AluOp.IS_LT, C1, fr)
    e = eq(fr, C1)
    tie_body = Src1 + ((c1 - fr) + (e & b0))
    u2b = Src0 * Src0
    kk = (One + (u2b >= C0)) * (One + (u2b >= C1))
    unf_body = Bin(AluOp.MIN, Src1 * kk, C2)
    return {
        "MXQ_IK": (Spec(body=ik_body), _ref_ik),
        "MXQ_EX3": (Spec(body=ex3_body), _ref_ex3),
        "MXQ_TIE": (Spec(body=tie_body), _ref_tie),
        "MXQ_UNF": (Spec(body=unf_body), _ref_unf),
    }


_MXQ = {}


def _register_mxq():
    if _MXQ:
        return
    if any(op.name == "MXQ_IK" for op in dve_ops.OPS):
        for op in dve_ops.OPS:
            if op.name.startswith("MXQ_"):
                _MXQ[op.name] = op
        return
    ver = dve_ver_for("TRN2")
    replaceable = ["DVEHASH32L_SEED", "DVEHASH32L_BODY", "DVEHASH32L_DRAIN"]
    for name, (spec, ref) in _mxq_specs().items():
        compiled = dve_lower(spec, ver=ver, perf_en=False, name=name)
        op = DveOp(name, spec, subdim=False, uops_sha={ver: compiled.sha(ver)},
                   reference=ref)
        if max(dve_ops._SUB_OPCODE_FOR_NAME.values()) < 0x1F:
            row = max(dve_ops._SUB_OPCODE_FOR_NAME.values()) + 1
            dve_ops.OPS.append(op)
        else:
            victim = replaceable.pop()
            idx = next(i for i, o in enumerate(dve_ops.OPS) if o.name == victim)
            row = dve_ops._SUB_OPCODE_FOR_NAME.pop(victim)
            dve_ops.OPS[idx] = op
            dve_ops.CUSTOM_DVE_SPECS.pop(victim, None)
            dve_ops.CUSTOM_DVE_REFS.pop(victim, None)
        dve_ops._SUB_OPCODE_FOR_NAME[name] = row
        dve_ops.CUSTOM_DVE_SPECS[name] = spec
        dve_ops.CUSTOM_DVE_REFS[name] = op.reference
        _MXQ[name] = op



DMA_ENGS = {"S", "G"}
ALL_ENGS = ("S", "G", "T", "V", "A")


class Sched:
    DRAIN_DIST = 3

    def __init__(self):
        self.steps = []
        self.lw = {}
        self.rd = {}

    def step(self, eng, fn, reads=(), writes=(), ndma=0, tag="", flush=False):
        i = len(self.steps)
        st = {"eng": eng, "fn": fn, "deps": set(), "ndma": ndma, "tag": tag,
              "flush": flush}
        self.steps.append(st)
        for r in reads:
            for j in self.lw.get(r, ()):
                st["deps"].add(j)
            self.rd.setdefault(r, []).append(i)
        for w in writes:
            for j in self.lw.get(w, ()):
                st["deps"].add(j)
            readers = self.rd.get(w, ())
            for j in readers:
                if j != i:
                    st["deps"].add(j)
            if readers:
                self.lw[w] = [i]
            else:
                self.lw.setdefault(w, []).append(i)
            self.rd[w] = []
        return i

    def barrier(self, tag="bar"):
        toks = [f"_{tag}_{e}" for e in ALL_ENGS]
        for e, t in zip(ALL_ENGS, toks):
            self.step(e, lambda h: None, writes=[t], tag=f"{tag}.mark.{e}",
                      flush=(e in DMA_ENGS))
        for e in ALL_ENGS:
            self.step(e, lambda h: None, reads=list(toks), tag=f"{tag}.wait.{e}")

    def emit(self, nc, block, sems):
        steps = self.steps
        n = len(steps)
        ords = {}
        cnt = {e: 0 for e in ALL_ENGS}
        for i, st in enumerate(steps):
            ords[i] = cnt[st["eng"]]
            cnt[st["eng"]] += 1
        needed = [False] * n
        drain_before = [False] * n
        for i, st in enumerate(steps):
            for j in st["deps"]:
                if steps[j]["eng"] == st["eng"]:
                    if (ords[i] - ords[j] < self.DRAIN_DIST
                            and st["eng"] in ("V", "A", "G")):
                        drain_before[i] = True
                else:
                    needed[j] = True
        K = 6
        ev = [None] * n
        run = {e: 0 for e in ALL_ENGS}
        dma_ord = {e: 0 for e in DMA_ENGS}
        sem_tot = {(e, k): 0 for e in DMA_ENGS for k in range(K)}
        self_wait = [None] * n
        for i, st in enumerate(steps):
            e = st["eng"]
            if e in DMA_ENGS and st["ndma"]:
                k = dma_ord[e] % K
                dma_ord[e] += 1
                self_wait[i] = (k, sem_tot[(e, k)])
                sem_tot[(e, k)] += 16 * st["ndma"]
                ev[i] = (e, k, sem_tot[(e, k)])
            else:
                if needed[i]:
                    run[e] += 1
                ev[i] = (e, "C" if e in DMA_ENGS else None, run[e])
            if st.get("flush"):
                st["flush_vals"] = tuple(sem_tot[(e, k)] for k in range(K)) \
                    if e in DMA_ENGS else None
        totals = {e: [sem_tot[(e, k)] for k in range(K)] for e in DMA_ENGS}
        waits = [[] for _ in range(n)]
        for i, st in enumerate(steps):
            for j in st["deps"]:
                pe = steps[j]["eng"]
                pj, kj, vj = ev[j]
                if pe == st["eng"]:
                    if pe in DMA_ENGS and kj != "C":
                        waits[i].append((pe, kj, vj))
                    continue
                if pe in DMA_ENGS and kj != "C":
                    waits[i].append((pe, kj, vj))
                elif needed[j]:
                    waits[i].append((pe, kj, vj))
        seen = {}
        pruned = [[] for _ in range(n)]
        for i, st in enumerate(steps):
            e = st["eng"]
            best = {}
            for (pe, k2, v) in waits[i]:
                key = (pe, k2)
                best[key] = max(best.get(key, 0), v)
            for (pe, k2), v in sorted(best.items(), key=lambda kv: (kv[0][0], str(kv[0][1]))):
                gk = (e, pe, k2)
                if v == 0 or seen.get(gk, -1) >= v:
                    continue
                seen[gk] = v
                pruned[i].append((pe, k2, v))

        def semof(pe, k2):
            if k2 is None:
                return sems[pe]
            return sems[pe + ("C" if k2 == "C" else str(k2))]

        def runner(eng):
            def body(h):
                inc_since_wait = [False]
                ninc = [0]
                for i, st in enumerate(steps):
                    if st["eng"] != eng:
                        continue
                    if pruned[i]:
                        inc_since_wait[0] = False
                    for (pe, k2, v) in pruned[i]:
                        h.wait_ge(semof(pe, k2), v)
                    if drain_before[i]:
                        h.drain()
                    if st.get("flush") and st.get("flush_vals"):
                        for k2f, vf in enumerate(st["flush_vals"]):
                            if vf:
                                h.wait_ge(semof(eng, k2f), vf)
                        inc_since_wait[0] = False
                    if eng in DMA_ENGS and st["ndma"]:
                        k2, v0 = self_wait[i]
                        if v0:
                            h.wait_ge(semof(eng, k2), v0)
                        last = st["fn"](h, semof(eng, k2))
                    else:
                        last = st["fn"](h)
                        if needed[i]:
                            csem = sems[eng + "C"] if eng in DMA_ENGS else sems[eng]
                            if inc_since_wait[0]:
                                h.wait_ge(csem, ninc[0])
                            if eng == "T":
                                if last is not None:
                                    last.then_inc(csem, 1)
                                else:
                                    h.nop().then_inc(csem, 1)
                            else:
                                h.drain()
                                h.nop().then_inc(csem, 1)
                            ninc[0] += 1
                            inc_since_wait[0] = True
                if eng in DMA_ENGS:
                    for k2 in range(K):
                        if totals[eng][k2]:
                            h.wait_ge(semof(eng, k2), totals[eng][k2])
            return body

        block.sync(runner("S"))
        block.gpsimd(runner("G"))
        block.tensor(runner("T"))
        block.vector(runner("V"))
        block.scalar(runner("A"))




def emit_scale_chain(v, amax_ap, zt, ft, ft2, invs_ap, sca_ap, pre, js,
                     post=1.0):
    v.tensor_scalar_mul(zt, amax_ap, 2.0**pre)
    v.memset(invs_ap, 2.0**(pre - 1))
    for j in js:
        v.drain()
        v.tensor_single_scalar(ft, zt, 8.0 * 2.0**j, ALU.is_lt)
        v.tensor_scalar(out=ft2, in0=zt, scalar1=8.0 * 2.0**j,
                        scalar2=2.0**-j, op0=ALU.is_ge, op1=ALU.mult)
        v.drain()
        v.tensor_add(ft, ft, ft2)
        v.drain()
        v.tensor_mul(zt, zt, ft)
        v.tensor_mul(invs_ap, invs_ap, ft)
    v.drain()
    if post != 1.0:
        v.tensor_scalar_mul(invs_ap, invs_ap, post)
        v.drain()
    v.reciprocal(sca_ap, invs_ap)




CH_FULL = (64, 32, 16, 8, 4, 2, 1)
CH_STD = (16, 8, 4, 2, 1)
CH_INT = (32, 16, 8, 4, 2, 1)




def build_program():
    _register_mxq()
    nc = bass.Bass("TRN2", target_bir_lowering=False)

    x_h = nc.dram_tensor("x", [N, E], F32, kind="ExternalInput").ap()
    wqkv_h = nc.dram_tensor("w_qkv", [E, QKVF], F32, kind="ExternalInput").ap()
    bqkv_h = nc.dram_tensor("b_qkv", [QKVF], F32, kind="ExternalInput").ap()
    wfc_h = nc.dram_tensor("w_fc", [E, E], F32, kind="ExternalInput").ap()
    bfc_h = nc.dram_tensor("b_fc", [E], F32, kind="ExternalInput").ap()
    wg_h = nc.dram_tensor("w_gate", [I, E], F32, kind="ExternalInput").ap()
    wu_h = nc.dram_tensor("w_up", [I, E], F32, kind="ExternalInput").ap()
    wd_h = nc.dram_tensor("w_down", [E, I], F32, kind="ExternalInput").ap()
    ident_h = nc.dram_tensor("ident", [128, 128], BF16, kind="ExternalInput").ap()
    cmask_h = nc.dram_tensor("cmask", [128, 128], F32, kind="ExternalInput").ap()
    out_h = nc.dram_tensor("out", [N, E], F32, kind="ExternalOutput").ap()

    ares_h = nc.dram_tensor("ares_scr", [N, E], F32).ap()
    wgq_h = nc.dram_tensor("wgq_c", [I, E], BF16).ap()
    wuq_h = nc.dram_tensor("wuq_c", [I, E], BF16).ap()
    wdq_h = nc.dram_tensor("wdq_c", [E, I], BF16).ap()

    live = {}

    def salloc(name, shape, dtype, side=None):
        g = nc.sbuf_tensor(name, shape, dtype, side=side)
        t = g.__enter__()
        live[name] = g
        return t

    def palloc(name, shape, dtype=F32):
        g = nc.psum_tensor(name, shape, dtype)
        t = g.__enter__()
        live[name] = g
        return t

    def free(*names):
        for nm in names:
            live.pop(nm).__exit__(None, None, None)

    semg = []
    SEMS = {}
    for e in ("T", "V", "A"):
        g = nc.semaphore("sem" + e)
        SEMS[e] = g.__enter__()
        semg.append(g)
    for e in ("S", "G"):
        g = nc.semaphore(f"sem{e}C")
        SEMS[e + "C"] = g.__enter__()
        semg.append(g)
        for k in range(6):
            g = nc.semaphore(f"sem{e}{k}")
            SEMS[e + str(k)] = g.__enter__()
            semg.append(g)

    sc = Sched()

    ident = salloc("identt", [128, 128], BF16)
    cmask = salloc("cmaskt", [128, 128], F32)
    shift0 = salloc("shift0t", [128, 1], F32)
    sh16 = salloc("sh16", [128, 1], F32)
    sh4 = salloc("sh4", [128, 1], F32)
    bq_sb = salloc("bq_sb", [128, QC], F32)
    sm_all = salloc("sm_all", [128, NT], F32)
    sq_all = salloc("sq_all", [128, NT], F32)
    mn_all = salloc("mn_all", [128, NT], F32)
    va_all = salloc("va_all", [128, NT], F32)
    t_all = salloc("t_all", [128, NT], F32)
    sd_all = salloc("sd_all", [128, NT], F32)
    rs_all = salloc("rs_all", [128, NT], F32)
    xn_st = salloc("xn_st", [128, E], F32)
    trp_t = palloc("trp", [128, 2 * 1024], BF16)
    trp = trp_t

    def st_setup(s, sem):
        s.dma_start(out=ident[:, :], in_=ident_h[:, :]).then_inc(sem, 16)
        s.dma_start(out=cmask[:, :], in_=cmask_h[:, :]).then_inc(sem, 16)
        for c in range(QC):
            s.dma_start(out=bq_sb[:, c:c + 1],
                        in_=bqkv_h[c * 128:(c + 1) * 128]
                        .rearrange("(p o) -> p o", o=1)).then_inc(sem, 16)

    sc.step("S", st_setup, writes=["ident", "cmask", "bq_sb"], ndma=2 + QC)
    def v_sh(v):
        v.memset(shift0[:, :], 0.0)
        v.memset(sh16[:, :], 16.0)
        v.memset(sh4[:, :], 4.0)
    sc.step("V", v_sh, writes=["shifts"])

    xstg = salloc("xstg", [128, 2 * E], F32)
    junk = salloc("junk", [128, E], F32)
    xn_fm = salloc("xn_fm", [128, EC * N], BF16)
    xnbf = salloc("xnbf", [128, E], BF16)

    def xs(i):
        return xstg[:, (i % 2) * E:(i % 2 + 1) * E]

    def load_x(T, slot, tag):
        sc.step("S", (lambda s, sem, T=T, slot=slot:
                      s.dma_start(out=xs(slot), in_=x_h[T * 128:(T + 1) * 128, :])
                      .then_inc(sem, 16)),
                writes=[f"xstg{slot % 2}"], ndma=1, tag=tag)

    for T in range(NT):
        load_x(T, T, f"ph0a.x{T}")
        sc.step("A", (lambda a, T=T: a.activation(out=junk[:, :], in_=xs(T),
                                                  func=AF.Square,
                                                  accum_out=sq_all[:, T:T + 1])),
                reads=[f"xstg{T % 2}"], writes=["sq_all", "junk"], tag=f"ph0a.sq{T}")
        sc.step("V", (lambda v, T=T: v.reduce_sum(sm_all[:, T:T + 1], xs(T), axis=AX.X)),
                reads=[f"xstg{T % 2}"], writes=["sm_all"], tag=f"ph0a.sm{T}")

    def st_stats(v):
        v.tensor_scalar_mul(mn_all[:, :], sm_all[:, :], 1.0 / E)
        v.tensor_scalar_mul(va_all[:, :], sq_all[:, :], 1.0 / E)
        v.drain()
        v.tensor_mul(t_all[:, :], mn_all[:, :], mn_all[:, :])
        v.drain()
        v.tensor_sub(va_all[:, :], va_all[:, :], t_all[:, :])
        v.drain()
        v.tensor_scalar_add(va_all[:, :], va_all[:, :], 1e-5)

    sc.step("V", st_stats, reads=["sm_all", "sq_all"],
            writes=["mn_all", "va_all", "t_all"], tag="stats")
    sc.step("A", lambda a: a.sqrt(sd_all[:, :], va_all[:, :]), reads=["va_all"],
            writes=["sd_all"], tag="sqrt")
    sc.step("V", lambda v: v.reciprocal(rs_all[:, :], sd_all[:, :]), reads=["sd_all"],
            writes=["rs_all"], tag="rstd")

    def emit_xn(v, slot, T):
        v.tensor_scalar(out=xn_st[:, :], in0=xs(slot), scalar1=mn_all[:, T:T + 1],
                        scalar2=rs_all[:, T:T + 1], op0=ALU.subtract, op1=ALU.mult)

    def tr16(src, dst, dstw, coloff_fn, tag, src_res, dst3d=None, cpeng="V"):
        for b in range(4):
            bank = b % 2

            def trb(t, b=b, bank=bank):
                last = None
                for j in range(4):
                    c = 4 * b + j
                    last = t.transpose(trp[:, bank * 1024 + j * 128: bank * 1024 + (j + 1) * 128],
                                       src[:, c * 128:(c + 1) * 128], ident[:, :])
                return last
            sc.step("T", trb, reads=list(src_res) + ["ident"], writes=[f"trp{bank}"],
                    tag=f"{tag}.tr{b}")

            if dst3d is not None:
                def cpb(h, b=b, bank=bank):
                    ap3, off = dst3d
                    cp = h.tensor_copy if cpeng == "V" else h.copy
                    cp(out=ap3[:, 4 * b:4 * b + 4, off:off + 128],
                       in_=trp[:, bank * 1024: bank * 1024 + 512]
                       .rearrange("p (j f) -> p j f", f=128))
            else:
                def cpb(h, b=b, bank=bank):
                    cp = h.tensor_copy if cpeng == "V" else h.copy
                    for j in range(4):
                        c = 4 * b + j
                        o = coloff_fn(c)
                        cp(out=dst[:, o:o + 128],
                           in_=trp[:, bank * 1024 + j * 128: bank * 1024 + (j + 1) * 128])
            sc.step(cpeng, cpb, reads=[f"trp{bank}"], writes=[dstw], tag=f"{tag}.cp{b}")

    for T in range(NT):
        load_x(T, T, f"ph0b.x{T}")
        sc.step("V", (lambda v, T=T: emit_xn(v, T, T)),
                reads=[f"xstg{T % 2}", "mn_all", "rs_all"], writes=["xn_st"],
                tag=f"ph0b.xn{T}")
        sc.step("A", (lambda a: a.activation(out=xnbf[:, :], in_=xn_st[:, :],
                                             func=AF.Copy)),
                reads=["xn_st"], writes=["xnbf"],
                tag=f"ph0b.cast{T}")
        tr16(xnbf, xn_fm, "xn_fm", None, f"ph0b.t{T}", ["xnbf"],
             dst3d=(xn_fm[:, :].rearrange("p (c n) -> p c n", n=N), T * 128),
             cpeng="A")

    q_fm = salloc("q_fm", [128, H * N], BF16, side="right")
    k_fm = salloc("k_fm", [128, N], BF16, side="right")
    vstage = salloc("vstage", [128, N], BF16, side="right")
    v_tok = salloc("v_tok", [128, N], BF16, side="right")
    wpan = salloc("wpan", [128, 2 * E], BF16)
    qk_ps = palloc("qk_ps", [128, 2 * 512])

    for po in range(QC):
        sc.step("G", (lambda g, sem, po=po:
                      g.dma_start(out=wpan[:, (po % 2) * E:(po % 2 + 1) * E]
                                  .rearrange("p (c f) -> p c f", f=128),
                                  in_=wqkv_h[:, po * 128:(po + 1) * 128]
                                  .rearrange("(c p) f -> p c f", p=128))
                      .then_inc(sem, 16)),
                writes=[f"wpan{po % 2}"], ndma=1, tag=f"ph1.w{po}")
        for sp in range(4):
            idx = po * 4 + sp

            def mm16(t, po=po, sp=sp, idx=idx):
                last = None
                for kc in range(EC):
                    last = t.matmul(
                        qk_ps[:, (idx % 2) * 512:(idx % 2 + 1) * 512],
                        wpan[:, (po % 2) * E + kc * 128:(po % 2) * E + (kc + 1) * 128],
                        xn_fm[:, kc * N + sp * 512: kc * N + (sp + 1) * 512],
                        start=(kc == 0), stop=(kc == EC - 1))
                return last
            sc.step("T", mm16, reads=[f"wpan{po % 2}", "xn_fm"],
                    writes=[f"qk_ps{idx % 2}"], tag=f"ph1.mm{po}.{sp}")

            def evac(a, po=po, sp=sp, idx=idx):
                src = qk_ps[:, (idx % 2) * 512:(idx % 2 + 1) * 512]
                if po < 16:
                    dst = q_fm[:, po * N + sp * 512: po * N + (sp + 1) * 512]
                elif po == 16:
                    dst = k_fm[:, sp * 512:(sp + 1) * 512]
                else:
                    dst = vstage[:, sp * 512:(sp + 1) * 512]
                a.activation(out=dst, in_=src, func=AF.Identity, bias=bq_sb[:, po:po + 1])
            wres = "q_fm" if po < 16 else ("k_fm" if po == 16 else "vstage")
            sc.step("A", evac, reads=[f"qk_ps{idx % 2}", "bq_sb"], writes=[wres],
                    tag=f"ph1.ev{po}.{sp}")

    tr16(vstage, v_tok, "v_tok", (lambda c: c * 128), "ph1v", ["vstage"])

    sc.barrier("b12")
    free("qk_ps", "wpan", "xnbf", "xn_fm", "junk")

    attn_fm = salloc("attn_fm", [128, EC * N], BF16)
    probs_sb = salloc("probs_sb", [128, 2 * N], F32)
    probs_bf = salloc("probs_bf", [128, 2 * N], BF16)
    probsT = salloc("probsT", [128, 2 * N], BF16)
    rm2 = salloc("rm2", [128, 2], F32)
    nb2 = salloc("nb2", [128, 2], F32)
    rsum2 = salloc("rsum2", [128, 2], F32)
    rinv2 = salloc("rinv2", [128, 2], F32)
    sc_ps = palloc("sc_ps", [128, N])
    av_ps = palloc("av_ps", [128, 128])

    for T in range(NT):
        L = (T + 1) * 128
        nb4 = (T + 4) // 4
        for h in range(H):
            hs = h % 2

            def mm_sc(t, T=T, h=h, L=L, nb4=nb4):
                last = None
                for c in range(nb4):
                    w = min(512, L - c * 512)
                    last = t.matmul(sc_ps[:, c * 512:c * 512 + w],
                                    q_fm[:, h * N + T * 128: h * N + (T + 1) * 128],
                                    k_fm[:, c * 512:c * 512 + w],
                                    start=True, stop=True)
                return last
            sc.step("T", mm_sc, reads=["q_fm", "k_fm"], writes=["sc_ps"],
                    tag=f"ph2.sc{T}.{h}")

            def pre(v, T=T, L=L, hs=hs):
                v.tensor_add(sc_ps[:, T * 128:(T + 1) * 128],
                             sc_ps[:, T * 128:(T + 1) * 128], cmask[:, :])
                v.drain()
                v.reduce_max(rm2[:, hs:hs + 1], sc_ps[:, 0:L], axis=AX.X)
                v.drain()
                v.tensor_scalar_mul(nb2[:, hs:hs + 1], rm2[:, hs:hs + 1], -SM)
            sc.step("V", pre, reads=["sc_ps", "cmask"],
                    writes=["sc_ps", f"rm{hs}"], tag=f"ph2.pre{T}.{h}")

            def expo(a, L=L, hs=hs):
                a.activation(out=probs_sb[:, hs * N: hs * N + L], in_=sc_ps[:, 0:L],
                             func=AF.Exp, scale=SM, bias=nb2[:, hs:hs + 1],
                             accum_out=rsum2[:, hs:hs + 1])
            sc.step("A", expo, reads=["sc_ps", f"rm{hs}"],
                    writes=[f"probs{hs}", f"rsum{hs}"], tag=f"ph2.exp{T}.{h}")

            def recip(v, hs=hs):
                v.reciprocal(rinv2[:, hs:hs + 1], rsum2[:, hs:hs + 1])
            sc.step("V", recip, reads=[f"rsum{hs}"], writes=[f"rinv{hs}"],
                    tag=f"ph2.rc{T}.{h}")

            def norm(a, L=L, hs=hs):
                a.activation(out=probs_bf[:, hs * N: hs * N + L],
                             in_=probs_sb[:, hs * N: hs * N + L],
                             func=AF.Identity, scale=rinv2[:, hs:hs + 1])
            sc.step("A", norm, reads=[f"probs{hs}", f"rinv{hs}"],
                    writes=[f"probsbf{hs}"], tag=f"ph2.cast{T}.{h}")

            for b in range(nb4):
                ntr = min(4, (T + 1) - 4 * b)

                bank = b % 2

                def trb(t, b=b, ntr=ntr, bank=bank, hs=hs):
                    last = None
                    for j in range(ntr):
                        kc = 4 * b + j
                        last = t.transpose(trp[:, bank * 1024 + j * 128: bank * 1024 + (j + 1) * 128],
                                           probs_bf[:, hs * N + kc * 128: hs * N + (kc + 1) * 128],
                                           ident[:, :])
                    return last
                sc.step("T", trb, reads=[f"probsbf{hs}", "ident"], writes=[f"trp{bank}"],
                        tag=f"ph2.tr{T}.{h}.{b}")

                def cpb(v, b=b, ntr=ntr, bank=bank, hs=hs):
                    v.tensor_copy(out=probsT[:, hs * N + 4 * b * 128: hs * N + (4 * b + ntr) * 128],
                                  in_=trp[:, bank * 1024: bank * 1024 + ntr * 128])
                sc.step("V", cpb, reads=[f"trp{bank}"], writes=[f"probsT{hs}"],
                        tag=f"ph2.cp{T}.{h}.{b}")

            def mm_av(t, T=T, hs=hs):
                last = None
                for kc in range(T + 1):
                    last = t.matmul(av_ps[:, :], v_tok[:, kc * 128:(kc + 1) * 128],
                                    probsT[:, hs * N + kc * 128: hs * N + (kc + 1) * 128],
                                    start=(kc == 0), stop=(kc == T))
                return last
            sc.step("T", mm_av, reads=["v_tok", f"probsT{hs}"], writes=["av_ps"],
                    tag=f"ph2.av{T}.{h}")

            def ev_av(a, T=T, h=h):
                a.activation(out=attn_fm[:, h * N + T * 128: h * N + (T + 1) * 128],
                             in_=av_ps[:, :], func=AF.Copy)
            sc.step("A", ev_av, reads=["av_ps"], writes=["attn_fm"],
                    tag=f"ph2.ev{T}.{h}")

    sc.barrier("b23")
    free("av_ps", "sc_ps", "rinv2", "rsum2", "nb2", "rm2", "probsT", "probs_bf",
         "probs_sb", "v_tok", "vstage", "k_fm", "q_fm")

    wfc_bf = salloc("wfc_bf", [128, EC * E], BF16, side="right")
    bfc_row = salloc("bfc_row", [1, E], F32)
    ones1 = salloc("ones1", [1, 128], F32)
    bfc_bc = salloc("bfc_bc", [128, E], F32)
    ares_st = salloc("ares_st", [128, 2 * E], F32)
    fc_ps = palloc("fc_ps", [128, 2 * 512])

    sc.step("G", (lambda g, sem: g.dma_start(out=bfc_row[:, :],
                                        in_=bfc_h[:].rearrange("(o f) -> o f", o=1))
                  .then_inc(sem, 16)), writes=["bfc_row"], ndma=1, tag="ph3.b")
    sc.step("V", lambda v: v.memset(ones1[:, :], 1.0), writes=["ones1"], tag="ph3.o")
    for es in range(4):
        sc.step("T", (lambda t, es=es:
                      t.matmul(fc_ps[:, (es % 2) * 512:(es % 2 + 1) * 512],
                               ones1[:, :], bfc_row[:, es * 512:(es + 1) * 512],
                               start=True, stop=True)),
                reads=["ones1", "bfc_row"], writes=[f"fc_ps{es % 2}"],
                tag=f"ph3.bm{es}")
        sc.step("A", (lambda a, es=es:
                      a.activation(out=bfc_bc[:, es * 512:(es + 1) * 512],
                                   in_=fc_ps[:, (es % 2) * 512:(es % 2 + 1) * 512],
                                   func=AF.Copy)),
                reads=[f"fc_ps{es % 2}"], writes=["bfc_bc"], tag=f"ph3.be{es}")

    for kc in range(EC):
        sc.step("G", (lambda g, sem, kc=kc:
                      g.dma_start(out=wfc_bf[:, kc * E:(kc + 1) * E],
                                  in_=wfc_h[kc * 128:(kc + 1) * 128, :])
                      .then_inc(sem, 16)),
                writes=["wfc_bf"], ndma=1, tag=f"ph3.w{kc}")

    for T in range(NT):
        load_x(T, T, f"ph3.x{T}")
        for es in range(4):
            idx = T * 4 + es

            def mmfc(t, T=T, es=es, idx=idx):
                last = None
                for kc in range(EC):
                    last = t.matmul(
                        fc_ps[:, (idx % 2) * 512:(idx % 2 + 1) * 512],
                        attn_fm[:, kc * N + T * 128: kc * N + (T + 1) * 128],
                        wfc_bf[:, kc * E + es * 512: kc * E + (es + 1) * 512],
                        start=(kc == 0), stop=(kc == EC - 1))
                return last
            sc.step("T", mmfc, reads=["attn_fm", "wfc_bf"], writes=[f"fc_ps{idx % 2}"],
                    tag=f"ph3.mm{T}.{es}")

            def evfc(v, T=T, es=es, idx=idx):
                dst = ares_st[:, (T % 2) * E + es * 512:(T % 2) * E + (es + 1) * 512]
                v.scalar_tensor_tensor(
                    out=dst, in0=fc_ps[:, (idx % 2) * 512:(idx % 2 + 1) * 512],
                    scalar=1.0, in1=bfc_bc[:, es * 512:(es + 1) * 512],
                    op0=ALU.mult, op1=ALU.add)
                v.drain()
                v.tensor_add(dst, dst,
                             xstg[:, (T % 2) * E + es * 512:(T % 2) * E + (es + 1) * 512])
            sc.step("V", evfc, reads=[f"fc_ps{idx % 2}", "bfc_bc", f"xstg{T % 2}"],
                    writes=[f"ares{T % 2}"], tag=f"ph3.ev{T}.{es}")

        sc.step("G", (lambda g, sem, T=T:
                      g.dma_start(out=ares_h[T * 128:(T + 1) * 128, :],
                                  in_=ares_st[:, (T % 2) * E:(T % 2 + 1) * E])
                      .then_inc(sem, 16)),
                reads=[f"ares{T % 2}"], writes=[f"aresh{T}"], ndma=1, tag=f"ph3.s{T}")

    sc.barrier("b34")
    free("fc_ps", "ares_st", "bfc_bc", "ones1", "bfc_row", "wfc_bf", "attn_fm",
         "xstg")

    QN = 4
    TQ = NT // QN
    xq_fm = salloc("xq_fm", [128, EC * (N // QN)], BF16)
    res = salloc("res", [128, TQ * E], F32)
    wgT = salloc("wgT", [128, EC * OSTRIP], BF16)
    wuT = salloc("wuT", [128, EC * OSTRIP], BF16)
    wdT = salloc("wdT", [128, 4 * E], BF16)
    ring = salloc("ring", [128, 4 * E], F32)
    qbf = salloc("qbf", [128, 2 * E], BF16)
    interall = salloc("interall", [128, TQ * 512], F32)
    sg_t = salloc("sg_t", [128, 2 * 512], F32)
    qs0A = salloc("qs0A", [128, 1024], F32)
    qs0B = salloc("qs0B", [128, 1024], F32)
    qs0C = salloc("qs0C", [128, 1024], F32)
    qs1A = salloc("qs1A", [128, 1024], F32)
    qs1B = salloc("qs1B", [128, 1024], F32)
    qs1C = salloc("qs1C", [128, 1024], F32)
    QSETS = [(qs0A, qs0B, qs0C), (qs1A, qs1B, qs1C)]
    iv4 = salloc("iv4", [128, 576], F32)
    sc4 = salloc("sc4", [128, 576], F32)
    am4 = salloc("am4", [128, 512], F32)
    chA = salloc("chA", [128, 256], F32)
    chB = salloc("chB", [128, 256], F32)
    chC = salloc("chC", [128, 256], F32)
    chD = salloc("chD", [128, 64], F32)
    chE = salloc("chE", [128, 64], F32)
    chF = salloc("chF", [128, 64], F32)
    intq4 = salloc("intq4", [128, TQ * 512], BF16)
    intT2 = salloc("intT2", [128, 2 * 512], BF16)
    xn_stB = salloc("xn_stB", [128, E], F32)
    _QP = [0]

    def q_body(jobs, fmt, tag):
        tinv = 8.0 if fmt == "act" else 2.0
        mn = 7.5 if fmt == "act" else 6.0
        t2 = tinv * tinv
        units = []
        for (srcf, srcr, dstf, dstr, ivo, ivr, extra) in jobs:
            F = srcf.shape[-1]
            for hh in range(0, F, 1024):
                Fh = min(1024, F - hh)
                p = _QP[0] % 2
                _QP[0] += 1
                units.append((srcf[:, hh:hh + Fh], list(srcr),
                              dstf[:, hh:hh + Fh], list(dstr),
                              ivo + hh // 32, ivr, list(extra), Fh, p))

        def e_p1(u, k):
            srch, srcr, dsth, dstr, ivo, ivr, extra, Fh, p = u
            nb = Fh // 32
            qA = QSETS[p][0]
            ivb = iv4[:, ivo:ivo + nb].rearrange("p (b o) -> p b o", o=1) \
                .broadcast_to([128, nb, 32])

            def f(g, A3=qA[:, 0:Fh].rearrange("p (b k) -> p b k", k=32),
                  s3=srch.rearrange("p (b k) -> p b k", k=32), ivb=ivb):
                g.tensor_tensor(out=A3, in0=s3, in1=ivb, op=ALU.mult)
            sc.step("G", f, reads=srcr + [f"iv4{ivr}"], writes=[f"qs{p}"],
                    tag=f"{tag}.p1.{k}")

        def e_v1(u, k):
            srch, srcr, dsth, dstr, ivo, ivr, extra, Fh, p = u
            qA, qB, _ = QSETS[p]

            def f(v, A=qA[:, 0:Fh], B=qB[:, 0:Fh]):
                v._custom_dve(_MXQ["MXQ_IK"], out=B, in0=A, s0=4.0 * t2,
                              s1=16.0 * t2, imm2=0.5)
            sc.step("V", f, reads=[f"qs{p}"], writes=[f"qs{p}"],
                    tag=f"{tag}.v1.{k}")

        def e_p2(u, k):
            srch, srcr, dsth, dstr, ivo, ivr, extra, Fh, p = u
            qA, qB, qC = QSETS[p]

            def f(g, A=qA[:, 0:Fh], B=qB[:, 0:Fh], Cb=qC[:, 0:Fh]):
                g.tensor_mul(Cb, A, B)
            sc.step("G", f, reads=[f"qs{p}"], writes=[f"qs{p}"],
                    tag=f"{tag}.p2.{k}")

        def e_v23(u, k):
            srch, srcr, dsth, dstr, ivo, ivr, extra, Fh, p = u
            nb = Fh // 32
            qA, qB, qC = QSETS[p]
            scb = sc4[:, ivo:ivo + nb].rearrange("p (b o) -> p b o", o=1) \
                .broadcast_to([128, nb, 32])

            def f(v, A=qA[:, 0:Fh], B=qB[:, 0:Fh], Cb=qC[:, 0:Fh],
                  d3=dsth.rearrange("p (b k) -> p b k", k=32),
                  c3=qC[:, 0:Fh].rearrange("p (b k) -> p b k", k=32), scb=scb):
                if fmt == "act":
                    v._custom_dve(_MXQ["MXQ_EX3"], out=B, in0=Cb, in1=sh16[:, :],
                                  s0=16.0, s1=8.0)
                    v.drain()
                    v._custom_dve(_MXQ["MXQ_EX3"], out=B, in0=B, in1=shift0[:, :],
                                  s0=4.0, s1=2.0)
                else:
                    v._custom_dve(_MXQ["MXQ_EX3"], out=B, in0=Cb, in1=sh4[:, :],
                                  s0=4.0, s1=2.0)
                v.drain()
                v._custom_dve(_MXQ["MXQ_TIE"], out=B, in0=B, in1=Cb, s1=0.5)
                v.drain()
                v._custom_dve(_MXQ["MXQ_UNF"], out=Cb, in0=A, in1=B,
                              s0=4.0 * t2, s1=16.0 * t2, imm2=mn * tinv)
                v.drain()
                v.scalar_tensor_tensor(out=d3, in0=c3, scalar=-mn * tinv,
                                       op0=ALU.max, in1=scb, op1=ALU.mult)
            sc.step("V", f, reads=[f"qs{p}", "shifts", f"sc4{ivr}"],
                    writes=[f"qs{p}"] + dstr + extra, tag=f"{tag}.v23.{k}")

        for b0 in range(0, len(units), 2):
            pr = list(enumerate(units[b0:b0 + 2], b0))
            for k, u in pr:
                e_p1(u, k)
            for k, u in pr:
                e_v1(u, k)
            for k, u in pr:
                e_p2(u, k)
            for k, u in pr:
                e_v23(u, k)

    g_ps = palloc("g_ps", [128, 2 * 512])
    u_ps = palloc("u_ps", [128, 2 * 512])
    dn_ps = palloc("dn_ps", [128, 2 * 512])

    def rg(i):
        return ring[:, i * E:(i + 1) * E]

    QBS = [[f"qbfs{4 * h + kk}" for kk in range(4)] for h in range(2)]

    for qq in range(QN):
        T0 = qq * TQ
        for i in range(TQ):
            T = T0 + i
            lnbuf = xn_st if i % 2 == 0 else xn_stB
            sc.step("S", (lambda s, sem, T=T, i=i:
                          s.dma_start(out=rg(i % 2), in_=x_h[T * 128:(T + 1) * 128, :])
                          .then_inc(sem, 16)),
                    writes=[f"ring{i % 2}"], ndma=1, tag=f"p4a{qq}.x{i}")

            def am(v, T=T, i=i, lnbuf=lnbuf):
                v.tensor_scalar(out=lnbuf[:, :], in0=rg(i % 2),
                                scalar1=mn_all[:, T:T + 1], scalar2=rs_all[:, T:T + 1],
                                op0=ALU.subtract, op1=ALU.mult)
                v.drain()
                v.tensor_reduce(out=am4[:, i * 64:(i + 1) * 64],
                                in_=lnbuf[:, :].rearrange("p (b k) -> p b k", k=32),
                                axis=AX.X, op=ALU.max, apply_absolute_value=True)
            sc.step("V", am, reads=[f"ring{i % 2}", "mn_all", "rs_all"],
                    writes=[f"xnst{i % 2}", "am4w"], tag=f"p4a{qq}.am{i}")

        sc.step("V", (lambda v: emit_scale_chain(v, am4[:, 0:256], chA[:, :], chB[:, :], chC[:, :],
                                                 iv4[:, 0:256], sc4[:, 0:256],
                                                 24, CH_STD, post=8.0)),
                reads=["am4w"], writes=["iv4w", "sc4w", "chsc"],
                tag=f"p4a{qq}.ch")

        for i0 in range(0, TQ, 2):
            jobs = []
            for i in (i0, i0 + 1):
                T = T0 + i
                lnbuf = xn_st if i % 2 == 0 else xn_stB
                sc.step("S", (lambda s, sem, T=T, i=i:
                              s.dma_start(out=rg(i % 2), in_=x_h[T * 128:(T + 1) * 128, :])
                              .then_inc(sem, 16)),
                        writes=[f"ring{i % 2}"], ndma=1, tag=f"p4b{qq}.x{i}")

                def qx(v, T=T, i=i, lnbuf=lnbuf):
                    v.tensor_scalar(out=lnbuf[:, :], in0=rg(i % 2),
                                    scalar1=mn_all[:, T:T + 1],
                                    scalar2=rs_all[:, T:T + 1],
                                    op0=ALU.subtract, op1=ALU.mult)
                sc.step("V", qx, reads=[f"ring{i % 2}", "mn_all", "rs_all"],
                        writes=[f"xnst{i % 2}"], tag=f"p4b{qq}.ln{i}")
                jobs.append((lnbuf, [f"xnst{i % 2}"],
                             qbf[:, (i % 2) * E:(i % 2 + 1) * E], QBS[i % 2],
                             i * 64, "w", []))
            q_body(jobs, "act", f"p4b{qq}.q{i0}")
            for i in (i0, i0 + 1):
                tr16(qbf[:, (i % 2) * E:(i % 2 + 1) * E], xq_fm, "xq_fm", None,
                     f"p4b{qq}.t{i}", QBS[i % 2],
                     dst3d=(xq_fm[:, :].rearrange("p (c n) -> p c n", n=N // QN),
                            i * 128), cpeng="A")
        for i in range(TQ):
            sc.step("V", (lambda v, i=i: v.memset(res[:, i * E:(i + 1) * E], 0.0)),
                    writes=[f"res{i}"], tag=f"p4b{qq}.z{i}")

        for s_i in range(NSTRIP):
            o0 = s_i * OSTRIP
            if qq == 0:
                for m in range(2):
                    srcw = wg_h if m == 0 else wu_h
                    dstc = wgq_h if m == 0 else wuq_h
                    dstTn = "wgT" if m == 0 else "wuT"
                    dstT = wgT if m == 0 else wuT
                    for it in range(4):
                        sc.step("S", (lambda s, sem, srcw=srcw, o0=o0, it=it:
                                      s.dma_start(out=rg(it),
                                                  in_=srcw[o0 + it * 128:o0 + (it + 1) * 128, :])
                                      .then_inc(sem, 16)),
                                writes=[f"ring{it}"], ndma=1, tag=f"w{m}l{s_i}.{it}")
                        sc.step("V", (lambda v, it=it:
                                      v.tensor_reduce(out=am4[:, it * 64:(it + 1) * 64],
                                                      in_=rg(it).rearrange("p (b k) -> p b k", k=32),
                                                      axis=AX.X, op=ALU.max,
                                                      apply_absolute_value=True)),
                                reads=[f"ring{it}"], writes=["am4w"], tag=f"w{m}a{s_i}.{it}")
                    sc.step("V", (lambda v: emit_scale_chain(v, am4[:, 0:256],
                                                             chA[:, :], chB[:, :],
                                                             chC[:, :],
                                                             iv4[:, 0:256],
                                                             sc4[:, 0:256], 24,
                                                             CH_STD, post=2.0)),
                            reads=["am4w"], writes=["iv4w", "sc4w", "chsc"],
                            tag=f"w{m}c{s_i}")
                    for itp in (0, 2):
                        q_body([(rg(it), [f"ring{it}"],
                                 qbf[:, (it % 2) * E:(it % 2 + 1) * E], QBS[it % 2],
                                 it * 64, "w", []) for it in (itp, itp + 1)],
                               "wgt", f"w{m}q{s_i}.{itp}")
                        for it in (itp, itp + 1):
                            sc.step("G", (lambda g, sem, dstc=dstc, o0=o0, it=it:
                                          g.dma_start(out=dstc[o0 + it * 128:o0 + (it + 1) * 128, :],
                                                      in_=qbf[:, (it % 2) * E:(it % 2 + 1) * E])
                                          .then_inc(sem, 16)),
                                    reads=QBS[it % 2], writes=[f"c{m}.{s_i}.{it}"],
                                    ndma=1, tag=f"w{m}cw{s_i}.{it}")
                            tr16(qbf[:, (it % 2) * E:(it % 2 + 1) * E], dstT, dstTn,
                                 None, f"w{m}t{s_i}.{it}", QBS[it % 2],
                                 dst3d=(dstT[:, :].rearrange("p (c n) -> p c n", n=OSTRIP),
                                        it * 128), cpeng="A")
                for j in range(16):
                    sc.step("S", (lambda s, sem, o0=o0, j=j:
                                  s.dma_start(out=ring[:, j * 512:(j + 1) * 512],
                                              in_=wd_h[j * 128:(j + 1) * 128, o0:o0 + OSTRIP])
                                  .then_inc(sem, 16)),
                            writes=[f"ring{j // 4}"], ndma=1, tag=f"wdl{s_i}.{j}")
                    sc.step("V", (lambda v, j=j:
                                  v.tensor_reduce(out=am4[:, j * 16:(j + 1) * 16],
                                                  in_=ring[:, j * 512:(j + 1) * 512]
                                                  .rearrange("p (b k) -> p b k", k=32),
                                                  axis=AX.X, op=ALU.max,
                                                  apply_absolute_value=True)),
                            reads=[f"ring{j // 4}"], writes=["am4w"], tag=f"wda{s_i}.{j}")
                sc.step("V", (lambda v: emit_scale_chain(v, am4[:, 0:256], chA[:, :],
                                                         chB[:, :], chC[:, :],
                                                         iv4[:, 256:512],
                                                         sc4[:, 256:512],
                                                         24, CH_STD, post=2.0)),
                        reads=["am4w"], writes=["iv4d", "sc4d", "chsc"],
                        tag=f"wdc{s_i}")
                for jb in (0, 8):
                    q_body([(ring[:, j * 512:(j + 1) * 512], [f"ring{j // 4}"],
                             qbf[:, (j % 8) * 512:(j % 8 + 1) * 512], [f"qbfs{j % 8}"],
                             256 + j * 16, "d", [f"wdq{s_i}.{j}"])
                            for j in range(jb, jb + 8)],
                           "wgt", f"wdq{s_i}.{jb}")
                    for j in range(jb, jb + 8):
                        sc.step("G", (lambda g, sem, o0=o0, j=j:
                                      g.dma_start(out=wdq_h[j * 128:(j + 1) * 128, o0:o0 + OSTRIP],
                                                  in_=qbf[:, (j % 8) * 512:(j % 8 + 1) * 512])
                                      .then_inc(sem, 16)),
                                reads=[f"wdq{s_i}.{j}", f"qbfs{j % 8}"],
                                writes=[f"cd.{s_i}.{j}"], ndma=1, tag=f"wdcw{s_i}.{j}")

                        bank = j % 2

                        def trd(t, j=j, bank=bank):
                            last = None
                            for oc in range(4):
                                last = t.transpose(
                                    trp[:, bank * 1024 + oc * 128: bank * 1024 + (oc + 1) * 128],
                                    qbf[:, (j % 8) * 512 + oc * 128:(j % 8) * 512 + (oc + 1) * 128],
                                    ident[:, :])
                            return last
                        sc.step("T", trd, reads=[f"wdq{s_i}.{j}", f"qbfs{j % 8}",
                                                 "ident"], writes=[f"trp{bank}"],
                                tag=f"wdt{s_i}.{j}")

                        def cpd(a, j=j, bank=bank):
                            a.copy(out=wdT[:, :].rearrange("p (oc e) -> p oc e", e=E)
                                   [:, :, j * 128:(j + 1) * 128],
                                   in_=trp[:, bank * 1024: bank * 1024 + 512]
                                   .rearrange("p (oc f) -> p oc f", f=128))
                        sc.step("A", cpd, reads=[f"trp{bank}"], writes=["wdT"],
                                tag=f"wdcp{s_i}.{j}")
            else:
                for m in range(2):
                    srcc = wgq_h if m == 0 else wuq_h
                    dstTn = "wgT" if m == 0 else "wuT"
                    dstT = wgT if m == 0 else wuT
                    for it in range(4):
                        sc.step("S", (lambda s, sem, srcc=srcc, o0=o0, it=it:
                                      s.dma_start(out=qbf[:, (it % 2) * E:(it % 2 + 1) * E],
                                                  in_=srcc[o0 + it * 128:o0 + (it + 1) * 128, :])
                                      .then_inc(sem, 16)),
                                reads=[f"c{m}.{s_i}.{it}"],
                                writes=QBS[it % 2],
                                ndma=1, tag=f"w{m}lb{s_i}.{it}")
                        tr16(qbf[:, (it % 2) * E:(it % 2 + 1) * E], dstT, dstTn,
                             None, f"w{m}t{s_i}b.{it}", QBS[it % 2],
                             dst3d=(dstT[:, :].rearrange("p (c n) -> p c n", n=OSTRIP),
                                    it * 128), cpeng="A")
                for j in range(16):
                    sc.step("S", (lambda s, sem, o0=o0, j=j:
                                  s.dma_start(out=qbf[:, (j % 8) * 512:(j % 8 + 1) * 512],
                                              in_=wdq_h[j * 128:(j + 1) * 128, o0:o0 + OSTRIP])
                                  .then_inc(sem, 16)),
                            reads=[f"cd.{s_i}.{j}"],
                            writes=[f"qbfs{j % 8}", f"wdq{s_i}b.{j}"],
                            ndma=1, tag=f"wdlb{s_i}.{j}")

                    bank = j % 2

                    def trd(t, j=j, bank=bank):
                        last = None
                        for oc in range(4):
                            last = t.transpose(
                                trp[:, bank * 1024 + oc * 128: bank * 1024 + (oc + 1) * 128],
                                qbf[:, (j % 8) * 512 + oc * 128:(j % 8) * 512 + (oc + 1) * 128],
                                ident[:, :])
                        return last
                    sc.step("T", trd, reads=[f"wdq{s_i}b.{j}", f"qbfs{j % 8}",
                                             "ident"], writes=[f"trp{bank}"],
                            tag=f"wdtb{s_i}.{j}")

                    def cpd(a, j=j, bank=bank):
                        a.copy(out=wdT[:, :].rearrange("p (oc e) -> p oc e", e=E)
                               [:, :, j * 128:(j + 1) * 128],
                               in_=trp[:, bank * 1024: bank * 1024 + 512]
                               .rearrange("p (oc f) -> p oc f", f=128))
                    sc.step("A", cpd, reads=[f"trp{bank}"], writes=["wdT"],
                            tag=f"wdcpb{s_i}.{j}")

            for tt in range(TQ):
                sl = tt % 2

                def mmg(t, tt=tt, sl=sl):
                    last = None
                    for hc in range(EC):
                        last = t.matmul(
                            g_ps[:, sl * 512:(sl + 1) * 512],
                            xq_fm[:, hc * (N // QN) + tt * 128: hc * (N // QN) + (tt + 1) * 128],
                            wgT[:, hc * OSTRIP:(hc + 1) * OSTRIP],
                            start=(hc == 0), stop=(hc == EC - 1))
                    return last
                sc.step("T", mmg, reads=["xq_fm", "wgT"], writes=[f"g_ps{sl}"],
                        tag=f"ffg{qq}.{s_i}.{tt}")
                sc.step("A", (lambda a, sl=sl:
                              a.activation(out=sg_t[:, sl * 512:(sl + 1) * 512],
                                           in_=g_ps[:, sl * 512:(sl + 1) * 512],
                                           func=AF.Silu)),
                        reads=[f"g_ps{sl}"], writes=[f"sg{sl}"], tag=f"ffs{qq}.{s_i}.{tt}")

                def mmu(t, tt=tt, sl=sl):
                    last = None
                    for hc in range(EC):
                        last = t.matmul(
                            u_ps[:, sl * 512:(sl + 1) * 512],
                            xq_fm[:, hc * (N // QN) + tt * 128: hc * (N // QN) + (tt + 1) * 128],
                            wuT[:, hc * OSTRIP:(hc + 1) * OSTRIP],
                            start=(hc == 0), stop=(hc == EC - 1))
                    return last
                sc.step("T", mmu, reads=["xq_fm", "wuT"], writes=[f"u_ps{sl}"],
                        tag=f"ffu{qq}.{s_i}.{tt}")

                def mul(v, tt=tt, sl=sl):
                    v.tensor_mul(interall[:, tt * 512:(tt + 1) * 512],
                                 sg_t[:, sl * 512:(sl + 1) * 512],
                                 u_ps[:, sl * 512:(sl + 1) * 512])
                    v.drain()
                    v.tensor_reduce(out=am4[:, 256 + tt * 16: 256 + (tt + 1) * 16],
                                    in_=interall[:, tt * 512:(tt + 1) * 512]
                                    .rearrange("p (b k) -> p b k", k=32),
                                    axis=AX.X, op=ALU.max, apply_absolute_value=True)
                sc.step("V", mul, reads=[f"sg{sl}", f"u_ps{sl}"],
                        writes=[f"inter{tt}", "am4i"], tag=f"ffm{qq}.{s_i}.{tt}")

            sc.step("V", (lambda v: emit_scale_chain(v, am4[:, 256:256 + TQ * 16],
                                                     chD[:, 0:TQ * 16],
                                                     chE[:, 0:TQ * 16],
                                                     chF[:, 0:TQ * 16],
                                                     iv4[:, 512:512 + TQ * 16],
                                                     sc4[:, 512:512 + TQ * 16],
                                                     43, CH_INT, post=8.0)),
                    reads=["am4i"], writes=["iv4i", "sc4i", "chsci"],
                    tag=f"ffc{qq}.{s_i}")

            q_body([(interall[:, tt * 512:(tt + 1) * 512], [f"inter{tt}"],
                     intq4[:, tt * 512:(tt + 1) * 512], [f"intq{tt}"],
                     512 + tt * 16, "i", []) for tt in range(TQ)],
                   "act", f"ffq{qq}.{s_i}")
            for tt in range(TQ):
                bank = tt % 2

                def tri(t, tt=tt, bank=bank):
                    last = None
                    for oc in range(4):
                        last = t.transpose(trp[:, bank * 1024 + oc * 128: bank * 1024 + (oc + 1) * 128],
                                           intq4[:, tt * 512 + oc * 128: tt * 512 + (oc + 1) * 128],
                                           ident[:, :])
                    return last
                sc.step("T", tri, reads=[f"intq{tt}", "ident"], writes=[f"trp{bank}"],
                        tag=f"fft{qq}.{s_i}.{tt}")
                sc.step("A", (lambda a, bank=bank:
                              a.copy(out=intT2[:, bank * 512:(bank + 1) * 512],
                                     in_=trp[:, bank * 1024: bank * 1024 + 512])),
                        reads=[f"trp{bank}"], writes=[f"intT{bank}"],
                        tag=f"ffcp{qq}.{s_i}.{tt}")

                for es in range(4):
                    def mmd(t, es=es, bank=bank):
                        last = None
                        for oc in range(4):
                            last = t.matmul(
                                dn_ps[:, (es % 2) * 512:(es % 2 + 1) * 512],
                                intT2[:, bank * 512 + oc * 128: bank * 512 + (oc + 1) * 128],
                                wdT[:, oc * E + es * 512: oc * E + (es + 1) * 512],
                                start=(oc == 0), stop=(oc == 3))
                        return last
                    sc.step("T", mmd, reads=[f"intT{bank}", "wdT"],
                            writes=[f"dn_ps{es % 2}"],
                            tag=f"ffd{qq}.{s_i}.{tt}.{es}")

                    def acc(v, tt=tt, es=es):
                        dst = res[:, tt * E + es * 512: tt * E + (es + 1) * 512]
                        v.tensor_add(dst, dst,
                                     dn_ps[:, (es % 2) * 512:(es % 2 + 1) * 512])
                    sc.step("V", acc, reads=[f"dn_ps{es % 2}"], writes=[f"res{tt}"],
                            tag=f"ffa{qq}.{s_i}.{tt}.{es}")

        for tt in range(TQ):
            T = T0 + tt
            sc.step("G", (lambda g, sem, T=T: g.dma_start(out=rg(0),
                                                     in_=ares_h[T * 128:(T + 1) * 128, :])
                          .then_inc(sem, 16)),
                    reads=[f"aresh{T}"], writes=["ring0"], ndma=1, tag=f"fin{qq}.l{tt}")
            sc.step("V", (lambda v, tt=tt: v.tensor_add(rg(1), rg(0),
                                                        res[:, tt * E:(tt + 1) * E])),
                    reads=["ring0", f"res{tt}"], writes=["ring1"], tag=f"fin{qq}.a{tt}")
            sc.step("G", (lambda g, sem, T=T: g.dma_start(out=out_h[T * 128:(T + 1) * 128, :],
                                                     in_=rg(1))
                          .then_inc(sem, 16)),
                    reads=["ring1"], writes=[], ndma=1, tag=f"fin{qq}.s{tt}")
        if qq < QN - 1:
            sc.barrier(f"bq{qq}")

    blk = nc.Block()
    block = blk.__enter__()
    sc.emit(nc, block, SEMS)
    blk.__exit__(None, None, None)
    for g in semg:
        g.__exit__(None, None, None)
    for nm in reversed(list(live)):
        live.pop(nm).__exit__(None, None, None)
    mybir.codegen_inst_isa_subclasses(nc)
    return nc


_NC_CACHE = []


def kernel(L0_attn_x, L0_attn_w_qkv, L0_attn_b_qkv, L0_attn_w_fc, L0_attn_b_fc,
           L0_ffn_w_gate, L0_ffn_w_up, L0_ffn_w_down):
    import ml_dtypes
    if not _NC_CACHE:
        _NC_CACHE.append(build_program())
    nc = _NC_CACHE[0]
    x = np.ascontiguousarray(np.asarray(L0_attn_x, dtype=np.float32))
    wqkv = np.ascontiguousarray(np.asarray(L0_attn_w_qkv, dtype=np.float32))
    bqkv = np.ascontiguousarray(np.asarray(L0_attn_b_qkv, dtype=np.float32))
    wfc = np.ascontiguousarray(np.asarray(L0_attn_w_fc, dtype=np.float32))
    bfc = np.ascontiguousarray(np.asarray(L0_attn_b_fc, dtype=np.float32))
    wg = np.ascontiguousarray(np.asarray(L0_ffn_w_gate, dtype=np.float32))
    wu = np.ascontiguousarray(np.asarray(L0_ffn_w_up, dtype=np.float32))
    wd = np.ascontiguousarray(np.asarray(L0_ffn_w_down, dtype=np.float32))
    ident = np.eye(128, dtype=ml_dtypes.bfloat16)
    cmask = np.triu(np.full((128, 128), -1.0e30, np.float32), k=1)
    in_maps = []
    for b in range(8):
        in_maps.append({
            "x": np.ascontiguousarray(x[b]),
            "w_qkv": wqkv, "b_qkv": bqkv, "w_fc": wfc, "b_fc": bfc,
            "w_gate": wg, "w_up": wu, "w_down": wd,
            "ident": ident, "cmask": cmask,
        })
    res = run_bass_kernel_spmd(nc, in_maps, core_ids=list(range(8)))
    out = np.stack([np.asarray(r["out"], dtype=np.float32) for r in res.results], axis=0)
    return out


# version 09
# speedup vs baseline: 2.1477x; 1.0931x over previous
import numpy as np
import concourse.bass as bass
import concourse.mybir as mybir
from concourse.bass_utils import run_bass_kernel_spmd
import concourse.dve_ops as dve_ops
from concourse.dve_op import DveOp
from concourse.dve_spec import (
    Spec, Src0, Src1, C0, C1, C2, C3, One, Bin, AluOp, eq, lower as dve_lower,
)
from concourse.dve_table_gen import dve_ver_for

F32 = mybir.dt.float32
BF16 = mybir.dt.bfloat16
ALU = mybir.AluOpType
AX = mybir.AxisListType
AF = mybir.ActivationFunctionType

N = 2048
E = 2048
NT = N // 128
EC = E // 128
H = 16
QKVF = 2304
QC = QKVF // 128
I = 8192
OSTRIP = 512
NSTRIP = I // OSTRIP
HT = NT // 2
SM = 1.0 / float(np.sqrt(np.float32(128)))



def _cdt(a):
    return np.float64 if a.dtype == np.float64 else np.float32


def _ref_ik(in0, in1, s0, s1, imm2):
    f = _cdt(in0)
    a2 = in0.astype(f) * in0
    return (1.0 - (a2 >= s0) * imm2) * (1.0 - (a2 >= s1) * imm2)


def _ref_ex3(in0, in1, s0, s1, imm2):
    f = _cdt(in0)
    z = in0.astype(f) + in1
    b = (z >= s0).astype(f)
    z1 = z - b * s0
    b1 = (z1 >= s1).astype(f)
    return z1 - b1 * s1


def _ref_tie(in0, in1, s0, s1, imm2):
    f = _cdt(in0)
    z0 = in0.astype(f)
    b0 = (z0 >= 1.0).astype(f)
    fr = z0 - b0
    c1 = (fr > s1).astype(f)
    e = (fr == s1).astype(f)
    return in1 + ((c1 - fr) + e * b0)


def _ref_unf(in0, in1, s0, s1, imm2):
    f = _cdt(in0)
    a2 = in0.astype(f) * in0
    kk = (1.0 + (a2 >= s0)) * (1.0 + (a2 >= s1))
    return np.minimum(in1 * kk, imm2)


def _mxq_specs():
    u2a = Src0 * Src0
    ik_body = (One - (u2a >= C0) * C2) * (One - (u2a >= C1) * C2)
    zs = Src0 + C3
    bb = zs >= C0
    zz1 = zs - bb * C0
    bb1 = zz1 >= C1
    ex3_body = zz1 - bb1 * C1
    b0 = Bin(AluOp.IS_GE, Src0, One)
    fr = Src0 - b0
    c1 = Bin(AluOp.IS_LT, C1, fr)
    e = eq(fr, C1)
    tie_body = Src1 + ((c1 - fr) + (e & b0))
    u2b = Src0 * Src0
    kk = (One + (u2b >= C0)) * (One + (u2b >= C1))
    unf_body = Bin(AluOp.MIN, Src1 * kk, C2)
    return {
        "MXQ_IK": (Spec(body=ik_body), _ref_ik),
        "MXQ_EX3": (Spec(body=ex3_body), _ref_ex3),
        "MXQ_TIE": (Spec(body=tie_body), _ref_tie),
        "MXQ_UNF": (Spec(body=unf_body), _ref_unf),
    }


_MXQ = {}


def _register_mxq():
    if _MXQ:
        return
    if any(op.name == "MXQ_IK" for op in dve_ops.OPS):
        for op in dve_ops.OPS:
            if op.name.startswith("MXQ_"):
                _MXQ[op.name] = op
        return
    ver = dve_ver_for("TRN2")
    replaceable = ["DVEHASH32L_SEED", "DVEHASH32L_BODY", "DVEHASH32L_DRAIN"]
    for name, (spec, ref) in _mxq_specs().items():
        compiled = dve_lower(spec, ver=ver, perf_en=False, name=name)
        op = DveOp(name, spec, subdim=False, uops_sha={ver: compiled.sha(ver)},
                   reference=ref)
        if max(dve_ops._SUB_OPCODE_FOR_NAME.values()) < 0x1F:
            row = max(dve_ops._SUB_OPCODE_FOR_NAME.values()) + 1
            dve_ops.OPS.append(op)
        else:
            victim = replaceable.pop()
            idx = next(i for i, o in enumerate(dve_ops.OPS) if o.name == victim)
            row = dve_ops._SUB_OPCODE_FOR_NAME.pop(victim)
            dve_ops.OPS[idx] = op
            dve_ops.CUSTOM_DVE_SPECS.pop(victim, None)
            dve_ops.CUSTOM_DVE_REFS.pop(victim, None)
        dve_ops._SUB_OPCODE_FOR_NAME[name] = row
        dve_ops.CUSTOM_DVE_SPECS[name] = spec
        dve_ops.CUSTOM_DVE_REFS[name] = op.reference
        _MXQ[name] = op



DMA_ENGS = {"S", "G"}
ALL_ENGS = ("S", "G", "T", "V", "A")


class Sched:
    DRAIN_DIST = 3

    def __init__(self):
        self.steps = []
        self.lw = {}
        self.rd = {}

    def step(self, eng, fn, reads=(), writes=(), ndma=0, tag="", flush=False):
        i = len(self.steps)
        st = {"eng": eng, "fn": fn, "deps": set(), "ndma": ndma, "tag": tag,
              "flush": flush}
        self.steps.append(st)
        for r in reads:
            for j in self.lw.get(r, ()):
                st["deps"].add(j)
            self.rd.setdefault(r, []).append(i)
        for w in writes:
            for j in self.lw.get(w, ()):
                st["deps"].add(j)
            readers = self.rd.get(w, ())
            for j in readers:
                if j != i:
                    st["deps"].add(j)
            if readers:
                self.lw[w] = [i]
            else:
                self.lw.setdefault(w, []).append(i)
            self.rd[w] = []
        return i

    def barrier(self, tag="bar"):
        toks = [f"_{tag}_{e}" for e in ALL_ENGS]
        for e, t in zip(ALL_ENGS, toks):
            self.step(e, lambda h: None, writes=[t], tag=f"{tag}.mark.{e}",
                      flush=(e in DMA_ENGS))
        for e in ALL_ENGS:
            self.step(e, lambda h: None, reads=list(toks), tag=f"{tag}.wait.{e}")

    def emit(self, nc, block, sems):
        steps = self.steps
        n = len(steps)
        ords = {}
        cnt = {e: 0 for e in ALL_ENGS}
        for i, st in enumerate(steps):
            ords[i] = cnt[st["eng"]]
            cnt[st["eng"]] += 1
        needed = [False] * n
        drain_before = [False] * n
        for i, st in enumerate(steps):
            for j in st["deps"]:
                if steps[j]["eng"] == st["eng"]:
                    if (ords[i] - ords[j] < self.DRAIN_DIST
                            and st["eng"] in ("V", "A", "G")):
                        drain_before[i] = True
                else:
                    needed[j] = True
        K = 6
        ev = [None] * n
        run = {e: 0 for e in ALL_ENGS}
        dma_ord = {e: 0 for e in DMA_ENGS}
        sem_tot = {(e, k): 0 for e in DMA_ENGS for k in range(K)}
        self_wait = [None] * n
        for i, st in enumerate(steps):
            e = st["eng"]
            if e in DMA_ENGS and st["ndma"]:
                k = dma_ord[e] % K
                dma_ord[e] += 1
                self_wait[i] = (k, sem_tot[(e, k)])
                sem_tot[(e, k)] += 16 * st["ndma"]
                ev[i] = (e, k, sem_tot[(e, k)])
            else:
                if needed[i]:
                    run[e] += 1
                ev[i] = (e, "C" if e in DMA_ENGS else None, run[e])
            if st.get("flush"):
                st["flush_vals"] = tuple(sem_tot[(e, k)] for k in range(K)) \
                    if e in DMA_ENGS else None
        totals = {e: [sem_tot[(e, k)] for k in range(K)] for e in DMA_ENGS}
        waits = [[] for _ in range(n)]
        for i, st in enumerate(steps):
            for j in st["deps"]:
                pe = steps[j]["eng"]
                pj, kj, vj = ev[j]
                if pe == st["eng"]:
                    if pe in DMA_ENGS and kj != "C":
                        waits[i].append((pe, kj, vj))
                    continue
                if pe in DMA_ENGS and kj != "C":
                    waits[i].append((pe, kj, vj))
                elif needed[j]:
                    waits[i].append((pe, kj, vj))
        seen = {}
        pruned = [[] for _ in range(n)]
        for i, st in enumerate(steps):
            e = st["eng"]
            best = {}
            for (pe, k2, v) in waits[i]:
                key = (pe, k2)
                best[key] = max(best.get(key, 0), v)
            for (pe, k2), v in sorted(best.items(), key=lambda kv: (kv[0][0], str(kv[0][1]))):
                gk = (e, pe, k2)
                if v == 0 or seen.get(gk, -1) >= v:
                    continue
                seen[gk] = v
                pruned[i].append((pe, k2, v))

        def semof(pe, k2):
            if k2 is None:
                return sems[pe]
            return sems[pe + ("C" if k2 == "C" else str(k2))]

        def runner(eng):
            def body(h):
                inc_since_wait = [False]
                ninc = [0]
                for i, st in enumerate(steps):
                    if st["eng"] != eng:
                        continue
                    if pruned[i]:
                        inc_since_wait[0] = False
                    for (pe, k2, v) in pruned[i]:
                        h.wait_ge(semof(pe, k2), v)
                    if drain_before[i]:
                        h.drain()
                    if st.get("flush") and st.get("flush_vals"):
                        for k2f, vf in enumerate(st["flush_vals"]):
                            if vf:
                                h.wait_ge(semof(eng, k2f), vf)
                        inc_since_wait[0] = False
                    if eng in DMA_ENGS and st["ndma"]:
                        k2, v0 = self_wait[i]
                        if v0:
                            h.wait_ge(semof(eng, k2), v0)
                        last = st["fn"](h, semof(eng, k2))
                    else:
                        last = st["fn"](h)
                        if needed[i]:
                            csem = sems[eng + "C"] if eng in DMA_ENGS else sems[eng]
                            if inc_since_wait[0]:
                                h.wait_ge(csem, ninc[0])
                            if eng == "T":
                                if last is not None:
                                    last.then_inc(csem, 1)
                                else:
                                    h.nop().then_inc(csem, 1)
                            else:
                                h.drain()
                                h.nop().then_inc(csem, 1)
                            ninc[0] += 1
                            inc_since_wait[0] = True
                if eng in DMA_ENGS:
                    for k2 in range(K):
                        if totals[eng][k2]:
                            h.wait_ge(semof(eng, k2), totals[eng][k2])
            return body

        block.sync(runner("S"))
        block.gpsimd(runner("G"))
        block.tensor(runner("T"))
        block.vector(runner("V"))
        block.scalar(runner("A"))




def emit_scale_chain(v, amax_ap, zt, ft, ft2, invs_ap, sca_ap, pre, js,
                     post=1.0):
    v.tensor_scalar_mul(zt, amax_ap, 2.0**pre)
    v.memset(invs_ap, 2.0**(pre - 1))
    for j in js:
        v.drain()
        v.tensor_single_scalar(ft, zt, 8.0 * 2.0**j, ALU.is_lt)
        v.tensor_scalar(out=ft2, in0=zt, scalar1=8.0 * 2.0**j,
                        scalar2=2.0**-j, op0=ALU.is_ge, op1=ALU.mult)
        v.drain()
        v.tensor_add(ft, ft, ft2)
        v.drain()
        v.tensor_mul(zt, zt, ft)
        v.tensor_mul(invs_ap, invs_ap, ft)
    v.drain()
    if post != 1.0:
        v.tensor_scalar_mul(invs_ap, invs_ap, post)
        v.drain()
    v.reciprocal(sca_ap, invs_ap)




CH_FULL = (64, 32, 16, 8, 4, 2, 1)
CH_STD = (16, 8, 4, 2, 1)
CH_INT = (32, 16, 8, 4, 2, 1)




def build_program():
    _register_mxq()
    nc = bass.Bass("TRN2", target_bir_lowering=False)

    x_h = nc.dram_tensor("x", [N, E], F32, kind="ExternalInput").ap()
    wqkv_h = nc.dram_tensor("w_qkv", [E, QKVF], F32, kind="ExternalInput").ap()
    bqkv_h = nc.dram_tensor("b_qkv", [QKVF], F32, kind="ExternalInput").ap()
    wfc_h = nc.dram_tensor("w_fc", [E, E], F32, kind="ExternalInput").ap()
    bfc_h = nc.dram_tensor("b_fc", [E], F32, kind="ExternalInput").ap()
    wg_h = nc.dram_tensor("w_gate", [I, E], F32, kind="ExternalInput").ap()
    wu_h = nc.dram_tensor("w_up", [I, E], F32, kind="ExternalInput").ap()
    wd_h = nc.dram_tensor("w_down", [E, I], F32, kind="ExternalInput").ap()
    ident_h = nc.dram_tensor("ident", [128, 128], BF16, kind="ExternalInput").ap()
    cmask_h = nc.dram_tensor("cmask", [128, 128], F32, kind="ExternalInput").ap()
    out_h = nc.dram_tensor("out", [N, E], F32, kind="ExternalOutput").ap()

    ares_h = nc.dram_tensor("ares_scr", [N, E], F32).ap()
    wgt_h = nc.dram_tensor("wgt_c", [NSTRIP, 128, (E // 128) * OSTRIP], BF16).ap()
    wut_h = nc.dram_tensor("wut_c", [NSTRIP, 128, (E // 128) * OSTRIP], BF16).ap()
    wdt_h = nc.dram_tensor("wdt_c", [NSTRIP, 4, 128, E], BF16).ap()

    live = {}

    def salloc(name, shape, dtype, side=None):
        g = nc.sbuf_tensor(name, shape, dtype, side=side)
        t = g.__enter__()
        live[name] = g
        return t

    def palloc(name, shape, dtype=F32):
        g = nc.psum_tensor(name, shape, dtype)
        t = g.__enter__()
        live[name] = g
        return t

    def free(*names):
        for nm in names:
            live.pop(nm).__exit__(None, None, None)

    semg = []
    SEMS = {}
    for e in ("T", "V", "A"):
        g = nc.semaphore("sem" + e)
        SEMS[e] = g.__enter__()
        semg.append(g)
    for e in ("S", "G"):
        g = nc.semaphore(f"sem{e}C")
        SEMS[e + "C"] = g.__enter__()
        semg.append(g)
        for k in range(6):
            g = nc.semaphore(f"sem{e}{k}")
            SEMS[e + str(k)] = g.__enter__()
            semg.append(g)

    sc = Sched()

    ident = salloc("identt", [128, 128], BF16)
    cmask = salloc("cmaskt", [128, 128], F32)
    shift0 = salloc("shift0t", [128, 1], F32)
    sh16 = salloc("sh16", [128, 1], F32)
    sh4 = salloc("sh4", [128, 1], F32)
    bq_sb = salloc("bq_sb", [128, QC], F32)
    sm_all = salloc("sm_all", [128, NT], F32)
    sq_all = salloc("sq_all", [128, NT], F32)
    mn_all = salloc("mn_all", [128, NT], F32)
    va_all = salloc("va_all", [128, NT], F32)
    t_all = salloc("t_all", [128, NT], F32)
    sd_all = salloc("sd_all", [128, NT], F32)
    rs_all = salloc("rs_all", [128, NT], F32)
    xn_st = salloc("xn_st", [128, E], F32)
    trp_t = palloc("trp", [128, 2 * 1024], BF16)
    trp = trp_t

    def st_setup(s, sem):
        s.dma_start(out=ident[:, :], in_=ident_h[:, :]).then_inc(sem, 16)
        s.dma_start(out=cmask[:, :], in_=cmask_h[:, :]).then_inc(sem, 16)
        for c in range(QC):
            s.dma_start(out=bq_sb[:, c:c + 1],
                        in_=bqkv_h[c * 128:(c + 1) * 128]
                        .rearrange("(p o) -> p o", o=1)).then_inc(sem, 16)

    sc.step("S", st_setup, writes=["ident", "cmask", "bq_sb"], ndma=2 + QC)
    def v_sh(v):
        v.memset(shift0[:, :], 0.0)
        v.memset(sh16[:, :], 16.0)
        v.memset(sh4[:, :], 4.0)
    sc.step("V", v_sh, writes=["shifts"])

    xstg = salloc("xstg", [128, 2 * E], F32)
    junk = salloc("junk", [128, E], F32)
    xn_fm = salloc("xn_fm", [128, EC * N], BF16)
    xnbf = salloc("xnbf", [128, E], BF16)

    def xs(i):
        return xstg[:, (i % 2) * E:(i % 2 + 1) * E]

    def load_x(T, slot, tag):
        sc.step("S", (lambda s, sem, T=T, slot=slot:
                      s.dma_start(out=xs(slot), in_=x_h[T * 128:(T + 1) * 128, :])
                      .then_inc(sem, 16)),
                writes=[f"xstg{slot % 2}"], ndma=1, tag=tag)

    for T in range(NT):
        load_x(T, T, f"ph0a.x{T}")
        sc.step("A", (lambda a, T=T: a.activation(out=junk[:, :], in_=xs(T),
                                                  func=AF.Square,
                                                  accum_out=sq_all[:, T:T + 1])),
                reads=[f"xstg{T % 2}"], writes=["sq_all", "junk"], tag=f"ph0a.sq{T}")
        sc.step("V", (lambda v, T=T: v.reduce_sum(sm_all[:, T:T + 1], xs(T), axis=AX.X)),
                reads=[f"xstg{T % 2}"], writes=["sm_all"], tag=f"ph0a.sm{T}")

    def st_stats(v):
        v.tensor_scalar_mul(mn_all[:, :], sm_all[:, :], 1.0 / E)
        v.tensor_scalar_mul(va_all[:, :], sq_all[:, :], 1.0 / E)
        v.drain()
        v.tensor_mul(t_all[:, :], mn_all[:, :], mn_all[:, :])
        v.drain()
        v.tensor_sub(va_all[:, :], va_all[:, :], t_all[:, :])
        v.drain()
        v.tensor_scalar_add(va_all[:, :], va_all[:, :], 1e-5)

    sc.step("V", st_stats, reads=["sm_all", "sq_all"],
            writes=["mn_all", "va_all", "t_all"], tag="stats")
    sc.step("A", lambda a: a.sqrt(sd_all[:, :], va_all[:, :]), reads=["va_all"],
            writes=["sd_all"], tag="sqrt")
    sc.step("V", lambda v: v.reciprocal(rs_all[:, :], sd_all[:, :]), reads=["sd_all"],
            writes=["rs_all"], tag="rstd")

    def emit_xn(v, slot, T):
        v.tensor_scalar(out=xn_st[:, :], in0=xs(slot), scalar1=mn_all[:, T:T + 1],
                        scalar2=rs_all[:, T:T + 1], op0=ALU.subtract, op1=ALU.mult)

    def tr16(src, dst, dstw, coloff_fn, tag, src_res, dst3d=None, cpeng="V"):
        for b in range(4):
            bank = b % 2

            def trb(t, b=b, bank=bank):
                last = None
                for j in range(4):
                    c = 4 * b + j
                    last = t.transpose(trp[:, bank * 1024 + j * 128: bank * 1024 + (j + 1) * 128],
                                       src[:, c * 128:(c + 1) * 128], ident[:, :])
                return last
            sc.step("T", trb, reads=list(src_res) + ["ident"], writes=[f"trp{bank}"],
                    tag=f"{tag}.tr{b}")

            if dst3d is not None:
                def cpb(h, b=b, bank=bank):
                    ap3, off = dst3d
                    cp = h.tensor_copy if cpeng == "V" else h.copy
                    cp(out=ap3[:, 4 * b:4 * b + 4, off:off + 128],
                       in_=trp[:, bank * 1024: bank * 1024 + 512]
                       .rearrange("p (j f) -> p j f", f=128))
            else:
                def cpb(h, b=b, bank=bank):
                    cp = h.tensor_copy if cpeng == "V" else h.copy
                    for j in range(4):
                        c = 4 * b + j
                        o = coloff_fn(c)
                        cp(out=dst[:, o:o + 128],
                           in_=trp[:, bank * 1024 + j * 128: bank * 1024 + (j + 1) * 128])
            wn = dstw(b) if callable(dstw) else dstw
            sc.step(cpeng, cpb, reads=[f"trp{bank}"], writes=[wn], tag=f"{tag}.cp{b}")

    for T in range(NT):
        load_x(T, T, f"ph0b.x{T}")
        sc.step("V", (lambda v, T=T: emit_xn(v, T, T)),
                reads=[f"xstg{T % 2}", "mn_all", "rs_all"], writes=["xn_st"],
                tag=f"ph0b.xn{T}")
        sc.step("A", (lambda a: a.activation(out=xnbf[:, :], in_=xn_st[:, :],
                                             func=AF.Copy)),
                reads=["xn_st"], writes=["xnbf"],
                tag=f"ph0b.cast{T}")
        tr16(xnbf, xn_fm, "xn_fm", None, f"ph0b.t{T}", ["xnbf"],
             dst3d=(xn_fm[:, :].rearrange("p (c n) -> p c n", n=N), T * 128),
             cpeng="A")

    q_fm = salloc("q_fm", [128, H * N], BF16, side="right")
    k_fm = salloc("k_fm", [128, N], BF16, side="right")
    vstage = salloc("vstage", [128, N], BF16, side="right")
    v_tok = salloc("v_tok", [128, N], BF16, side="right")
    wpan = salloc("wpan", [128, 2 * E], BF16)
    qk_ps = palloc("qk_ps", [128, 2 * 512])

    for po in range(QC):
        sc.step("G", (lambda g, sem, po=po:
                      g.dma_start(out=wpan[:, (po % 2) * E:(po % 2 + 1) * E]
                                  .rearrange("p (c f) -> p c f", f=128),
                                  in_=wqkv_h[:, po * 128:(po + 1) * 128]
                                  .rearrange("(c p) f -> p c f", p=128))
                      .then_inc(sem, 16)),
                writes=[f"wpan{po % 2}"], ndma=1, tag=f"ph1.w{po}")
        for sp in range(4):
            idx = po * 4 + sp

            def mm16(t, po=po, sp=sp, idx=idx):
                last = None
                for kc in range(EC):
                    last = t.matmul(
                        qk_ps[:, (idx % 2) * 512:(idx % 2 + 1) * 512],
                        wpan[:, (po % 2) * E + kc * 128:(po % 2) * E + (kc + 1) * 128],
                        xn_fm[:, kc * N + sp * 512: kc * N + (sp + 1) * 512],
                        start=(kc == 0), stop=(kc == EC - 1))
                return last
            sc.step("T", mm16, reads=[f"wpan{po % 2}", "xn_fm"],
                    writes=[f"qk_ps{idx % 2}"], tag=f"ph1.mm{po}.{sp}")

            def evac(a, po=po, sp=sp, idx=idx):
                src = qk_ps[:, (idx % 2) * 512:(idx % 2 + 1) * 512]
                if po < 16:
                    dst = q_fm[:, po * N + sp * 512: po * N + (sp + 1) * 512]
                elif po == 16:
                    dst = k_fm[:, sp * 512:(sp + 1) * 512]
                else:
                    dst = vstage[:, sp * 512:(sp + 1) * 512]
                a.activation(out=dst, in_=src, func=AF.Identity, bias=bq_sb[:, po:po + 1])
            wres = "q_fm" if po < 16 else ("k_fm" if po == 16 else "vstage")
            sc.step("A", evac, reads=[f"qk_ps{idx % 2}", "bq_sb"], writes=[wres],
                    tag=f"ph1.ev{po}.{sp}")

    tr16(vstage, v_tok, "v_tok", (lambda c: c * 128), "ph1v", ["vstage"])

    sc.barrier("b12")
    free("qk_ps", "wpan", "xnbf", "xn_fm", "junk")

    attn_fm = salloc("attn_fm", [128, EC * N], BF16)
    probs_sb = salloc("probs_sb", [128, 2 * N], F32)
    probs_bf = salloc("probs_bf", [128, 2 * N], BF16)
    probsT = salloc("probsT", [128, 2 * N], BF16)
    rm2 = salloc("rm2", [128, 2], F32)
    nb2 = salloc("nb2", [128, 2], F32)
    rsum2 = salloc("rsum2", [128, 2], F32)
    rinv2 = salloc("rinv2", [128, 2], F32)
    sc_ps = palloc("sc_ps", [128, N])
    av_ps = palloc("av_ps", [128, 128])

    for T in range(NT):
        L = (T + 1) * 128
        nb4 = (T + 4) // 4
        for h in range(H):
            hs = h % 2

            def mm_sc(t, T=T, h=h, L=L, nb4=nb4):
                last = None
                for c in range(nb4):
                    w = min(512, L - c * 512)
                    last = t.matmul(sc_ps[:, c * 512:c * 512 + w],
                                    q_fm[:, h * N + T * 128: h * N + (T + 1) * 128],
                                    k_fm[:, c * 512:c * 512 + w],
                                    start=True, stop=True)
                return last
            sc.step("T", mm_sc, reads=["q_fm", "k_fm"], writes=["sc_ps"],
                    tag=f"ph2.sc{T}.{h}")

            def pre(v, T=T, L=L, hs=hs):
                v.tensor_add(sc_ps[:, T * 128:(T + 1) * 128],
                             sc_ps[:, T * 128:(T + 1) * 128], cmask[:, :])
                v.drain()
                v.reduce_max(rm2[:, hs:hs + 1], sc_ps[:, 0:L], axis=AX.X)
                v.drain()
                v.tensor_scalar_mul(nb2[:, hs:hs + 1], rm2[:, hs:hs + 1], -SM)
            sc.step("V", pre, reads=["sc_ps", "cmask"],
                    writes=["sc_ps", f"rm{hs}"], tag=f"ph2.pre{T}.{h}")

            def expo(a, L=L, hs=hs):
                a.activation(out=probs_sb[:, hs * N: hs * N + L], in_=sc_ps[:, 0:L],
                             func=AF.Exp, scale=SM, bias=nb2[:, hs:hs + 1],
                             accum_out=rsum2[:, hs:hs + 1])
            sc.step("A", expo, reads=["sc_ps", f"rm{hs}"],
                    writes=[f"probs{hs}", f"rsum{hs}"], tag=f"ph2.exp{T}.{h}")

            def recip(v, hs=hs):
                v.reciprocal(rinv2[:, hs:hs + 1], rsum2[:, hs:hs + 1])
            sc.step("V", recip, reads=[f"rsum{hs}"], writes=[f"rinv{hs}"],
                    tag=f"ph2.rc{T}.{h}")

            def norm(a, L=L, hs=hs):
                a.activation(out=probs_bf[:, hs * N: hs * N + L],
                             in_=probs_sb[:, hs * N: hs * N + L],
                             func=AF.Identity, scale=rinv2[:, hs:hs + 1])
            sc.step("A", norm, reads=[f"probs{hs}", f"rinv{hs}"],
                    writes=[f"probsbf{hs}"], tag=f"ph2.cast{T}.{h}")

            for b in range(nb4):
                ntr = min(4, (T + 1) - 4 * b)

                bank = b % 2

                def trb(t, b=b, ntr=ntr, bank=bank, hs=hs):
                    last = None
                    for j in range(ntr):
                        kc = 4 * b + j
                        last = t.transpose(trp[:, bank * 1024 + j * 128: bank * 1024 + (j + 1) * 128],
                                           probs_bf[:, hs * N + kc * 128: hs * N + (kc + 1) * 128],
                                           ident[:, :])
                    return last
                sc.step("T", trb, reads=[f"probsbf{hs}", "ident"], writes=[f"trp{bank}"],
                        tag=f"ph2.tr{T}.{h}.{b}")

                def cpb(v, b=b, ntr=ntr, bank=bank, hs=hs):
                    v.tensor_copy(out=probsT[:, hs * N + 4 * b * 128: hs * N + (4 * b + ntr) * 128],
                                  in_=trp[:, bank * 1024: bank * 1024 + ntr * 128])
                sc.step("V", cpb, reads=[f"trp{bank}"], writes=[f"probsT{hs}"],
                        tag=f"ph2.cp{T}.{h}.{b}")

            def mm_av(t, T=T, hs=hs):
                last = None
                for kc in range(T + 1):
                    last = t.matmul(av_ps[:, :], v_tok[:, kc * 128:(kc + 1) * 128],
                                    probsT[:, hs * N + kc * 128: hs * N + (kc + 1) * 128],
                                    start=(kc == 0), stop=(kc == T))
                return last
            sc.step("T", mm_av, reads=["v_tok", f"probsT{hs}"], writes=["av_ps"],
                    tag=f"ph2.av{T}.{h}")

            def ev_av(a, T=T, h=h):
                a.activation(out=attn_fm[:, h * N + T * 128: h * N + (T + 1) * 128],
                             in_=av_ps[:, :], func=AF.Copy)
            sc.step("A", ev_av, reads=["av_ps"], writes=["attn_fm"],
                    tag=f"ph2.ev{T}.{h}")

    sc.barrier("b23")
    free("av_ps", "sc_ps", "rinv2", "rsum2", "nb2", "rm2", "probsT", "probs_bf",
         "probs_sb", "v_tok", "vstage", "k_fm", "q_fm")

    wfc_bf = salloc("wfc_bf", [128, EC * E], BF16, side="right")
    bfc_row = salloc("bfc_row", [1, E], F32)
    ones1 = salloc("ones1", [1, 128], F32)
    bfc_bc = salloc("bfc_bc", [128, E], F32)
    ares_st = salloc("ares_st", [128, 2 * E], F32)
    fc_ps = palloc("fc_ps", [128, 2 * 512])

    sc.step("G", (lambda g, sem: g.dma_start(out=bfc_row[:, :],
                                        in_=bfc_h[:].rearrange("(o f) -> o f", o=1))
                  .then_inc(sem, 16)), writes=["bfc_row"], ndma=1, tag="ph3.b")
    sc.step("V", lambda v: v.memset(ones1[:, :], 1.0), writes=["ones1"], tag="ph3.o")
    for es in range(4):
        sc.step("T", (lambda t, es=es:
                      t.matmul(fc_ps[:, (es % 2) * 512:(es % 2 + 1) * 512],
                               ones1[:, :], bfc_row[:, es * 512:(es + 1) * 512],
                               start=True, stop=True)),
                reads=["ones1", "bfc_row"], writes=[f"fc_ps{es % 2}"],
                tag=f"ph3.bm{es}")
        sc.step("A", (lambda a, es=es:
                      a.activation(out=bfc_bc[:, es * 512:(es + 1) * 512],
                                   in_=fc_ps[:, (es % 2) * 512:(es % 2 + 1) * 512],
                                   func=AF.Copy)),
                reads=[f"fc_ps{es % 2}"], writes=["bfc_bc"], tag=f"ph3.be{es}")

    for kc in range(EC):
        sc.step("G", (lambda g, sem, kc=kc:
                      g.dma_start(out=wfc_bf[:, kc * E:(kc + 1) * E],
                                  in_=wfc_h[kc * 128:(kc + 1) * 128, :])
                      .then_inc(sem, 16)),
                writes=["wfc_bf"], ndma=1, tag=f"ph3.w{kc}")

    for T in range(NT):
        load_x(T, T, f"ph3.x{T}")
        for es in range(4):
            idx = T * 4 + es

            def mmfc(t, T=T, es=es, idx=idx):
                last = None
                for kc in range(EC):
                    last = t.matmul(
                        fc_ps[:, (idx % 2) * 512:(idx % 2 + 1) * 512],
                        attn_fm[:, kc * N + T * 128: kc * N + (T + 1) * 128],
                        wfc_bf[:, kc * E + es * 512: kc * E + (es + 1) * 512],
                        start=(kc == 0), stop=(kc == EC - 1))
                return last
            sc.step("T", mmfc, reads=["attn_fm", "wfc_bf"], writes=[f"fc_ps{idx % 2}"],
                    tag=f"ph3.mm{T}.{es}")

            def evfc(v, T=T, es=es, idx=idx):
                dst = ares_st[:, (T % 2) * E + es * 512:(T % 2) * E + (es + 1) * 512]
                v.scalar_tensor_tensor(
                    out=dst, in0=fc_ps[:, (idx % 2) * 512:(idx % 2 + 1) * 512],
                    scalar=1.0, in1=bfc_bc[:, es * 512:(es + 1) * 512],
                    op0=ALU.mult, op1=ALU.add)
                v.drain()
                v.tensor_add(dst, dst,
                             xstg[:, (T % 2) * E + es * 512:(T % 2) * E + (es + 1) * 512])
            sc.step("V", evfc, reads=[f"fc_ps{idx % 2}", "bfc_bc", f"xstg{T % 2}"],
                    writes=[f"ares{T % 2}"], tag=f"ph3.ev{T}.{es}")

        sc.step("G", (lambda g, sem, T=T:
                      g.dma_start(out=ares_h[T * 128:(T + 1) * 128, :],
                                  in_=ares_st[:, (T % 2) * E:(T % 2 + 1) * E])
                      .then_inc(sem, 16)),
                reads=[f"ares{T % 2}"], writes=[f"aresh{T}"], ndma=1, tag=f"ph3.s{T}")

    sc.barrier("b34")
    free("fc_ps", "ares_st", "bfc_bc", "ones1", "bfc_row", "wfc_bf", "attn_fm",
         "xstg")

    QN = 8
    TQ = NT // QN
    NG = N // QN
    xq_fm = salloc("xq_fm", [128, EC * NG], BF16)
    res = salloc("res", [128, TQ * E], F32)
    wgT = salloc("wgT", [128, EC * OSTRIP], BF16)
    wuT = salloc("wuT", [128, EC * OSTRIP], BF16)
    wdT = salloc("wdT", [128, 4 * E], BF16)
    ring = salloc("ring", [128, 4 * E], F32)
    qbf = salloc("qbf", [128, 2 * E], BF16)
    interall = salloc("interall", [128, TQ * 512], F32)
    sg_t = salloc("sg_t", [128, 2 * 512], F32)
    qs0A = salloc("qs0A", [128, 1024], F32)
    qs0B = salloc("qs0B", [128, 1024], F32)
    qs0C = salloc("qs0C", [128, 1024], F32)
    qs1A = salloc("qs1A", [128, 1024], F32)
    qs1B = salloc("qs1B", [128, 1024], F32)
    qs1C = salloc("qs1C", [128, 1024], F32)
    QSETS = [(qs0A, qs0B, qs0C), (qs1A, qs1B, qs1C)]
    IVO = {"g": 0, "u": 256, "d": 512, "i": 768}
    iv4 = salloc("iv4", [128, 800], F32)
    sc4 = salloc("sc4", [128, 800], F32)
    am4 = salloc("am4", [128, 800], F32)
    chA = salloc("chA", [128, 256], F32)
    chB = salloc("chB", [128, 256], F32)
    chC = salloc("chC", [128, 256], F32)
    chD = salloc("chD", [128, 64], F32)
    chE = salloc("chE", [128, 64], F32)
    chF = salloc("chF", [128, 64], F32)
    intq4 = salloc("intq4", [128, TQ * 512], BF16)
    intT2 = salloc("intT2", [128, 2 * 512], BF16)
    xn_stB = salloc("xn_stB", [128, E], F32)
    _QP = [0]

    def q_body(jobs, fmt, tag):
        tinv = 8.0 if fmt == "act" else 2.0
        mn = 7.5 if fmt == "act" else 6.0
        t2 = tinv * tinv
        units = []
        for (srcf, srcr, dstf, dstr, ivo, ivr, extra) in jobs:
            F = srcf.shape[-1]
            for hh in range(0, F, 1024):
                Fh = min(1024, F - hh)
                p = _QP[0] % 2
                _QP[0] += 1
                units.append((srcf[:, hh:hh + Fh], list(srcr),
                              dstf[:, hh:hh + Fh], list(dstr),
                              ivo + hh // 32, ivr, list(extra), Fh, p))

        def e_p1(u, k):
            srch, srcr, dsth, dstr, ivo, ivr, extra, Fh, p = u
            nb = Fh // 32
            qA = QSETS[p][0]
            ivb = iv4[:, ivo:ivo + nb].rearrange("p (b o) -> p b o", o=1) \
                .broadcast_to([128, nb, 32])

            def f(g, A3=qA[:, 0:Fh].rearrange("p (b k) -> p b k", k=32),
                  s3=srch.rearrange("p (b k) -> p b k", k=32), ivb=ivb):
                g.tensor_tensor(out=A3, in0=s3, in1=ivb, op=ALU.mult)
            sc.step("G", f, reads=srcr + [f"iv4{ivr}"], writes=[f"qs{p}"],
                    tag=f"{tag}.p1.{k}")

        def e_v1(u, k):
            srch, srcr, dsth, dstr, ivo, ivr, extra, Fh, p = u
            qA, qB, _ = QSETS[p]

            def f(v, A=qA[:, 0:Fh], B=qB[:, 0:Fh]):
                v._custom_dve(_MXQ["MXQ_IK"], out=B, in0=A, s0=4.0 * t2,
                              s1=16.0 * t2, imm2=0.5)
            sc.step("V", f, reads=[f"qs{p}"], writes=[f"qs{p}"],
                    tag=f"{tag}.v1.{k}")

        def e_p2(u, k):
            srch, srcr, dsth, dstr, ivo, ivr, extra, Fh, p = u
            qA, qB, qC = QSETS[p]

            def f(g, A=qA[:, 0:Fh], B=qB[:, 0:Fh], Cb=qC[:, 0:Fh]):
                g.tensor_mul(Cb, A, B)
            sc.step("G", f, reads=[f"qs{p}"], writes=[f"qs{p}"],
                    tag=f"{tag}.p2.{k}")

        def e_v23(u, k):
            srch, srcr, dsth, dstr, ivo, ivr, extra, Fh, p = u
            nb = Fh // 32
            qA, qB, qC = QSETS[p]
            scb = sc4[:, ivo:ivo + nb].rearrange("p (b o) -> p b o", o=1) \
                .broadcast_to([128, nb, 32])

            def f(v, A=qA[:, 0:Fh], B=qB[:, 0:Fh], Cb=qC[:, 0:Fh],
                  d3=dsth.rearrange("p (b k) -> p b k", k=32),
                  c3=qC[:, 0:Fh].rearrange("p (b k) -> p b k", k=32), scb=scb):
                if fmt == "act":
                    v._custom_dve(_MXQ["MXQ_EX3"], out=B, in0=Cb, in1=sh16[:, :],
                                  s0=16.0, s1=8.0)
                    v.drain()
                    v._custom_dve(_MXQ["MXQ_EX3"], out=B, in0=B, in1=shift0[:, :],
                                  s0=4.0, s1=2.0)
                else:
                    v._custom_dve(_MXQ["MXQ_EX3"], out=B, in0=Cb, in1=sh4[:, :],
                                  s0=4.0, s1=2.0)
                v.drain()
                v._custom_dve(_MXQ["MXQ_TIE"], out=B, in0=B, in1=Cb, s1=0.5)
                v.drain()
                v._custom_dve(_MXQ["MXQ_UNF"], out=Cb, in0=A, in1=B,
                              s0=4.0 * t2, s1=16.0 * t2, imm2=mn * tinv)
                v.drain()
                v.scalar_tensor_tensor(out=d3, in0=c3, scalar=-mn * tinv,
                                       op0=ALU.max, in1=scb, op1=ALU.mult)
            sc.step("V", f, reads=[f"qs{p}", "shifts", f"sc4{ivr}"],
                    writes=[f"qs{p}"] + dstr + extra, tag=f"{tag}.v23.{k}")

        for b0 in range(0, len(units), 2):
            pr = list(enumerate(units[b0:b0 + 2], b0))
            for k, u in pr:
                e_p1(u, k)
            for k, u in pr:
                e_v1(u, k)
            for k, u in pr:
                e_p2(u, k)
            for k, u in pr:
                e_v23(u, k)

    g_ps = palloc("g_ps", [128, 2 * 512])
    u_ps = palloc("u_ps", [128, 2 * 512])
    dn_ps = palloc("dn_ps", [128, 2 * 512])

    def rg(i):
        return ring[:, i * E:(i + 1) * E]

    QBS = [[f"qbfs{4 * h + kk}" for kk in range(4)] for h in range(2)]
    wdT3 = wdT[:, :].rearrange("p (oc es e) -> p es oc e", es=4, e=512)

    for qq in range(QN):
        T0 = qq * TQ
        for i in range(TQ):
            T = T0 + i
            lnbuf = xn_st if i % 2 == 0 else xn_stB
            sc.step("S", (lambda s, sem, T=T, i=i:
                          s.dma_start(out=rg(i % 2), in_=x_h[T * 128:(T + 1) * 128, :])
                          .then_inc(sem, 16)),
                    writes=[f"ring{i % 2}"], ndma=1, tag=f"p4a{qq}.x{i}")

            def am(v, T=T, i=i, lnbuf=lnbuf):
                v.tensor_scalar(out=lnbuf[:, :], in0=rg(i % 2),
                                scalar1=mn_all[:, T:T + 1], scalar2=rs_all[:, T:T + 1],
                                op0=ALU.subtract, op1=ALU.mult)
                v.drain()
                v.tensor_reduce(out=am4[:, i * 64:(i + 1) * 64],
                                in_=lnbuf[:, :].rearrange("p (b k) -> p b k", k=32),
                                axis=AX.X, op=ALU.max, apply_absolute_value=True)
            sc.step("V", am, reads=[f"ring{i % 2}", "mn_all", "rs_all"],
                    writes=[f"xnst{i % 2}", "am4g"], tag=f"p4a{qq}.am{i}")

        sc.step("V", (lambda v: emit_scale_chain(v, am4[:, 0:TQ * 64],
                                                 chA[:, 0:TQ * 64], chB[:, 0:TQ * 64],
                                                 chC[:, 0:TQ * 64],
                                                 iv4[:, 0:TQ * 64], sc4[:, 0:TQ * 64],
                                                 24, CH_STD, post=8.0)),
                reads=["am4g"], writes=["iv4g", "sc4g", "chsc"],
                tag=f"p4a{qq}.ch")

        jobs = []
        for i in range(TQ):
            T = T0 + i
            lnbuf = xn_st if i % 2 == 0 else xn_stB
            sc.step("S", (lambda s, sem, T=T, i=i:
                          s.dma_start(out=rg(i % 2), in_=x_h[T * 128:(T + 1) * 128, :])
                          .then_inc(sem, 16)),
                    writes=[f"ring{i % 2}"], ndma=1, tag=f"p4b{qq}.x{i}")

            def qx(v, T=T, i=i, lnbuf=lnbuf):
                v.tensor_scalar(out=lnbuf[:, :], in0=rg(i % 2),
                                scalar1=mn_all[:, T:T + 1],
                                scalar2=rs_all[:, T:T + 1],
                                op0=ALU.subtract, op1=ALU.mult)
            sc.step("V", qx, reads=[f"ring{i % 2}", "mn_all", "rs_all"],
                    writes=[f"xnst{i % 2}"], tag=f"p4b{qq}.ln{i}")
            jobs.append((lnbuf, [f"xnst{i % 2}"],
                         qbf[:, (i % 2) * E:(i % 2 + 1) * E], QBS[i % 2],
                         i * 64, "g", []))
        q_body(jobs, "act", f"p4b{qq}.q")
        for i in range(TQ):
            tr16(qbf[:, (i % 2) * E:(i % 2 + 1) * E], xq_fm, "xq_fm", None,
                 f"p4b{qq}.t{i}", QBS[i % 2],
                 dst3d=(xq_fm[:, :].rearrange("p (c n) -> p c n", n=NG),
                        i * 128), cpeng="A")
        for i in range(TQ):
            sc.step("V", (lambda v, i=i: v.memset(res[:, i * E:(i + 1) * E], 0.0)),
                    writes=[f"res{i}"], tag=f"p4b{qq}.z{i}")

        for s_i in range(NSTRIP):
            o0 = s_i * OSTRIP
            if qq == 0:
                for m in range(2):
                    srcw = wg_h if m == 0 else wu_h
                    dstc = wgt_h if m == 0 else wut_h
                    dstT = wgT if m == 0 else wuT
                    pfx = "wgTq" if m == 0 else "wuTq"
                    reg = "g" if m == 0 else "u"
                    v0 = IVO[reg]
                    for it in range(4):
                        sc.step("S", (lambda s, sem, srcw=srcw, o0=o0, it=it:
                                      s.dma_start(out=rg(it),
                                                  in_=srcw[o0 + it * 128:o0 + (it + 1) * 128, :])
                                      .then_inc(sem, 16)),
                                writes=[f"ring{it}"], ndma=1, tag=f"w{m}l{s_i}.{it}")
                        sc.step("V", (lambda v, it=it, v0=v0:
                                      v.tensor_reduce(out=am4[:, v0 + it * 64:v0 + (it + 1) * 64],
                                                      in_=rg(it).rearrange("p (b k) -> p b k", k=32),
                                                      axis=AX.X, op=ALU.max,
                                                      apply_absolute_value=True)),
                                reads=[f"ring{it}"], writes=[f"am4{reg}"], tag=f"w{m}a{s_i}.{it}")
                    sc.step("V", (lambda v, v0=v0: emit_scale_chain(
                                v, am4[:, v0:v0 + 256], chA[:, :], chB[:, :],
                                chC[:, :], iv4[:, v0:v0 + 256], sc4[:, v0:v0 + 256],
                                24, CH_STD, post=2.0)),
                            reads=[f"am4{reg}"], writes=[f"iv4{reg}", f"sc4{reg}", "chsc"],
                            tag=f"w{m}c{s_i}")
                    for itp in (0, 2):
                        q_body([(rg(it), [f"ring{it}"],
                                 qbf[:, (it % 2) * E:(it % 2 + 1) * E], QBS[it % 2],
                                 v0 + it * 64, reg, []) for it in (itp, itp + 1)],
                               "wgt", f"w{m}q{s_i}.{itp}")
                        for it in (itp, itp + 1):
                            tr16(qbf[:, (it % 2) * E:(it % 2 + 1) * E], dstT, 
                                 (lambda b, pfx=pfx: f"{pfx}{b}"),
                                 None, f"w{m}t{s_i}.{it}", QBS[it % 2],
                                 dst3d=(dstT[:, :].rearrange("p (c n) -> p c n", n=OSTRIP),
                                        it * 128), cpeng="A")
                    for hcq in range(4):
                        sc.step("G", (lambda g, sem, dstc=dstc, s_i=s_i, hcq=hcq, dstT=dstT:
                                      g.dma_start(out=dstc[s_i, :, hcq * E:(hcq + 1) * E],
                                                  in_=dstT[:, hcq * E:(hcq + 1) * E])
                                      .then_inc(sem, 16)),
                                reads=[f"{pfx}{hcq}"], writes=[f"c{m}.{s_i}.{hcq}"],
                                ndma=1, tag=f"w{m}cw{s_i}.{hcq}")
                for j in range(16):
                    sc.step("S", (lambda s, sem, o0=o0, j=j:
                                  s.dma_start(out=ring[:, j * 512:(j + 1) * 512],
                                              in_=wd_h[j * 128:(j + 1) * 128, o0:o0 + OSTRIP])
                                  .then_inc(sem, 16)),
                            writes=[f"ring{j // 4}"], ndma=1, tag=f"wdl{s_i}.{j}")
                    sc.step("V", (lambda v, j=j:
                                  v.tensor_reduce(out=am4[:, 512 + j * 16:512 + (j + 1) * 16],
                                                  in_=ring[:, j * 512:(j + 1) * 512]
                                                  .rearrange("p (b k) -> p b k", k=32),
                                                  axis=AX.X, op=ALU.max,
                                                  apply_absolute_value=True)),
                            reads=[f"ring{j // 4}"], writes=["am4d"], tag=f"wda{s_i}.{j}")
                sc.step("V", (lambda v: emit_scale_chain(v, am4[:, 512:768], chA[:, :],
                                                         chB[:, :], chC[:, :],
                                                         iv4[:, 512:768],
                                                         sc4[:, 512:768],
                                                         24, CH_STD, post=2.0)),
                        reads=["am4d"], writes=["iv4d", "sc4d", "chsc"],
                        tag=f"wdc{s_i}")
                for jb in (0, 8):
                    q_body([(ring[:, j * 512:(j + 2) * 512],
                             [f"ring{j // 4}", f"ring{(j + 1) // 4}"],
                             qbf[:, (j % 8) * 512:(j % 8 + 2) * 512],
                             [f"qbfs{j % 8}", f"qbfs{j % 8 + 1}"],
                             512 + j * 16, "d",
                             [f"wdq{s_i}.{j}", f"wdq{s_i}.{j + 1}"])
                            for j in range(jb, jb + 8, 2)],
                           "wgt", f"wdq{s_i}.{jb}")
                    for j in range(jb, jb + 8):
                        bank = j % 2

                        def trd(t, j=j, bank=bank):
                            last = None
                            for oc in range(4):
                                last = t.transpose(
                                    trp[:, bank * 1024 + oc * 128: bank * 1024 + (oc + 1) * 128],
                                    qbf[:, (j % 8) * 512 + oc * 128:(j % 8) * 512 + (oc + 1) * 128],
                                    ident[:, :])
                            return last
                        sc.step("T", trd, reads=[f"wdq{s_i}.{j}", f"qbfs{j % 8}",
                                                 "ident"], writes=[f"trp{bank}"],
                                tag=f"wdt{s_i}.{j}")

                        def cpd(a, j=j, bank=bank):
                            a.copy(out=wdT[:, :].rearrange("p (oc e) -> p oc e", e=E)
                                   [:, :, j * 128:(j + 1) * 128],
                                   in_=trp[:, bank * 1024: bank * 1024 + 512]
                                   .rearrange("p (oc f) -> p oc f", f=128))
                        sc.step("A", cpd, reads=[f"trp{bank}"], writes=[f"wdTe{j // 4}"],
                                tag=f"wdcp{s_i}.{j}")
                for es in range(4):
                    sc.step("G", (lambda g, sem, s_i=s_i, es=es:
                                  g.dma_start(out=wdt_h[s_i, es, :, :],
                                              in_=wdT3[:, es])
                                  .then_inc(sem, 16)),
                            reads=[f"wdTe{es}"], writes=[f"cd.{s_i}.{es}"],
                            ndma=1, tag=f"wdcw{s_i}.{es}")
            else:
                for m in range(2):
                    srcc = wgt_h if m == 0 else wut_h
                    dstT = wgT if m == 0 else wuT
                    pfx = "wgTq" if m == 0 else "wuTq"
                    for hcq in range(4):
                        sc.step("S", (lambda s, sem, srcc=srcc, s_i=s_i, hcq=hcq, dstT=dstT:
                                      s.dma_start(out=dstT[:, hcq * E:(hcq + 1) * E],
                                                  in_=srcc[s_i, :, hcq * E:(hcq + 1) * E])
                                      .then_inc(sem, 16)),
                                reads=[f"c{m}.{s_i}.{hcq}"], writes=[f"{pfx}{hcq}"],
                                ndma=1, tag=f"w{m}lb{s_i}.{hcq}")
                for es in range(4):
                    sc.step("S", (lambda s, sem, s_i=s_i, es=es:
                                  s.dma_start(out=wdT3[:, es],
                                              in_=wdt_h[s_i, es, :, :])
                                  .then_inc(sem, 16)),
                            reads=[f"cd.{s_i}.{es}"], writes=[f"wdTe{es}"],
                            ndma=1, tag=f"wdlb{s_i}.{es}")

            for m in range(2):
                dstT = wgT if m == 0 else wuT
                pfx = "wgTq" if m == 0 else "wuTq"
                ps = g_ps if m == 0 else u_ps
                pn = "g_ps" if m == 0 else "u_ps"
                for hcq in range(4):
                    def mmq(t, hcq=hcq, dstT=dstT, ps=ps):
                        last = None
                        for tt in range(TQ):
                            for hc in range(4 * hcq, 4 * hcq + 4):
                                last = t.matmul(
                                    ps[:, tt * 512:(tt + 1) * 512],
                                    xq_fm[:, hc * NG + tt * 128: hc * NG + (tt + 1) * 128],
                                    dstT[:, hc * OSTRIP:(hc + 1) * OSTRIP],
                                    start=(hc == 0), stop=(hc == EC - 1))
                        return last
                    sc.step("T", mmq, reads=["xq_fm", f"{pfx}{hcq}"], writes=[pn],
                            tag=f"ff{m}{qq}.{s_i}.{hcq}")
            for tt in range(TQ):
                sc.step("A", (lambda a, tt=tt:
                              a.activation(out=sg_t[:, tt * 512:(tt + 1) * 512],
                                           in_=g_ps[:, tt * 512:(tt + 1) * 512],
                                           func=AF.Silu)),
                        reads=["g_ps"], writes=[f"sg{tt}"], tag=f"ffs{qq}.{s_i}.{tt}")

                def mul(v, tt=tt):
                    v.tensor_mul(interall[:, tt * 512:(tt + 1) * 512],
                                 sg_t[:, tt * 512:(tt + 1) * 512],
                                 u_ps[:, tt * 512:(tt + 1) * 512])
                    v.drain()
                    v.tensor_reduce(out=am4[:, 768 + tt * 16: 768 + (tt + 1) * 16],
                                    in_=interall[:, tt * 512:(tt + 1) * 512]
                                    .rearrange("p (b k) -> p b k", k=32),
                                    axis=AX.X, op=ALU.max, apply_absolute_value=True)
                sc.step("V", mul, reads=[f"sg{tt}", "u_ps"],
                        writes=[f"inter{tt}", "am4i"], tag=f"ffm{qq}.{s_i}.{tt}")

            sc.step("V", (lambda v: emit_scale_chain(v, am4[:, 768:768 + TQ * 16],
                                                     chD[:, 0:TQ * 16],
                                                     chE[:, 0:TQ * 16],
                                                     chF[:, 0:TQ * 16],
                                                     iv4[:, 768:768 + TQ * 16],
                                                     sc4[:, 768:768 + TQ * 16],
                                                     43, CH_INT, post=8.0)),
                    reads=["am4i"], writes=["iv4i", "sc4i", "chsci"],
                    tag=f"ffc{qq}.{s_i}")

            q_body([(interall[:, 0:TQ * 512], [f"inter{tt}" for tt in range(TQ)],
                     intq4[:, 0:TQ * 512], [f"intq{tt}" for tt in range(TQ)],
                     768, "i", [])],
                   "act", f"ffq{qq}.{s_i}")
            for tt in range(TQ):
                bank = tt % 2

                def tri(t, tt=tt, bank=bank):
                    last = None
                    for oc in range(4):
                        last = t.transpose(trp[:, bank * 1024 + oc * 128: bank * 1024 + (oc + 1) * 128],
                                           intq4[:, tt * 512 + oc * 128: tt * 512 + (oc + 1) * 128],
                                           ident[:, :])
                    return last
                sc.step("T", tri, reads=[f"intq{tt}", "ident"], writes=[f"trp{bank}"],
                        tag=f"fft{qq}.{s_i}.{tt}")
                sc.step("A", (lambda a, tt=tt, bank=bank:
                              a.copy(out=intT2[:, tt * 512:(tt + 1) * 512],
                                     in_=trp[:, bank * 1024: bank * 1024 + 512])),
                        reads=[f"trp{bank}"], writes=[f"intT{tt}"],
                        tag=f"ffcp{qq}.{s_i}.{tt}")

            for es in range(4):
                for tt in range(TQ):
                    def mmd(t, es=es, tt=tt):
                        last = None
                        for oc in range(4):
                            last = t.matmul(
                                dn_ps[:, (tt % 2) * 512:(tt % 2 + 1) * 512],
                                intT2[:, tt * 512 + oc * 128: tt * 512 + (oc + 1) * 128],
                                wdT[:, oc * E + es * 512: oc * E + (es + 1) * 512],
                                start=(oc == 0), stop=(oc == 3))
                        return last
                    sc.step("T", mmd, reads=[f"intT{tt}", f"wdTe{es}"],
                            writes=[f"dn_ps{tt % 2}"],
                            tag=f"ffd{qq}.{s_i}.{es}.{tt}")

                    def acc(v, tt=tt, es=es):
                        dst = res[:, tt * E + es * 512: tt * E + (es + 1) * 512]
                        v.tensor_add(dst, dst,
                                     dn_ps[:, (tt % 2) * 512:(tt % 2 + 1) * 512])
                    sc.step("V", acc, reads=[f"dn_ps{tt % 2}"], writes=[f"res{tt}"],
                            tag=f"ffa{qq}.{s_i}.{es}.{tt}")

        for tt in range(TQ):
            T = T0 + tt
            sc.step("G", (lambda g, sem, T=T: g.dma_start(out=rg(0),
                                                     in_=ares_h[T * 128:(T + 1) * 128, :])
                          .then_inc(sem, 16)),
                    reads=[f"aresh{T}"], writes=["ring0"], ndma=1, tag=f"fin{qq}.l{tt}")
            sc.step("V", (lambda v, tt=tt: v.tensor_add(rg(1), rg(0),
                                                        res[:, tt * E:(tt + 1) * E])),
                    reads=["ring0", f"res{tt}"], writes=["ring1"], tag=f"fin{qq}.a{tt}")
            sc.step("G", (lambda g, sem, T=T: g.dma_start(out=out_h[T * 128:(T + 1) * 128, :],
                                                     in_=rg(1))
                          .then_inc(sem, 16)),
                    reads=["ring1"], writes=[], ndma=1, tag=f"fin{qq}.s{tt}")
        if qq < QN - 1:
            sc.barrier(f"bq{qq}")

    blk = nc.Block()
    block = blk.__enter__()
    sc.emit(nc, block, SEMS)
    blk.__exit__(None, None, None)
    for g in semg:
        g.__exit__(None, None, None)
    for nm in reversed(list(live)):
        live.pop(nm).__exit__(None, None, None)
    mybir.codegen_inst_isa_subclasses(nc)
    return nc


_NC_CACHE = []


def kernel(L0_attn_x, L0_attn_w_qkv, L0_attn_b_qkv, L0_attn_w_fc, L0_attn_b_fc,
           L0_ffn_w_gate, L0_ffn_w_up, L0_ffn_w_down):
    import ml_dtypes
    if not _NC_CACHE:
        _NC_CACHE.append(build_program())
    nc = _NC_CACHE[0]
    x = np.ascontiguousarray(np.asarray(L0_attn_x, dtype=np.float32))
    wqkv = np.ascontiguousarray(np.asarray(L0_attn_w_qkv, dtype=np.float32))
    bqkv = np.ascontiguousarray(np.asarray(L0_attn_b_qkv, dtype=np.float32))
    wfc = np.ascontiguousarray(np.asarray(L0_attn_w_fc, dtype=np.float32))
    bfc = np.ascontiguousarray(np.asarray(L0_attn_b_fc, dtype=np.float32))
    wg = np.ascontiguousarray(np.asarray(L0_ffn_w_gate, dtype=np.float32))
    wu = np.ascontiguousarray(np.asarray(L0_ffn_w_up, dtype=np.float32))
    wd = np.ascontiguousarray(np.asarray(L0_ffn_w_down, dtype=np.float32))
    ident = np.eye(128, dtype=ml_dtypes.bfloat16)
    cmask = np.triu(np.full((128, 128), -1.0e30, np.float32), k=1)
    in_maps = []
    for b in range(8):
        in_maps.append({
            "x": np.ascontiguousarray(x[b]),
            "w_qkv": wqkv, "b_qkv": bqkv, "w_fc": wfc, "b_fc": bfc,
            "w_gate": wg, "w_up": wu, "w_down": wd,
            "ident": ident, "cmask": cmask,
        })
    res = run_bass_kernel_spmd(nc, in_maps, core_ids=list(range(8)))
    out = np.stack([np.asarray(r["out"], dtype=np.float32) for r in res.results], axis=0)
    return out


# version 10
# speedup vs baseline: 2.5832x; 1.2028x over previous
import numpy as np
import concourse.bass as bass
import concourse.mybir as mybir
from concourse.bass_utils import run_bass_kernel_spmd
import concourse.dve_ops as dve_ops
from concourse.dve_op import DveOp
from concourse.dve_spec import (
    Spec, Src0, Src1, C0, C1, C2, C3, One, Bin, AluOp, eq, lower as dve_lower,
)
from concourse.dve_table_gen import dve_ver_for

F32 = mybir.dt.float32
BF16 = mybir.dt.bfloat16
ALU = mybir.AluOpType
AX = mybir.AxisListType
AF = mybir.ActivationFunctionType

N = 2048
E = 2048
NT = N // 128
EC = E // 128
H = 16
QKVF = 2304
QC = QKVF // 128
I = 8192
OSTRIP = 512
NSTRIP = I // OSTRIP
HT = NT // 2
SM = 1.0 / float(np.sqrt(np.float32(128)))



def _cdt(a):
    return np.float64 if a.dtype == np.float64 else np.float32


def _ref_ik(in0, in1, s0, s1, imm2):
    f = _cdt(in0)
    a2 = in0.astype(f) * in0
    return (1.0 - (a2 >= s0) * imm2) * (1.0 - (a2 >= s1) * imm2)


def _ref_ex3(in0, in1, s0, s1, imm2):
    f = _cdt(in0)
    z = in0.astype(f) + in1
    b = (z >= s0).astype(f)
    z1 = z - b * s0
    b1 = (z1 >= s1).astype(f)
    return z1 - b1 * s1


def _ref_tie(in0, in1, s0, s1, imm2):
    f = _cdt(in0)
    z0 = in0.astype(f)
    b0 = (z0 >= 1.0).astype(f)
    fr = z0 - b0
    c1 = (fr > s1).astype(f)
    e = (fr == s1).astype(f)
    return in1 + ((c1 - fr) + e * b0)


def _ref_chi(in0, in1, s0, s1, imm2):
    f = _cdt(in0)
    t = in0.astype(f) * in1
    return in1 + in1 * ((t >= s0) * s1)


def _ref_unf(in0, in1, s0, s1, imm2):
    f = _cdt(in0)
    a2 = in0.astype(f) * in0
    kk = (1.0 + (a2 >= s0)) * (1.0 + (a2 >= s1))
    return np.minimum(in1 * kk, imm2)


def _mxq_specs():
    u2a = Src0 * Src0
    ik_body = (One - (u2a >= C0) * C2) * (One - (u2a >= C1) * C2)
    zs = Src0 + C3
    bb = zs >= C0
    zz1 = zs - bb * C0
    bb1 = zz1 >= C1
    ex3_body = zz1 - bb1 * C1
    b0 = Bin(AluOp.IS_GE, Src0, One)
    fr = Src0 - b0
    c1 = Bin(AluOp.IS_LT, C1, fr)
    e = eq(fr, C1)
    tie_body = Src1 + ((c1 - fr) + (e & b0))
    u2b = Src0 * Src0
    kk = (One + (u2b >= C0)) * (One + (u2b >= C1))
    unf_body = Bin(AluOp.MIN, Src1 * kk, C2)
    chi_body = Src1 + Src1 * (((Src0 * Src1) >= C0) * C1)
    return {
        "MXQ_CHI": (Spec(body=chi_body), _ref_chi),
        "MXQ_IK": (Spec(body=ik_body), _ref_ik),
        "MXQ_EX3": (Spec(body=ex3_body), _ref_ex3),
        "MXQ_TIE": (Spec(body=tie_body), _ref_tie),
        "MXQ_UNF": (Spec(body=unf_body), _ref_unf),
    }


_MXQ = {}


def _register_mxq():
    if _MXQ:
        return
    if any(op.name == "MXQ_IK" for op in dve_ops.OPS):
        for op in dve_ops.OPS:
            if op.name.startswith("MXQ_"):
                _MXQ[op.name] = op
        return
    ver = dve_ver_for("TRN2")
    replaceable = ["DVEHASH32L_SEED", "DVEHASH32L_BODY", "DVEHASH32L_DRAIN"]
    for name, (spec, ref) in _mxq_specs().items():
        compiled = dve_lower(spec, ver=ver, perf_en=False, name=name)
        op = DveOp(name, spec, subdim=False, uops_sha={ver: compiled.sha(ver)},
                   reference=ref)
        if max(dve_ops._SUB_OPCODE_FOR_NAME.values()) < 0x1F:
            row = max(dve_ops._SUB_OPCODE_FOR_NAME.values()) + 1
            dve_ops.OPS.append(op)
        else:
            victim = replaceable.pop()
            idx = next(i for i, o in enumerate(dve_ops.OPS) if o.name == victim)
            row = dve_ops._SUB_OPCODE_FOR_NAME.pop(victim)
            dve_ops.OPS[idx] = op
            dve_ops.CUSTOM_DVE_SPECS.pop(victim, None)
            dve_ops.CUSTOM_DVE_REFS.pop(victim, None)
        dve_ops._SUB_OPCODE_FOR_NAME[name] = row
        dve_ops.CUSTOM_DVE_SPECS[name] = spec
        dve_ops.CUSTOM_DVE_REFS[name] = op.reference
        _MXQ[name] = op



DMA_ENGS = {"S", "G"}
ALL_ENGS = ("S", "G", "T", "V", "A")


class Sched:
    DRAIN_DIST = 3

    def __init__(self):
        self.steps = []
        self.lw = {}
        self.rd = {}

    def step(self, eng, fn, reads=(), writes=(), ndma=0, tag="", flush=False):
        i = len(self.steps)
        st = {"eng": eng, "fn": fn, "deps": set(), "ndma": ndma, "tag": tag,
              "flush": flush}
        self.steps.append(st)
        for r in reads:
            for j in self.lw.get(r, ()):
                st["deps"].add(j)
            self.rd.setdefault(r, []).append(i)
        for w in writes:
            for j in self.lw.get(w, ()):
                st["deps"].add(j)
            readers = self.rd.get(w, ())
            for j in readers:
                if j != i:
                    st["deps"].add(j)
            if readers:
                self.lw[w] = [i]
            else:
                self.lw.setdefault(w, []).append(i)
            self.rd[w] = []
        return i

    def barrier(self, tag="bar"):
        toks = [f"_{tag}_{e}" for e in ALL_ENGS]
        for e, t in zip(ALL_ENGS, toks):
            self.step(e, lambda h: None, writes=[t], tag=f"{tag}.mark.{e}",
                      flush=(e in DMA_ENGS))
        for e in ALL_ENGS:
            self.step(e, lambda h: None, reads=list(toks), tag=f"{tag}.wait.{e}")

    def emit(self, nc, block, sems):
        steps = self.steps
        n = len(steps)
        ords = {}
        cnt = {e: 0 for e in ALL_ENGS}
        for i, st in enumerate(steps):
            ords[i] = cnt[st["eng"]]
            cnt[st["eng"]] += 1
        needed = [False] * n
        drain_before = [False] * n
        for i, st in enumerate(steps):
            for j in st["deps"]:
                if steps[j]["eng"] == st["eng"]:
                    if (ords[i] - ords[j] < self.DRAIN_DIST
                            and st["eng"] in ("V", "A", "G")):
                        drain_before[i] = True
                else:
                    needed[j] = True
        K = 6
        ev = [None] * n
        run = {e: 0 for e in ALL_ENGS}
        dma_ord = {e: 0 for e in DMA_ENGS}
        sem_tot = {(e, k): 0 for e in DMA_ENGS for k in range(K)}
        self_wait = [None] * n
        for i, st in enumerate(steps):
            e = st["eng"]
            if e in DMA_ENGS and st["ndma"]:
                k = dma_ord[e] % K
                dma_ord[e] += 1
                self_wait[i] = (k, sem_tot[(e, k)])
                sem_tot[(e, k)] += 16 * st["ndma"]
                ev[i] = (e, k, sem_tot[(e, k)])
            else:
                if needed[i]:
                    run[e] += 1
                ev[i] = (e, "C" if e in DMA_ENGS else None, run[e])
            if st.get("flush"):
                st["flush_vals"] = tuple(sem_tot[(e, k)] for k in range(K)) \
                    if e in DMA_ENGS else None
        totals = {e: [sem_tot[(e, k)] for k in range(K)] for e in DMA_ENGS}
        waits = [[] for _ in range(n)]
        for i, st in enumerate(steps):
            for j in st["deps"]:
                pe = steps[j]["eng"]
                pj, kj, vj = ev[j]
                if pe == st["eng"]:
                    if pe in DMA_ENGS and kj != "C":
                        waits[i].append((pe, kj, vj))
                    continue
                if pe in DMA_ENGS and kj != "C":
                    waits[i].append((pe, kj, vj))
                elif needed[j]:
                    waits[i].append((pe, kj, vj))
        seen = {}
        pruned = [[] for _ in range(n)]
        for i, st in enumerate(steps):
            e = st["eng"]
            best = {}
            for (pe, k2, v) in waits[i]:
                key = (pe, k2)
                best[key] = max(best.get(key, 0), v)
            for (pe, k2), v in sorted(best.items(), key=lambda kv: (kv[0][0], str(kv[0][1]))):
                gk = (e, pe, k2)
                if v == 0 or seen.get(gk, -1) >= v:
                    continue
                seen[gk] = v
                pruned[i].append((pe, k2, v))

        def semof(pe, k2):
            if k2 is None:
                return sems[pe]
            return sems[pe + ("C" if k2 == "C" else str(k2))]

        def runner(eng):
            def body(h):
                inc_since_wait = [False]
                ninc = [0]
                for i, st in enumerate(steps):
                    if st["eng"] != eng:
                        continue
                    if pruned[i]:
                        inc_since_wait[0] = False
                    for (pe, k2, v) in pruned[i]:
                        h.wait_ge(semof(pe, k2), v)
                    if drain_before[i]:
                        h.drain()
                    if st.get("flush") and st.get("flush_vals"):
                        for k2f, vf in enumerate(st["flush_vals"]):
                            if vf:
                                h.wait_ge(semof(eng, k2f), vf)
                        inc_since_wait[0] = False
                    if eng in DMA_ENGS and st["ndma"]:
                        k2, v0 = self_wait[i]
                        if v0:
                            h.wait_ge(semof(eng, k2), v0)
                        last = st["fn"](h, semof(eng, k2))
                    else:
                        last = st["fn"](h)
                        if needed[i]:
                            csem = sems[eng + "C"] if eng in DMA_ENGS else sems[eng]
                            if inc_since_wait[0]:
                                h.wait_ge(csem, ninc[0])
                            if eng == "T":
                                if last is not None:
                                    last.then_inc(csem, 1)
                                else:
                                    h.nop().then_inc(csem, 1)
                            else:
                                h.drain()
                                h.nop().then_inc(csem, 1)
                            ninc[0] += 1
                            inc_since_wait[0] = True
                if eng in DMA_ENGS:
                    for k2 in range(K):
                        if totals[eng][k2]:
                            h.wait_ge(semof(eng, k2), totals[eng][k2])
            return body

        block.sync(runner("S"))
        block.gpsimd(runner("G"))
        block.tensor(runner("T"))
        block.vector(runner("V"))
        block.scalar(runner("A"))




def emit_scale_chain(v, amax_ap, tmp_ap, invs_ap, sca_ap, pre, js,
                     post=1.0):
    assert len(js) % 2 == 1
    v.memset(tmp_ap, 2.0**(pre - 1))
    cur, nxt = tmp_ap, invs_ap
    for j in js:
        v.drain()
        v._custom_dve(_MXQ["MXQ_CHI"], out=nxt, in0=amax_ap, in1=cur,
                      s0=4.0 * 2.0**j, s1=2.0**-j - 1.0)
        cur, nxt = nxt, cur
    v.drain()
    if post != 1.0:
        v.tensor_scalar_mul(invs_ap, invs_ap, post)
        v.drain()
    v.reciprocal(sca_ap, invs_ap)




CH_FULL = (64, 32, 16, 8, 4, 2, 1)
CH_STD = (16, 8, 4, 2, 1)
CH_INT = (16, 16, 16, 8, 4, 2, 1)




def build_program():
    _register_mxq()
    nc = bass.Bass("TRN2", target_bir_lowering=False)

    x_h = nc.dram_tensor("x", [N, E], F32, kind="ExternalInput").ap()
    wqkv_h = nc.dram_tensor("w_qkv", [E, QKVF], F32, kind="ExternalInput").ap()
    bqkv_h = nc.dram_tensor("b_qkv", [QKVF], F32, kind="ExternalInput").ap()
    wfc_h = nc.dram_tensor("w_fc", [E, E], F32, kind="ExternalInput").ap()
    bfc_h = nc.dram_tensor("b_fc", [E], F32, kind="ExternalInput").ap()
    wg_h = nc.dram_tensor("w_gate", [I, E], F32, kind="ExternalInput").ap()
    wu_h = nc.dram_tensor("w_up", [I, E], F32, kind="ExternalInput").ap()
    wd_h = nc.dram_tensor("w_down", [E, I], F32, kind="ExternalInput").ap()
    ident_h = nc.dram_tensor("ident", [128, 128], BF16, kind="ExternalInput").ap()
    cmask_h = nc.dram_tensor("cmask", [128, 128], F32, kind="ExternalInput").ap()
    out_h = nc.dram_tensor("out", [N, E], F32, kind="ExternalOutput").ap()

    ares_h = nc.dram_tensor("ares_scr", [N, E], F32).ap()
    wgt_h = nc.dram_tensor("wgt_c", [NSTRIP, 128, (E // 128) * OSTRIP], BF16).ap()
    wut_h = nc.dram_tensor("wut_c", [NSTRIP, 128, (E // 128) * OSTRIP], BF16).ap()
    wdt_h = nc.dram_tensor("wdt_c", [NSTRIP, 4, 128, E], BF16).ap()

    live = {}

    def salloc(name, shape, dtype, side=None):
        g = nc.sbuf_tensor(name, shape, dtype, side=side)
        t = g.__enter__()
        live[name] = g
        return t

    def palloc(name, shape, dtype=F32):
        g = nc.psum_tensor(name, shape, dtype)
        t = g.__enter__()
        live[name] = g
        return t

    def free(*names):
        for nm in names:
            live.pop(nm).__exit__(None, None, None)

    semg = []
    SEMS = {}
    for e in ("T", "V", "A"):
        g = nc.semaphore("sem" + e)
        SEMS[e] = g.__enter__()
        semg.append(g)
    for e in ("S", "G"):
        g = nc.semaphore(f"sem{e}C")
        SEMS[e + "C"] = g.__enter__()
        semg.append(g)
        for k in range(6):
            g = nc.semaphore(f"sem{e}{k}")
            SEMS[e + str(k)] = g.__enter__()
            semg.append(g)

    sc = Sched()

    ident = salloc("identt", [128, 128], BF16)
    cmask = salloc("cmaskt", [128, 128], F32)
    shift0 = salloc("shift0t", [128, 1], F32)
    sh16 = salloc("sh16", [128, 1], F32)
    sh4 = salloc("sh4", [128, 1], F32)
    bq_sb = salloc("bq_sb", [128, QC], F32)
    sm_all = salloc("sm_all", [128, NT], F32)
    sq_all = salloc("sq_all", [128, NT], F32)
    mn_all = salloc("mn_all", [128, NT], F32)
    va_all = salloc("va_all", [128, NT], F32)
    t_all = salloc("t_all", [128, NT], F32)
    sd_all = salloc("sd_all", [128, NT], F32)
    rs_all = salloc("rs_all", [128, NT], F32)
    xn_st = salloc("xn_st", [128, E], F32)
    trp_t = palloc("trp", [128, 2 * 1024], BF16)
    trp = trp_t

    def st_setup(s, sem):
        s.dma_start(out=ident[:, :], in_=ident_h[:, :]).then_inc(sem, 16)
        s.dma_start(out=cmask[:, :], in_=cmask_h[:, :]).then_inc(sem, 16)
        for c in range(QC):
            s.dma_start(out=bq_sb[:, c:c + 1],
                        in_=bqkv_h[c * 128:(c + 1) * 128]
                        .rearrange("(p o) -> p o", o=1)).then_inc(sem, 16)

    sc.step("S", st_setup, writes=["ident", "cmask", "bq_sb"], ndma=2 + QC)
    def v_sh(v):
        v.memset(shift0[:, :], 0.0)
        v.memset(sh16[:, :], 16.0)
        v.memset(sh4[:, :], 4.0)
    sc.step("V", v_sh, writes=["shifts"])

    xstg = salloc("xstg", [128, 2 * E], F32)
    junk = salloc("junk", [128, E], F32)
    xn_fm = salloc("xn_fm", [128, EC * N], BF16)
    xnbf = salloc("xnbf", [128, E], BF16)

    def xs(i):
        return xstg[:, (i % 2) * E:(i % 2 + 1) * E]

    def load_x(T, slot, tag):
        sc.step("S", (lambda s, sem, T=T, slot=slot:
                      s.dma_start(out=xs(slot), in_=x_h[T * 128:(T + 1) * 128, :])
                      .then_inc(sem, 16)),
                writes=[f"xstg{slot % 2}"], ndma=1, tag=tag)

    for T in range(NT):
        load_x(T, T, f"ph0a.x{T}")
        sc.step("A", (lambda a, T=T: a.activation(out=junk[:, :], in_=xs(T),
                                                  func=AF.Square,
                                                  accum_out=sq_all[:, T:T + 1])),
                reads=[f"xstg{T % 2}"], writes=["sq_all", "junk"], tag=f"ph0a.sq{T}")
        sc.step("V", (lambda v, T=T: v.reduce_sum(sm_all[:, T:T + 1], xs(T), axis=AX.X)),
                reads=[f"xstg{T % 2}"], writes=["sm_all"], tag=f"ph0a.sm{T}")

    def st_stats(v):
        v.tensor_scalar_mul(mn_all[:, :], sm_all[:, :], 1.0 / E)
        v.tensor_scalar_mul(va_all[:, :], sq_all[:, :], 1.0 / E)
        v.drain()
        v.tensor_mul(t_all[:, :], mn_all[:, :], mn_all[:, :])
        v.drain()
        v.tensor_sub(va_all[:, :], va_all[:, :], t_all[:, :])
        v.drain()
        v.tensor_scalar_add(va_all[:, :], va_all[:, :], 1e-5)

    sc.step("V", st_stats, reads=["sm_all", "sq_all"],
            writes=["mn_all", "va_all", "t_all"], tag="stats")
    sc.step("A", lambda a: a.sqrt(sd_all[:, :], va_all[:, :]), reads=["va_all"],
            writes=["sd_all"], tag="sqrt")
    sc.step("V", lambda v: v.reciprocal(rs_all[:, :], sd_all[:, :]), reads=["sd_all"],
            writes=["rs_all"], tag="rstd")

    def emit_xn(v, slot, T):
        v.tensor_scalar(out=xn_st[:, :], in0=xs(slot), scalar1=mn_all[:, T:T + 1],
                        scalar2=rs_all[:, T:T + 1], op0=ALU.subtract, op1=ALU.mult)

    def tr16(src, dst, dstw, coloff_fn, tag, src_res, dst3d=None, cpeng="V"):
        for b in range(4):
            bank = b % 2

            def trb(t, b=b, bank=bank):
                last = None
                for j in range(4):
                    c = 4 * b + j
                    last = t.transpose(trp[:, bank * 1024 + j * 128: bank * 1024 + (j + 1) * 128],
                                       src[:, c * 128:(c + 1) * 128], ident[:, :])
                return last
            sc.step("T", trb, reads=list(src_res) + ["ident"], writes=[f"trp{bank}"],
                    tag=f"{tag}.tr{b}")

            if dst3d is not None:
                def cpb(h, b=b, bank=bank):
                    ap3, off = dst3d
                    cp = h.tensor_copy if cpeng == "V" else h.copy
                    cp(out=ap3[:, 4 * b:4 * b + 4, off:off + 128],
                       in_=trp[:, bank * 1024: bank * 1024 + 512]
                       .rearrange("p (j f) -> p j f", f=128))
            else:
                def cpb(h, b=b, bank=bank):
                    cp = h.tensor_copy if cpeng == "V" else h.copy
                    for j in range(4):
                        c = 4 * b + j
                        o = coloff_fn(c)
                        cp(out=dst[:, o:o + 128],
                           in_=trp[:, bank * 1024 + j * 128: bank * 1024 + (j + 1) * 128])
            wn = dstw(b) if callable(dstw) else dstw
            sc.step(cpeng, cpb, reads=[f"trp{bank}"], writes=[wn], tag=f"{tag}.cp{b}")

    for T in range(NT):
        load_x(T, T, f"ph0b.x{T}")
        sc.step("V", (lambda v, T=T: emit_xn(v, T, T)),
                reads=[f"xstg{T % 2}", "mn_all", "rs_all"], writes=["xn_st"],
                tag=f"ph0b.xn{T}")
        sc.step("A", (lambda a: a.activation(out=xnbf[:, :], in_=xn_st[:, :],
                                             func=AF.Copy)),
                reads=["xn_st"], writes=["xnbf"],
                tag=f"ph0b.cast{T}")
        tr16(xnbf, xn_fm, "xn_fm", None, f"ph0b.t{T}", ["xnbf"],
             dst3d=(xn_fm[:, :].rearrange("p (c n) -> p c n", n=N), T * 128),
             cpeng="A")

    q_fm = salloc("q_fm", [128, H * N], BF16, side="right")
    k_fm = salloc("k_fm", [128, N], BF16, side="right")
    vstage = salloc("vstage", [128, N], BF16, side="right")
    v_tok = salloc("v_tok", [128, N], BF16, side="right")
    wpan = salloc("wpan", [128, 2 * E], BF16)
    qk_ps = palloc("qk_ps", [128, 2 * 512])

    for po in range(QC):
        sc.step("G", (lambda g, sem, po=po:
                      g.dma_start(out=wpan[:, (po % 2) * E:(po % 2 + 1) * E]
                                  .rearrange("p (c f) -> p c f", f=128),
                                  in_=wqkv_h[:, po * 128:(po + 1) * 128]
                                  .rearrange("(c p) f -> p c f", p=128))
                      .then_inc(sem, 16)),
                writes=[f"wpan{po % 2}"], ndma=1, tag=f"ph1.w{po}")
        for sp in range(4):
            idx = po * 4 + sp

            def mm16(t, po=po, sp=sp, idx=idx):
                last = None
                for kc in range(EC):
                    last = t.matmul(
                        qk_ps[:, (idx % 2) * 512:(idx % 2 + 1) * 512],
                        wpan[:, (po % 2) * E + kc * 128:(po % 2) * E + (kc + 1) * 128],
                        xn_fm[:, kc * N + sp * 512: kc * N + (sp + 1) * 512],
                        start=(kc == 0), stop=(kc == EC - 1))
                return last
            sc.step("T", mm16, reads=[f"wpan{po % 2}", "xn_fm"],
                    writes=[f"qk_ps{idx % 2}"], tag=f"ph1.mm{po}.{sp}")

            def evac(a, po=po, sp=sp, idx=idx):
                src = qk_ps[:, (idx % 2) * 512:(idx % 2 + 1) * 512]
                if po < 16:
                    dst = q_fm[:, po * N + sp * 512: po * N + (sp + 1) * 512]
                elif po == 16:
                    dst = k_fm[:, sp * 512:(sp + 1) * 512]
                else:
                    dst = vstage[:, sp * 512:(sp + 1) * 512]
                a.activation(out=dst, in_=src, func=AF.Identity, bias=bq_sb[:, po:po + 1])
            wres = "q_fm" if po < 16 else ("k_fm" if po == 16 else "vstage")
            sc.step("A", evac, reads=[f"qk_ps{idx % 2}", "bq_sb"], writes=[wres],
                    tag=f"ph1.ev{po}.{sp}")

    tr16(vstage, v_tok, "v_tok", (lambda c: c * 128), "ph1v", ["vstage"])

    sc.barrier("b12")
    free("qk_ps", "wpan", "xnbf", "xn_fm", "junk")

    attn_fm = salloc("attn_fm", [128, EC * N], BF16)
    probs_sb = salloc("probs_sb", [128, 2 * N], F32)
    probs_bf = salloc("probs_bf", [128, 2 * N], BF16)
    probsT = salloc("probsT", [128, 2 * N], BF16)
    rm2 = salloc("rm2", [128, 2], F32)
    nb2 = salloc("nb2", [128, 2], F32)
    rsum2 = salloc("rsum2", [128, 2], F32)
    rinv2 = salloc("rinv2", [128, 2], F32)
    sc_ps = palloc("sc_ps", [128, N])
    av_ps = palloc("av_ps", [128, 128])

    for T in range(NT):
        L = (T + 1) * 128
        nb4 = (T + 4) // 4
        for h in range(H):
            hs = h % 2

            def mm_sc(t, T=T, h=h, L=L, nb4=nb4):
                last = None
                for c in range(nb4):
                    w = min(512, L - c * 512)
                    last = t.matmul(sc_ps[:, c * 512:c * 512 + w],
                                    q_fm[:, h * N + T * 128: h * N + (T + 1) * 128],
                                    k_fm[:, c * 512:c * 512 + w],
                                    start=True, stop=True)
                return last
            sc.step("T", mm_sc, reads=["q_fm", "k_fm"], writes=["sc_ps"],
                    tag=f"ph2.sc{T}.{h}")

            def pre(v, T=T, L=L, hs=hs):
                v.tensor_add(sc_ps[:, T * 128:(T + 1) * 128],
                             sc_ps[:, T * 128:(T + 1) * 128], cmask[:, :])
                v.drain()
                v.reduce_max(rm2[:, hs:hs + 1], sc_ps[:, 0:L], axis=AX.X)
                v.drain()
                v.tensor_scalar_mul(nb2[:, hs:hs + 1], rm2[:, hs:hs + 1], -SM)
            sc.step("V", pre, reads=["sc_ps", "cmask"],
                    writes=["sc_ps", f"rm{hs}"], tag=f"ph2.pre{T}.{h}")

            def expo(a, L=L, hs=hs):
                a.activation(out=probs_sb[:, hs * N: hs * N + L], in_=sc_ps[:, 0:L],
                             func=AF.Exp, scale=SM, bias=nb2[:, hs:hs + 1],
                             accum_out=rsum2[:, hs:hs + 1])
            sc.step("A", expo, reads=["sc_ps", f"rm{hs}"],
                    writes=[f"probs{hs}", f"rsum{hs}"], tag=f"ph2.exp{T}.{h}")

            def recip(v, hs=hs):
                v.reciprocal(rinv2[:, hs:hs + 1], rsum2[:, hs:hs + 1])
            sc.step("V", recip, reads=[f"rsum{hs}"], writes=[f"rinv{hs}"],
                    tag=f"ph2.rc{T}.{h}")

            def norm(a, L=L, hs=hs):
                a.activation(out=probs_bf[:, hs * N: hs * N + L],
                             in_=probs_sb[:, hs * N: hs * N + L],
                             func=AF.Identity, scale=rinv2[:, hs:hs + 1])
            sc.step("A", norm, reads=[f"probs{hs}", f"rinv{hs}"],
                    writes=[f"probsbf{hs}"], tag=f"ph2.cast{T}.{h}")

            for b in range(nb4):
                ntr = min(4, (T + 1) - 4 * b)

                bank = b % 2

                def trb(t, b=b, ntr=ntr, bank=bank, hs=hs):
                    last = None
                    for j in range(ntr):
                        kc = 4 * b + j
                        last = t.transpose(trp[:, bank * 1024 + j * 128: bank * 1024 + (j + 1) * 128],
                                           probs_bf[:, hs * N + kc * 128: hs * N + (kc + 1) * 128],
                                           ident[:, :])
                    return last
                sc.step("T", trb, reads=[f"probsbf{hs}", "ident"], writes=[f"trp{bank}"],
                        tag=f"ph2.tr{T}.{h}.{b}")

                def cpb(v, b=b, ntr=ntr, bank=bank, hs=hs):
                    v.tensor_copy(out=probsT[:, hs * N + 4 * b * 128: hs * N + (4 * b + ntr) * 128],
                                  in_=trp[:, bank * 1024: bank * 1024 + ntr * 128])
                sc.step("V", cpb, reads=[f"trp{bank}"], writes=[f"probsT{hs}"],
                        tag=f"ph2.cp{T}.{h}.{b}")

            def mm_av(t, T=T, hs=hs):
                last = None
                for kc in range(T + 1):
                    last = t.matmul(av_ps[:, :], v_tok[:, kc * 128:(kc + 1) * 128],
                                    probsT[:, hs * N + kc * 128: hs * N + (kc + 1) * 128],
                                    start=(kc == 0), stop=(kc == T))
                return last
            sc.step("T", mm_av, reads=["v_tok", f"probsT{hs}"], writes=["av_ps"],
                    tag=f"ph2.av{T}.{h}")

            def ev_av(a, T=T, h=h):
                a.activation(out=attn_fm[:, h * N + T * 128: h * N + (T + 1) * 128],
                             in_=av_ps[:, :], func=AF.Copy)
            sc.step("A", ev_av, reads=["av_ps"], writes=["attn_fm"],
                    tag=f"ph2.ev{T}.{h}")

    sc.barrier("b23")
    free("av_ps", "sc_ps", "rinv2", "rsum2", "nb2", "rm2", "probsT", "probs_bf",
         "probs_sb", "v_tok", "vstage", "k_fm", "q_fm")

    wfc_bf = salloc("wfc_bf", [128, EC * E], BF16, side="right")
    bfc_row = salloc("bfc_row", [1, E], F32)
    ones1 = salloc("ones1", [1, 128], F32)
    bfc_bc = salloc("bfc_bc", [128, E], F32)
    ares_st = salloc("ares_st", [128, 2 * E], F32)
    fc_ps = palloc("fc_ps", [128, 2 * 512])

    sc.step("G", (lambda g, sem: g.dma_start(out=bfc_row[:, :],
                                        in_=bfc_h[:].rearrange("(o f) -> o f", o=1))
                  .then_inc(sem, 16)), writes=["bfc_row"], ndma=1, tag="ph3.b")
    sc.step("V", lambda v: v.memset(ones1[:, :], 1.0), writes=["ones1"], tag="ph3.o")
    for es in range(4):
        sc.step("T", (lambda t, es=es:
                      t.matmul(fc_ps[:, (es % 2) * 512:(es % 2 + 1) * 512],
                               ones1[:, :], bfc_row[:, es * 512:(es + 1) * 512],
                               start=True, stop=True)),
                reads=["ones1", "bfc_row"], writes=[f"fc_ps{es % 2}"],
                tag=f"ph3.bm{es}")
        sc.step("A", (lambda a, es=es:
                      a.activation(out=bfc_bc[:, es * 512:(es + 1) * 512],
                                   in_=fc_ps[:, (es % 2) * 512:(es % 2 + 1) * 512],
                                   func=AF.Copy)),
                reads=[f"fc_ps{es % 2}"], writes=["bfc_bc"], tag=f"ph3.be{es}")

    for kc in range(EC):
        sc.step("G", (lambda g, sem, kc=kc:
                      g.dma_start(out=wfc_bf[:, kc * E:(kc + 1) * E],
                                  in_=wfc_h[kc * 128:(kc + 1) * 128, :])
                      .then_inc(sem, 16)),
                writes=["wfc_bf"], ndma=1, tag=f"ph3.w{kc}")

    for T in range(NT):
        load_x(T, T, f"ph3.x{T}")
        for es in range(4):
            idx = T * 4 + es

            def mmfc(t, T=T, es=es, idx=idx):
                last = None
                for kc in range(EC):
                    last = t.matmul(
                        fc_ps[:, (idx % 2) * 512:(idx % 2 + 1) * 512],
                        attn_fm[:, kc * N + T * 128: kc * N + (T + 1) * 128],
                        wfc_bf[:, kc * E + es * 512: kc * E + (es + 1) * 512],
                        start=(kc == 0), stop=(kc == EC - 1))
                return last
            sc.step("T", mmfc, reads=["attn_fm", "wfc_bf"], writes=[f"fc_ps{idx % 2}"],
                    tag=f"ph3.mm{T}.{es}")

            def evfc(v, T=T, es=es, idx=idx):
                dst = ares_st[:, (T % 2) * E + es * 512:(T % 2) * E + (es + 1) * 512]
                v.scalar_tensor_tensor(
                    out=dst, in0=fc_ps[:, (idx % 2) * 512:(idx % 2 + 1) * 512],
                    scalar=1.0, in1=bfc_bc[:, es * 512:(es + 1) * 512],
                    op0=ALU.mult, op1=ALU.add)
                v.drain()
                v.tensor_add(dst, dst,
                             xstg[:, (T % 2) * E + es * 512:(T % 2) * E + (es + 1) * 512])
            sc.step("V", evfc, reads=[f"fc_ps{idx % 2}", "bfc_bc", f"xstg{T % 2}"],
                    writes=[f"ares{T % 2}"], tag=f"ph3.ev{T}.{es}")

        sc.step("G", (lambda g, sem, T=T:
                      g.dma_start(out=ares_h[T * 128:(T + 1) * 128, :],
                                  in_=ares_st[:, (T % 2) * E:(T % 2 + 1) * E])
                      .then_inc(sem, 16)),
                reads=[f"ares{T % 2}"], writes=[f"aresh{T}"], ndma=1, tag=f"ph3.s{T}")

    sc.barrier("b34")
    free("fc_ps", "ares_st", "bfc_bc", "ones1", "bfc_row", "wfc_bf", "attn_fm",
         "xstg")

    QN = 8
    TQ = NT // QN
    NG = N // QN
    xq_fm = salloc("xq_fm", [128, EC * NG], BF16)
    res = salloc("res", [128, TQ * E], F32)
    wgT = salloc("wgT", [128, EC * OSTRIP], BF16)
    wuT = salloc("wuT", [128, EC * OSTRIP], BF16)
    wdT = salloc("wdT", [128, 2 * 4 * E], BF16)
    ring = salloc("ring", [128, 4 * E], F32)
    qbf = salloc("qbf", [128, 2 * E], BF16)
    interall = salloc("interall", [128, 2 * TQ * 512], F32)
    sg_t = salloc("sg_t", [128, 2 * 512], F32)
    qs0A = salloc("qs0A", [128, 1024], F32)
    qs0B = salloc("qs0B", [128, 1024], F32)
    qs0C = salloc("qs0C", [128, 1024], F32)
    qs1A = salloc("qs1A", [128, 1024], F32)
    qs1B = salloc("qs1B", [128, 1024], F32)
    qs1C = salloc("qs1C", [128, 1024], F32)
    QSETS = [(qs0A, qs0B, qs0C), (qs1A, qs1B, qs1C)]
    IVO = {"g": 0, "u": 256, "d": 512, "i0": 768, "i1": 800}
    iv4 = salloc("iv4", [128, 832], F32)
    sc4 = salloc("sc4", [128, 832], F32)
    am4 = salloc("am4", [128, 832], F32)
    chA = salloc("chA", [128, 256], F32)
    chD = salloc("chD", [128, 64], F32)
    intq4 = salloc("intq4", [128, TQ * 512], BF16)
    intT2 = salloc("intT2", [128, 2 * 512], BF16)
    xn_stB = salloc("xn_stB", [128, E], F32)
    _QP = [0]

    def q_body(jobs, fmt, tag):
        tinv = 8.0 if fmt == "act" else 2.0
        mn = 7.5 if fmt == "act" else 6.0
        t2 = tinv * tinv
        units = []
        for (srcf, srcr, dstf, dstr, ivo, ivr, extra) in jobs:
            F = srcf.shape[-1]
            for hh in range(0, F, 1024):
                Fh = min(1024, F - hh)
                p = _QP[0] % 2
                _QP[0] += 1
                units.append((srcf[:, hh:hh + Fh], list(srcr),
                              dstf[:, hh:hh + Fh], list(dstr),
                              ivo + hh // 32, ivr, list(extra), Fh, p))

        def e_p1(u, k):
            srch, srcr, dsth, dstr, ivo, ivr, extra, Fh, p = u
            nb = Fh // 32
            qA = QSETS[p][0]
            ivb = iv4[:, ivo:ivo + nb].rearrange("p (b o) -> p b o", o=1) \
                .broadcast_to([128, nb, 32])

            def f(g, A3=qA[:, 0:Fh].rearrange("p (b k) -> p b k", k=32),
                  s3=srch.rearrange("p (b k) -> p b k", k=32), ivb=ivb):
                g.tensor_tensor(out=A3, in0=s3, in1=ivb, op=ALU.mult)
            sc.step("G", f, reads=srcr + [f"iv4{ivr}"], writes=[f"qs{p}"],
                    tag=f"{tag}.p1.{k}")

        def e_v1(u, k):
            srch, srcr, dsth, dstr, ivo, ivr, extra, Fh, p = u
            qA, qB, _ = QSETS[p]

            def f(v, A=qA[:, 0:Fh], B=qB[:, 0:Fh]):
                v._custom_dve(_MXQ["MXQ_IK"], out=B, in0=A, s0=4.0 * t2,
                              s1=16.0 * t2, imm2=0.5)
            sc.step("V", f, reads=[f"qs{p}"], writes=[f"qs{p}"],
                    tag=f"{tag}.v1.{k}")

        def e_p2(u, k):
            srch, srcr, dsth, dstr, ivo, ivr, extra, Fh, p = u
            qA, qB, qC = QSETS[p]

            def f(g, A=qA[:, 0:Fh], B=qB[:, 0:Fh], Cb=qC[:, 0:Fh]):
                g.tensor_mul(Cb, A, B)
            sc.step("G", f, reads=[f"qs{p}"], writes=[f"qs{p}"],
                    tag=f"{tag}.p2.{k}")

        def e_v23(u, k):
            srch, srcr, dsth, dstr, ivo, ivr, extra, Fh, p = u
            nb = Fh // 32
            qA, qB, qC = QSETS[p]
            scb = sc4[:, ivo:ivo + nb].rearrange("p (b o) -> p b o", o=1) \
                .broadcast_to([128, nb, 32])

            def f(v, A=qA[:, 0:Fh], B=qB[:, 0:Fh], Cb=qC[:, 0:Fh],
                  d3=dsth.rearrange("p (b k) -> p b k", k=32),
                  c3=qC[:, 0:Fh].rearrange("p (b k) -> p b k", k=32), scb=scb):
                if fmt == "act":
                    v._custom_dve(_MXQ["MXQ_EX3"], out=B, in0=Cb, in1=sh16[:, :],
                                  s0=16.0, s1=8.0)
                    v.drain()
                    v._custom_dve(_MXQ["MXQ_EX3"], out=B, in0=B, in1=shift0[:, :],
                                  s0=4.0, s1=2.0)
                else:
                    v._custom_dve(_MXQ["MXQ_EX3"], out=B, in0=Cb, in1=sh4[:, :],
                                  s0=4.0, s1=2.0)
                v.drain()
                v._custom_dve(_MXQ["MXQ_TIE"], out=B, in0=B, in1=Cb, s1=0.5)
                v.drain()
                v._custom_dve(_MXQ["MXQ_UNF"], out=Cb, in0=A, in1=B,
                              s0=4.0 * t2, s1=16.0 * t2, imm2=mn * tinv)
                v.drain()
                v.scalar_tensor_tensor(out=d3, in0=c3, scalar=-mn * tinv,
                                       op0=ALU.max, in1=scb, op1=ALU.mult)
            sc.step("V", f, reads=[f"qs{p}", "shifts", f"sc4{ivr}"],
                    writes=[f"qs{p}"] + dstr + extra, tag=f"{tag}.v23.{k}")

        for b0 in range(0, len(units), 2):
            pr = list(enumerate(units[b0:b0 + 2], b0))
            for k, u in pr:
                e_p1(u, k)
            for k, u in pr:
                e_v1(u, k)
            for k, u in pr:
                e_p2(u, k)
            for k, u in pr:
                e_v23(u, k)

    g_ps = palloc("g_ps", [128, 2 * 512])
    u_ps = palloc("u_ps", [128, 2 * 512])
    dn_ps = palloc("dn_ps", [128, 2 * 512])

    def rg(i):
        return ring[:, i * E:(i + 1) * E]

    QBS = [[f"qbfs{4 * h + kk}" for kk in range(4)] for h in range(2)]

    def wdTs(p):
        return wdT[:, p * 4 * E:(p + 1) * 4 * E]

    def wdT3(p):
        return wdTs(p).rearrange("p (oc es e) -> p es oc e", es=4, e=512)

    for qq in range(QN):
        T0 = qq * TQ
        for i in range(TQ):
            T = T0 + i
            lnbuf = xn_st if i % 2 == 0 else xn_stB
            sc.step("S", (lambda s, sem, T=T, i=i:
                          s.dma_start(out=rg(i % 2), in_=x_h[T * 128:(T + 1) * 128, :])
                          .then_inc(sem, 16)),
                    writes=[f"ring{i % 2}"], ndma=1, tag=f"p4a{qq}.x{i}")

            def am(v, T=T, i=i, lnbuf=lnbuf):
                v.tensor_scalar(out=lnbuf[:, :], in0=rg(i % 2),
                                scalar1=mn_all[:, T:T + 1], scalar2=rs_all[:, T:T + 1],
                                op0=ALU.subtract, op1=ALU.mult)
                v.drain()
                v.tensor_reduce(out=am4[:, i * 64:(i + 1) * 64],
                                in_=lnbuf[:, :].rearrange("p (b k) -> p b k", k=32),
                                axis=AX.X, op=ALU.max, apply_absolute_value=True)
            sc.step("V", am, reads=[f"ring{i % 2}", "mn_all", "rs_all"],
                    writes=[f"xnst{i % 2}", "am4g"], tag=f"p4a{qq}.am{i}")

        sc.step("V", (lambda v: emit_scale_chain(v, am4[:, 0:TQ * 64],
                                                 chA[:, 0:TQ * 64],
                                                 iv4[:, 0:TQ * 64], sc4[:, 0:TQ * 64],
                                                 24, CH_STD, post=8.0)),
                reads=["am4g"], writes=["iv4g", "sc4g", "chsc"],
                tag=f"p4a{qq}.ch")

        jobs = []
        for i in range(TQ):
            T = T0 + i
            lnbuf = xn_st if i % 2 == 0 else xn_stB
            sc.step("S", (lambda s, sem, T=T, i=i:
                          s.dma_start(out=rg(i % 2), in_=x_h[T * 128:(T + 1) * 128, :])
                          .then_inc(sem, 16)),
                    writes=[f"ring{i % 2}"], ndma=1, tag=f"p4b{qq}.x{i}")

            def qx(v, T=T, i=i, lnbuf=lnbuf):
                v.tensor_scalar(out=lnbuf[:, :], in0=rg(i % 2),
                                scalar1=mn_all[:, T:T + 1],
                                scalar2=rs_all[:, T:T + 1],
                                op0=ALU.subtract, op1=ALU.mult)
            sc.step("V", qx, reads=[f"ring{i % 2}", "mn_all", "rs_all"],
                    writes=[f"xnst{i % 2}"], tag=f"p4b{qq}.ln{i}")
            jobs.append((lnbuf, [f"xnst{i % 2}"],
                         qbf[:, (i % 2) * E:(i % 2 + 1) * E], QBS[i % 2],
                         i * 64, "g", []))
        q_body(jobs, "act", f"p4b{qq}.q")
        for i in range(TQ):
            tr16(qbf[:, (i % 2) * E:(i % 2 + 1) * E], xq_fm, "xq_fm", None,
                 f"p4b{qq}.t{i}", QBS[i % 2],
                 dst3d=(xq_fm[:, :].rearrange("p (c n) -> p c n", n=NG),
                        i * 128), cpeng="A")
        for i in range(TQ):
            sc.step("V", (lambda v, i=i: v.memset(res[:, i * E:(i + 1) * E], 0.0)),
                    writes=[f"res{i}"], tag=f"p4b{qq}.z{i}")

        def stage_weights(s_i):
            o0 = s_i * OSTRIP
            if qq == 0:
                for m in range(2):
                    srcw = wg_h if m == 0 else wu_h
                    dstc = wgt_h if m == 0 else wut_h
                    dstT = wgT if m == 0 else wuT
                    pfx = "wgTq" if m == 0 else "wuTq"
                    reg = "g" if m == 0 else "u"
                    v0 = IVO[reg]
                    for it in range(4):
                        sc.step("S", (lambda s, sem, srcw=srcw, o0=o0, it=it:
                                      s.dma_start(out=rg(it),
                                                  in_=srcw[o0 + it * 128:o0 + (it + 1) * 128, :])
                                      .then_inc(sem, 16)),
                                writes=[f"ring{it}"], ndma=1, tag=f"w{m}l{s_i}.{it}")
                        sc.step("V", (lambda v, it=it, v0=v0:
                                      v.tensor_reduce(out=am4[:, v0 + it * 64:v0 + (it + 1) * 64],
                                                      in_=rg(it).rearrange("p (b k) -> p b k", k=32),
                                                      axis=AX.X, op=ALU.max,
                                                      apply_absolute_value=True)),
                                reads=[f"ring{it}"], writes=[f"am4{reg}"], tag=f"w{m}a{s_i}.{it}")
                    sc.step("V", (lambda v, v0=v0: emit_scale_chain(
                                v, am4[:, v0:v0 + 256], chA[:, :],
                                iv4[:, v0:v0 + 256], sc4[:, v0:v0 + 256],
                                24, CH_STD, post=2.0)),
                            reads=[f"am4{reg}"], writes=[f"iv4{reg}", f"sc4{reg}", "chsc"],
                            tag=f"w{m}c{s_i}")
                    for itp in (0, 2):
                        q_body([(rg(it), [f"ring{it}"],
                                 qbf[:, (it % 2) * E:(it % 2 + 1) * E], QBS[it % 2],
                                 v0 + it * 64, reg, []) for it in (itp, itp + 1)],
                               "wgt", f"w{m}q{s_i}.{itp}")
                        for it in (itp, itp + 1):
                            tr16(qbf[:, (it % 2) * E:(it % 2 + 1) * E], dstT,
                                 (lambda b, pfx=pfx: f"{pfx}{b}"),
                                 None, f"w{m}t{s_i}.{it}", QBS[it % 2],
                                 dst3d=(dstT[:, :].rearrange("p (c n) -> p c n", n=OSTRIP),
                                        it * 128), cpeng="A")
                    for hcq in range(4):
                        sc.step("G", (lambda g, sem, dstc=dstc, s_i=s_i, hcq=hcq, dstT=dstT:
                                      g.dma_start(out=dstc[s_i, :, hcq * E:(hcq + 1) * E],
                                                  in_=dstT[:, hcq * E:(hcq + 1) * E])
                                      .then_inc(sem, 16)),
                                reads=[f"{pfx}{hcq}"], writes=[f"c{m}.{s_i}.{hcq}"],
                                ndma=1, tag=f"w{m}cw{s_i}.{hcq}")
                for j in range(16):
                    sc.step("S", (lambda s, sem, o0=o0, j=j:
                                  s.dma_start(out=ring[:, j * 512:(j + 1) * 512],
                                              in_=wd_h[j * 128:(j + 1) * 128, o0:o0 + OSTRIP])
                                  .then_inc(sem, 16)),
                            writes=[f"ring{j // 4}"], ndma=1, tag=f"wdl{s_i}.{j}")
                    sc.step("V", (lambda v, j=j:
                                  v.tensor_reduce(out=am4[:, 512 + j * 16:512 + (j + 1) * 16],
                                                  in_=ring[:, j * 512:(j + 1) * 512]
                                                  .rearrange("p (b k) -> p b k", k=32),
                                                  axis=AX.X, op=ALU.max,
                                                  apply_absolute_value=True)),
                            reads=[f"ring{j // 4}"], writes=["am4d"], tag=f"wda{s_i}.{j}")
                sc.step("V", (lambda v: emit_scale_chain(v, am4[:, 512:768], chA[:, :],
                                                         iv4[:, 512:768],
                                                         sc4[:, 512:768],
                                                         24, CH_STD, post=2.0)),
                        reads=["am4d"], writes=["iv4d", "sc4d", "chsc"],
                        tag=f"wdc{s_i}")
                for jb in (0, 8):
                    q_body([(ring[:, j * 512:(j + 2) * 512],
                             [f"ring{j // 4}", f"ring{(j + 1) // 4}"],
                             qbf[:, (j % 8) * 512:(j % 8 + 2) * 512],
                             [f"qbfs{j % 8}", f"qbfs{j % 8 + 1}"],
                             512 + j * 16, "d",
                             [f"wdq{s_i}.{j}", f"wdq{s_i}.{j + 1}"])
                            for j in range(jb, jb + 8, 2)],
                           "wgt", f"wdq{s_i}.{jb}")
                    for j in range(jb, jb + 8):
                        bank = j % 2

                        def trd(t, j=j, bank=bank):
                            last = None
                            for oc in range(4):
                                last = t.transpose(
                                    trp[:, bank * 1024 + oc * 128: bank * 1024 + (oc + 1) * 128],
                                    qbf[:, (j % 8) * 512 + oc * 128:(j % 8) * 512 + (oc + 1) * 128],
                                    ident[:, :])
                            return last
                        sc.step("T", trd, reads=[f"wdq{s_i}.{j}", f"qbfs{j % 8}",
                                                 "ident"], writes=[f"trp{bank}"],
                                tag=f"wdt{s_i}.{j}")

                        def cpd(a, j=j, bank=bank, p=s_i % 2):
                            a.copy(out=wdTs(p).rearrange("p (oc e) -> p oc e", e=E)
                                   [:, :, j * 128:(j + 1) * 128],
                                   in_=trp[:, bank * 1024: bank * 1024 + 512]
                                   .rearrange("p (oc f) -> p oc f", f=128))
                        sc.step("A", cpd, reads=[f"trp{bank}"],
                                writes=[f"wdTe{s_i % 2}.{j // 4}"],
                                tag=f"wdcp{s_i}.{j}")
                for es in range(4):
                    sc.step("G", (lambda g, sem, s_i=s_i, es=es:
                                  g.dma_start(out=wdt_h[s_i, es, :, :],
                                              in_=wdT3(s_i % 2)[:, es])
                                  .then_inc(sem, 16)),
                            reads=[f"wdTe{s_i % 2}.{es}"], writes=[f"cd.{s_i}.{es}"],
                            ndma=1, tag=f"wdcw{s_i}.{es}")
            else:
                for m in range(2):
                    srcc = wgt_h if m == 0 else wut_h
                    dstT = wgT if m == 0 else wuT
                    pfx = "wgTq" if m == 0 else "wuTq"
                    for hcq in range(4):
                        sc.step("S", (lambda s, sem, srcc=srcc, s_i=s_i, hcq=hcq, dstT=dstT:
                                      s.dma_start(out=dstT[:, hcq * E:(hcq + 1) * E],
                                                  in_=srcc[s_i, :, hcq * E:(hcq + 1) * E])
                                      .then_inc(sem, 16)),
                                reads=[f"c{m}.{s_i}.{hcq}"], writes=[f"{pfx}{hcq}"],
                                ndma=1, tag=f"w{m}lb{s_i}.{hcq}")
                for es in range(4):
                    sc.step("S", (lambda s, sem, s_i=s_i, es=es:
                                  s.dma_start(out=wdT3(s_i % 2)[:, es],
                                              in_=wdt_h[s_i, es, :, :])
                                  .then_inc(sem, 16)),
                            reads=[f"cd.{s_i}.{es}"], writes=[f"wdTe{s_i % 2}.{es}"],
                            ndma=1, tag=f"wdlb{s_i}.{es}")

        def pass1_T(s_i):
            for m in range(2):
                dstT = wgT if m == 0 else wuT
                pfx = "wgTq" if m == 0 else "wuTq"
                ps = g_ps if m == 0 else u_ps
                pn = "g_ps" if m == 0 else "u_ps"
                for hcq in range(4):
                    def mmq(t, hcq=hcq, dstT=dstT, ps=ps):
                        last = None
                        for tt in range(TQ):
                            for hc in range(4 * hcq, 4 * hcq + 4):
                                last = t.matmul(
                                    ps[:, tt * 512:(tt + 1) * 512],
                                    xq_fm[:, hc * NG + tt * 128: hc * NG + (tt + 1) * 128],
                                    dstT[:, hc * OSTRIP:(hc + 1) * OSTRIP],
                                    start=(hc == 0), stop=(hc == EC - 1))
                        return last
                    sc.step("T", mmq, reads=["xq_fm", f"{pfx}{hcq}"], writes=[pn],
                            tag=f"ff{m}{qq}.{s_i}.{hcq}")
            for tt in range(TQ):
                sc.step("A", (lambda a, tt=tt:
                              a.activation(out=sg_t[:, tt * 512:(tt + 1) * 512],
                                           in_=g_ps[:, tt * 512:(tt + 1) * 512],
                                           func=AF.Silu)),
                        reads=["g_ps"], writes=[f"sg{tt}"], tag=f"ffs{qq}.{s_i}.{tt}")

        def pass1_V(s_i):
            p = s_i % 2
            i0 = IVO[f"i{p}"]
            for tt in range(TQ):
                def mul(v, tt=tt, p=p, i0=i0):
                    v.tensor_mul(interall[:, (p * TQ + tt) * 512:(p * TQ + tt + 1) * 512],
                                 sg_t[:, tt * 512:(tt + 1) * 512],
                                 u_ps[:, tt * 512:(tt + 1) * 512])
                    v.drain()
                    v.tensor_reduce(out=am4[:, i0 + tt * 16: i0 + (tt + 1) * 16],
                                    in_=interall[:, (p * TQ + tt) * 512:(p * TQ + tt + 1) * 512]
                                    .rearrange("p (b k) -> p b k", k=32),
                                    axis=AX.X, op=ALU.max, apply_absolute_value=True)
                sc.step("V", mul, reads=[f"sg{tt}", "u_ps"],
                        writes=[f"inter{p}.{tt}", f"am4i{p}"], tag=f"ffm{qq}.{s_i}.{tt}")

            sc.step("V", (lambda v, i0=i0: emit_scale_chain(
                        v, am4[:, i0:i0 + TQ * 16], chD[:, 0:TQ * 16],
                        iv4[:, i0:i0 + TQ * 16], sc4[:, i0:i0 + TQ * 16],
                        43, CH_INT, post=8.0)),
                    reads=[f"am4i{p}"], writes=[f"iv4i{p}", f"sc4i{p}", "chsci"],
                    tag=f"ffc{qq}.{s_i}")

        def pass2(s_i):
            p = s_i % 2
            i0 = IVO[f"i{p}"]
            q_body([(interall[:, (p * TQ + tt) * 512:(p * TQ + tt + 1) * 512],
                     [f"inter{p}.{tt}"],
                     intq4[:, tt * 512:(tt + 1) * 512], [f"intq{tt}"],
                     i0 + tt * 16, f"i{p}", []) for tt in range(TQ)],
                   "act", f"ffq{qq}.{s_i}")
            for tt in range(TQ):
                bank = tt % 2

                def tri(t, tt=tt, bank=bank):
                    last = None
                    for oc in range(4):
                        last = t.transpose(trp[:, bank * 1024 + oc * 128: bank * 1024 + (oc + 1) * 128],
                                           intq4[:, tt * 512 + oc * 128: tt * 512 + (oc + 1) * 128],
                                           ident[:, :])
                    return last
                sc.step("T", tri, reads=[f"intq{tt}", "ident"], writes=[f"trp{bank}"],
                        tag=f"fft{qq}.{s_i}.{tt}")
                sc.step("A", (lambda a, tt=tt, bank=bank:
                              a.copy(out=intT2[:, tt * 512:(tt + 1) * 512],
                                     in_=trp[:, bank * 1024: bank * 1024 + 512])),
                        reads=[f"trp{bank}"], writes=[f"intT{tt}"],
                        tag=f"ffcp{qq}.{s_i}.{tt}")

            for es in range(4):
                for tt in range(TQ):
                    def mmd(t, es=es, tt=tt, p=p):
                        last = None
                        for oc in range(4):
                            last = t.matmul(
                                dn_ps[:, (tt % 2) * 512:(tt % 2 + 1) * 512],
                                intT2[:, tt * 512 + oc * 128: tt * 512 + (oc + 1) * 128],
                                wdTs(p)[:, oc * E + es * 512: oc * E + (es + 1) * 512],
                                start=(oc == 0), stop=(oc == 3))
                        return last
                    sc.step("T", mmd, reads=[f"intT{tt}", f"wdTe{p}.{es}"],
                            writes=[f"dn_ps{tt % 2}"],
                            tag=f"ffd{qq}.{s_i}.{es}.{tt}")

                    def acc(v, tt=tt, es=es):
                        dst = res[:, tt * E + es * 512: tt * E + (es + 1) * 512]
                        v.tensor_add(dst, dst,
                                     dn_ps[:, (tt % 2) * 512:(tt % 2 + 1) * 512])
                    sc.step("V", acc, reads=[f"dn_ps{tt % 2}"], writes=[f"res{tt}"],
                            tag=f"ffa{qq}.{s_i}.{es}.{tt}")

        for s_i in range(NSTRIP + 1):
            if s_i < NSTRIP:
                stage_weights(s_i)
                pass1_T(s_i)
            if s_i > 0:
                pass2(s_i - 1)
            if s_i < NSTRIP:
                pass1_V(s_i)

        for tt in range(TQ):
            T = T0 + tt
            sc.step("G", (lambda g, sem, T=T: g.dma_start(out=rg(0),
                                                     in_=ares_h[T * 128:(T + 1) * 128, :])
                          .then_inc(sem, 16)),
                    reads=[f"aresh{T}"], writes=["ring0"], ndma=1, tag=f"fin{qq}.l{tt}")
            sc.step("V", (lambda v, tt=tt: v.tensor_add(rg(1), rg(0),
                                                        res[:, tt * E:(tt + 1) * E])),
                    reads=["ring0", f"res{tt}"], writes=["ring1"], tag=f"fin{qq}.a{tt}")
            sc.step("G", (lambda g, sem, T=T: g.dma_start(out=out_h[T * 128:(T + 1) * 128, :],
                                                     in_=rg(1))
                          .then_inc(sem, 16)),
                    reads=["ring1"], writes=[], ndma=1, tag=f"fin{qq}.s{tt}")
        if qq < QN - 1:
            sc.barrier(f"bq{qq}")

    blk = nc.Block()
    block = blk.__enter__()
    sc.emit(nc, block, SEMS)
    blk.__exit__(None, None, None)
    for g in semg:
        g.__exit__(None, None, None)
    for nm in reversed(list(live)):
        live.pop(nm).__exit__(None, None, None)
    mybir.codegen_inst_isa_subclasses(nc)
    return nc


_NC_CACHE = []


def kernel(L0_attn_x, L0_attn_w_qkv, L0_attn_b_qkv, L0_attn_w_fc, L0_attn_b_fc,
           L0_ffn_w_gate, L0_ffn_w_up, L0_ffn_w_down):
    import ml_dtypes
    if not _NC_CACHE:
        _NC_CACHE.append(build_program())
    nc = _NC_CACHE[0]
    x = np.ascontiguousarray(np.asarray(L0_attn_x, dtype=np.float32))
    wqkv = np.ascontiguousarray(np.asarray(L0_attn_w_qkv, dtype=np.float32))
    bqkv = np.ascontiguousarray(np.asarray(L0_attn_b_qkv, dtype=np.float32))
    wfc = np.ascontiguousarray(np.asarray(L0_attn_w_fc, dtype=np.float32))
    bfc = np.ascontiguousarray(np.asarray(L0_attn_b_fc, dtype=np.float32))
    wg = np.ascontiguousarray(np.asarray(L0_ffn_w_gate, dtype=np.float32))
    wu = np.ascontiguousarray(np.asarray(L0_ffn_w_up, dtype=np.float32))
    wd = np.ascontiguousarray(np.asarray(L0_ffn_w_down, dtype=np.float32))
    ident = np.eye(128, dtype=ml_dtypes.bfloat16)
    cmask = np.triu(np.full((128, 128), -1.0e30, np.float32), k=1)
    in_maps = []
    for b in range(8):
        in_maps.append({
            "x": np.ascontiguousarray(x[b]),
            "w_qkv": wqkv, "b_qkv": bqkv, "w_fc": wfc, "b_fc": bfc,
            "w_gate": wg, "w_up": wu, "w_down": wd,
            "ident": ident, "cmask": cmask,
        })
    res = run_bass_kernel_spmd(nc, in_maps, core_ids=list(range(8)))
    out = np.stack([np.asarray(r["out"], dtype=np.float32) for r in res.results], axis=0)
    return out


# version 12 (final)
# speedup vs baseline: 2.5842x; 1.0004x over previous
import numpy as np
import concourse.bass as bass
import concourse.mybir as mybir
from concourse.bass_utils import run_bass_kernel_spmd
import concourse.dve_ops as dve_ops
from concourse.dve_op import DveOp
from concourse.dve_spec import (
    Spec, Src0, Src1, C0, C1, C2, C3, One, Bin, AluOp, eq, lower as dve_lower,
)
from concourse.dve_table_gen import dve_ver_for

F32 = mybir.dt.float32
BF16 = mybir.dt.bfloat16
ALU = mybir.AluOpType
AX = mybir.AxisListType
AF = mybir.ActivationFunctionType

N = 2048
E = 2048
NT = N // 128
EC = E // 128
H = 16
QKVF = 2304
QC = QKVF // 128
I = 8192
OSTRIP = 512
NSTRIP = I // OSTRIP
HT = NT // 2
SM = 1.0 / float(np.sqrt(np.float32(128)))



def _cdt(a):
    return np.float64 if a.dtype == np.float64 else np.float32


def _ref_ik(in0, in1, s0, s1, imm2):
    f = _cdt(in0)
    a2 = in0.astype(f) * in0
    return (1.0 - (a2 >= s0) * imm2) * (1.0 - (a2 >= s1) * imm2)


def _ref_ex3(in0, in1, s0, s1, imm2):
    f = _cdt(in0)
    z = in0.astype(f) + in1
    b = (z >= s0).astype(f)
    z1 = z - b * s0
    b1 = (z1 >= s1).astype(f)
    return z1 - b1 * s1


def _ref_tie(in0, in1, s0, s1, imm2):
    f = _cdt(in0)
    z0 = in0.astype(f)
    b0 = (z0 >= 1.0).astype(f)
    fr = z0 - b0
    c1 = (fr > s1).astype(f)
    e = (fr == s1).astype(f)
    return in1 + ((c1 - fr) + e * b0)


def _ref_chi(in0, in1, s0, s1, imm2):
    f = _cdt(in0)
    t = in0.astype(f) * in1
    return in1 + in1 * ((t >= s0) * s1)


def _ref_unf(in0, in1, s0, s1, imm2):
    f = _cdt(in0)
    a2 = in0.astype(f) * in0
    kk = (1.0 + (a2 >= s0)) * (1.0 + (a2 >= s1))
    return np.minimum(in1 * kk, imm2)


def _mxq_specs():
    u2a = Src0 * Src0
    ik_body = (One - (u2a >= C0) * C2) * (One - (u2a >= C1) * C2)
    zs = Src0 + C3
    bb = zs >= C0
    zz1 = zs - bb * C0
    bb1 = zz1 >= C1
    ex3_body = zz1 - bb1 * C1
    b0 = Bin(AluOp.IS_GE, Src0, One)
    fr = Src0 - b0
    c1 = Bin(AluOp.IS_LT, C1, fr)
    e = eq(fr, C1)
    tie_body = Src1 + ((c1 - fr) + (e & b0))
    u2b = Src0 * Src0
    kk = (One + (u2b >= C0)) * (One + (u2b >= C1))
    unf_body = Bin(AluOp.MIN, Src1 * kk, C2)
    chi_body = Src1 + Src1 * (((Src0 * Src1) >= C0) * C1)
    return {
        "MXQ_CHI": (Spec(body=chi_body), _ref_chi),
        "MXQ_IK": (Spec(body=ik_body), _ref_ik),
        "MXQ_EX3": (Spec(body=ex3_body), _ref_ex3),
        "MXQ_TIE": (Spec(body=tie_body), _ref_tie),
        "MXQ_UNF": (Spec(body=unf_body), _ref_unf),
    }


_MXQ = {}


def _register_mxq():
    if _MXQ:
        return
    if any(op.name == "MXQ_IK" for op in dve_ops.OPS):
        for op in dve_ops.OPS:
            if op.name.startswith("MXQ_"):
                _MXQ[op.name] = op
        return
    ver = dve_ver_for("TRN2")
    replaceable = ["DVEHASH32L_SEED", "DVEHASH32L_BODY", "DVEHASH32L_DRAIN"]
    for name, (spec, ref) in _mxq_specs().items():
        compiled = dve_lower(spec, ver=ver, perf_en=False, name=name)
        op = DveOp(name, spec, subdim=False, uops_sha={ver: compiled.sha(ver)},
                   reference=ref)
        if max(dve_ops._SUB_OPCODE_FOR_NAME.values()) < 0x1F:
            row = max(dve_ops._SUB_OPCODE_FOR_NAME.values()) + 1
            dve_ops.OPS.append(op)
        else:
            victim = replaceable.pop()
            idx = next(i for i, o in enumerate(dve_ops.OPS) if o.name == victim)
            row = dve_ops._SUB_OPCODE_FOR_NAME.pop(victim)
            dve_ops.OPS[idx] = op
            dve_ops.CUSTOM_DVE_SPECS.pop(victim, None)
            dve_ops.CUSTOM_DVE_REFS.pop(victim, None)
        dve_ops._SUB_OPCODE_FOR_NAME[name] = row
        dve_ops.CUSTOM_DVE_SPECS[name] = spec
        dve_ops.CUSTOM_DVE_REFS[name] = op.reference
        _MXQ[name] = op



DMA_ENGS = {"S", "G"}
ALL_ENGS = ("S", "G", "T", "V", "A")


class Sched:
    DRAIN_DIST = 3

    def __init__(self):
        self.steps = []
        self.lw = {}
        self.rd = {}

    def step(self, eng, fn, reads=(), writes=(), ndma=0, tag="", flush=False):
        i = len(self.steps)
        st = {"eng": eng, "fn": fn, "deps": set(), "ndma": ndma, "tag": tag,
              "flush": flush}
        self.steps.append(st)
        for r in reads:
            for j in self.lw.get(r, ()):
                st["deps"].add(j)
            self.rd.setdefault(r, []).append(i)
        for w in writes:
            for j in self.lw.get(w, ()):
                st["deps"].add(j)
            readers = self.rd.get(w, ())
            for j in readers:
                if j != i:
                    st["deps"].add(j)
            if readers:
                self.lw[w] = [i]
            else:
                self.lw.setdefault(w, []).append(i)
            self.rd[w] = []
        return i

    def barrier(self, tag="bar"):
        toks = [f"_{tag}_{e}" for e in ALL_ENGS]
        for e, t in zip(ALL_ENGS, toks):
            self.step(e, lambda h: None, writes=[t], tag=f"{tag}.mark.{e}",
                      flush=(e in DMA_ENGS))
        for e in ALL_ENGS:
            self.step(e, lambda h: None, reads=list(toks), tag=f"{tag}.wait.{e}")

    def emit(self, nc, block, sems):
        steps = self.steps
        n = len(steps)
        ords = {}
        cnt = {e: 0 for e in ALL_ENGS}
        for i, st in enumerate(steps):
            ords[i] = cnt[st["eng"]]
            cnt[st["eng"]] += 1
        needed = [False] * n
        drain_before = [False] * n
        for i, st in enumerate(steps):
            for j in st["deps"]:
                if steps[j]["eng"] == st["eng"]:
                    if (ords[i] - ords[j] < self.DRAIN_DIST
                            and st["eng"] in ("V", "A", "G")):
                        drain_before[i] = True
                else:
                    needed[j] = True
        K = 6
        ev = [None] * n
        run = {e: 0 for e in ALL_ENGS}
        dma_ord = {e: 0 for e in DMA_ENGS}
        sem_tot = {(e, k): 0 for e in DMA_ENGS for k in range(K)}
        self_wait = [None] * n
        for i, st in enumerate(steps):
            e = st["eng"]
            if e in DMA_ENGS and st["ndma"]:
                k = dma_ord[e] % K
                dma_ord[e] += 1
                self_wait[i] = (k, sem_tot[(e, k)])
                sem_tot[(e, k)] += 16 * st["ndma"]
                ev[i] = (e, k, sem_tot[(e, k)])
            else:
                if needed[i]:
                    run[e] += 1
                ev[i] = (e, "C" if e in DMA_ENGS else None, run[e])
            if st.get("flush"):
                st["flush_vals"] = tuple(sem_tot[(e, k)] for k in range(K)) \
                    if e in DMA_ENGS else None
        totals = {e: [sem_tot[(e, k)] for k in range(K)] for e in DMA_ENGS}
        waits = [[] for _ in range(n)]
        for i, st in enumerate(steps):
            for j in st["deps"]:
                pe = steps[j]["eng"]
                pj, kj, vj = ev[j]
                if pe == st["eng"]:
                    if pe in DMA_ENGS and kj != "C":
                        waits[i].append((pe, kj, vj))
                    continue
                if pe in DMA_ENGS and kj != "C":
                    waits[i].append((pe, kj, vj))
                elif needed[j]:
                    waits[i].append((pe, kj, vj))
        seen = {}
        pruned = [[] for _ in range(n)]
        for i, st in enumerate(steps):
            e = st["eng"]
            best = {}
            for (pe, k2, v) in waits[i]:
                key = (pe, k2)
                best[key] = max(best.get(key, 0), v)
            for (pe, k2), v in sorted(best.items(), key=lambda kv: (kv[0][0], str(kv[0][1]))):
                gk = (e, pe, k2)
                if v == 0 or seen.get(gk, -1) >= v:
                    continue
                seen[gk] = v
                pruned[i].append((pe, k2, v))

        def semof(pe, k2):
            if k2 is None:
                return sems[pe]
            return sems[pe + ("C" if k2 == "C" else str(k2))]

        def runner(eng):
            def body(h):
                inc_since_wait = [False]
                ninc = [0]
                for i, st in enumerate(steps):
                    if st["eng"] != eng:
                        continue
                    if pruned[i]:
                        inc_since_wait[0] = False
                    for (pe, k2, v) in pruned[i]:
                        h.wait_ge(semof(pe, k2), v)
                    if drain_before[i]:
                        h.drain()
                    if st.get("flush") and st.get("flush_vals"):
                        for k2f, vf in enumerate(st["flush_vals"]):
                            if vf:
                                h.wait_ge(semof(eng, k2f), vf)
                        inc_since_wait[0] = False
                    if eng in DMA_ENGS and st["ndma"]:
                        k2, v0 = self_wait[i]
                        if v0:
                            h.wait_ge(semof(eng, k2), v0)
                        last = st["fn"](h, semof(eng, k2))
                    else:
                        last = st["fn"](h)
                        if needed[i]:
                            csem = sems[eng + "C"] if eng in DMA_ENGS else sems[eng]
                            if inc_since_wait[0]:
                                h.wait_ge(csem, ninc[0])
                            if eng == "T":
                                if last is not None:
                                    last.then_inc(csem, 1)
                                else:
                                    h.nop().then_inc(csem, 1)
                            else:
                                h.drain()
                                h.nop().then_inc(csem, 1)
                            ninc[0] += 1
                            inc_since_wait[0] = True
                if eng in DMA_ENGS:
                    for k2 in range(K):
                        if totals[eng][k2]:
                            h.wait_ge(semof(eng, k2), totals[eng][k2])
            return body

        block.sync(runner("S"))
        block.gpsimd(runner("G"))
        block.tensor(runner("T"))
        block.vector(runner("V"))
        block.scalar(runner("A"))




def emit_scale_chain(v, amax_ap, tmp_ap, invs_ap, sca_ap, pre, js,
                     post=1.0):
    assert len(js) % 2 == 1
    v.memset(tmp_ap, 2.0**(pre - 1))
    v.drain()
    cur, nxt = tmp_ap, invs_ap
    for j in js:
        v._custom_dve(_MXQ["MXQ_CHI"], out=nxt, in0=amax_ap, in1=cur,
                      s0=4.0 * 2.0**j, s1=2.0**-j - 1.0)
        v.drain()
        cur, nxt = nxt, cur
    if post != 1.0:
        v.tensor_scalar_mul(invs_ap, invs_ap, post)
        v.drain()
    v.reciprocal(sca_ap, invs_ap)




CH_FULL = (64, 32, 16, 8, 4, 2, 1)
CH_STD = (16, 8, 4, 2, 1)
CH_INT = (16, 16, 16, 8, 4, 2, 1)




def build_program():
    _register_mxq()
    nc = bass.Bass("TRN2", target_bir_lowering=False)

    x_h = nc.dram_tensor("x", [N, E], F32, kind="ExternalInput").ap()
    wqkv_h = nc.dram_tensor("w_qkv", [E, QKVF], F32, kind="ExternalInput").ap()
    bqkv_h = nc.dram_tensor("b_qkv", [QKVF], F32, kind="ExternalInput").ap()
    wfc_h = nc.dram_tensor("w_fc", [E, E], F32, kind="ExternalInput").ap()
    bfc_h = nc.dram_tensor("b_fc", [E], F32, kind="ExternalInput").ap()
    wg_h = nc.dram_tensor("w_gate", [I, E], F32, kind="ExternalInput").ap()
    wu_h = nc.dram_tensor("w_up", [I, E], F32, kind="ExternalInput").ap()
    wd_h = nc.dram_tensor("w_down", [E, I], F32, kind="ExternalInput").ap()
    ident_h = nc.dram_tensor("ident", [128, 128], BF16, kind="ExternalInput").ap()
    cmask_h = nc.dram_tensor("cmask", [128, 128], F32, kind="ExternalInput").ap()
    out_h = nc.dram_tensor("out", [N, E], F32, kind="ExternalOutput").ap()

    ares_h = nc.dram_tensor("ares_scr", [N, E], F32).ap()
    wgt_h = nc.dram_tensor("wgt_c", [NSTRIP, 128, (E // 128) * OSTRIP], BF16).ap()
    wut_h = nc.dram_tensor("wut_c", [NSTRIP, 128, (E // 128) * OSTRIP], BF16).ap()
    wdt_h = nc.dram_tensor("wdt_c", [NSTRIP, 4, 128, E], BF16).ap()

    live = {}

    def salloc(name, shape, dtype, side=None):
        g = nc.sbuf_tensor(name, shape, dtype, side=side)
        t = g.__enter__()
        live[name] = g
        return t

    def palloc(name, shape, dtype=F32):
        g = nc.psum_tensor(name, shape, dtype)
        t = g.__enter__()
        live[name] = g
        return t

    def free(*names):
        for nm in names:
            live.pop(nm).__exit__(None, None, None)

    semg = []
    SEMS = {}
    for e in ("T", "V", "A"):
        g = nc.semaphore("sem" + e)
        SEMS[e] = g.__enter__()
        semg.append(g)
    for e in ("S", "G"):
        g = nc.semaphore(f"sem{e}C")
        SEMS[e + "C"] = g.__enter__()
        semg.append(g)
        for k in range(6):
            g = nc.semaphore(f"sem{e}{k}")
            SEMS[e + str(k)] = g.__enter__()
            semg.append(g)

    sc = Sched()

    ident = salloc("identt", [128, 128], BF16)
    cmask = salloc("cmaskt", [128, 128], F32)
    shift0 = salloc("shift0t", [128, 1], F32)
    sh16 = salloc("sh16", [128, 1], F32)
    sh4 = salloc("sh4", [128, 1], F32)
    bq_sb = salloc("bq_sb", [128, QC], F32)
    sm_all = salloc("sm_all", [128, NT], F32)
    sq_all = salloc("sq_all", [128, NT], F32)
    mn_all = salloc("mn_all", [128, NT], F32)
    va_all = salloc("va_all", [128, NT], F32)
    t_all = salloc("t_all", [128, NT], F32)
    sd_all = salloc("sd_all", [128, NT], F32)
    rs_all = salloc("rs_all", [128, NT], F32)
    xn_st = salloc("xn_st", [128, E], F32)
    trp_t = palloc("trp", [128, 2 * 1024], BF16)
    trp = trp_t

    def st_setup(s, sem):
        s.dma_start(out=ident[:, :], in_=ident_h[:, :]).then_inc(sem, 16)
        s.dma_start(out=cmask[:, :], in_=cmask_h[:, :]).then_inc(sem, 16)
        for c in range(QC):
            s.dma_start(out=bq_sb[:, c:c + 1],
                        in_=bqkv_h[c * 128:(c + 1) * 128]
                        .rearrange("(p o) -> p o", o=1)).then_inc(sem, 16)

    sc.step("S", st_setup, writes=["ident", "cmask", "bq_sb"], ndma=2 + QC)
    def v_sh(v):
        v.memset(shift0[:, :], 0.0)
        v.memset(sh16[:, :], 16.0)
        v.memset(sh4[:, :], 4.0)
    sc.step("V", v_sh, writes=["shifts"])

    xstg = salloc("xstg", [128, 2 * E], F32)
    junk = salloc("junk", [128, E], F32)
    xn_fm = salloc("xn_fm", [128, EC * N], BF16)
    xnbf = salloc("xnbf", [128, E], BF16)

    def xs(i):
        return xstg[:, (i % 2) * E:(i % 2 + 1) * E]

    def load_x(T, slot, tag):
        sc.step("S", (lambda s, sem, T=T, slot=slot:
                      s.dma_start(out=xs(slot), in_=x_h[T * 128:(T + 1) * 128, :])
                      .then_inc(sem, 16)),
                writes=[f"xstg{slot % 2}"], ndma=1, tag=tag)

    for T in range(NT):
        load_x(T, T, f"ph0a.x{T}")
        sc.step("A", (lambda a, T=T: a.activation(out=junk[:, :], in_=xs(T),
                                                  func=AF.Square,
                                                  accum_out=sq_all[:, T:T + 1])),
                reads=[f"xstg{T % 2}"], writes=["sq_all", "junk"], tag=f"ph0a.sq{T}")
        sc.step("V", (lambda v, T=T: v.reduce_sum(sm_all[:, T:T + 1], xs(T), axis=AX.X)),
                reads=[f"xstg{T % 2}"], writes=["sm_all"], tag=f"ph0a.sm{T}")

    def st_stats(v):
        v.tensor_scalar_mul(mn_all[:, :], sm_all[:, :], 1.0 / E)
        v.tensor_scalar_mul(va_all[:, :], sq_all[:, :], 1.0 / E)
        v.drain()
        v.tensor_mul(t_all[:, :], mn_all[:, :], mn_all[:, :])
        v.drain()
        v.tensor_sub(va_all[:, :], va_all[:, :], t_all[:, :])
        v.drain()
        v.tensor_scalar_add(va_all[:, :], va_all[:, :], 1e-5)

    sc.step("V", st_stats, reads=["sm_all", "sq_all"],
            writes=["mn_all", "va_all", "t_all"], tag="stats")
    sc.step("A", lambda a: a.sqrt(sd_all[:, :], va_all[:, :]), reads=["va_all"],
            writes=["sd_all"], tag="sqrt")
    sc.step("V", lambda v: v.reciprocal(rs_all[:, :], sd_all[:, :]), reads=["sd_all"],
            writes=["rs_all"], tag="rstd")

    def emit_xn(v, slot, T):
        v.tensor_scalar(out=xn_st[:, :], in0=xs(slot), scalar1=mn_all[:, T:T + 1],
                        scalar2=rs_all[:, T:T + 1], op0=ALU.subtract, op1=ALU.mult)

    def tr16(src, dst, dstw, coloff_fn, tag, src_res, dst3d=None, cpeng="V"):
        for b in range(4):
            bank = b % 2

            def trb(t, b=b, bank=bank):
                last = None
                for j in range(4):
                    c = 4 * b + j
                    last = t.transpose(trp[:, bank * 1024 + j * 128: bank * 1024 + (j + 1) * 128],
                                       src[:, c * 128:(c + 1) * 128], ident[:, :])
                return last
            sc.step("T", trb, reads=list(src_res) + ["ident"], writes=[f"trp{bank}"],
                    tag=f"{tag}.tr{b}")

            if dst3d is not None:
                def cpb(h, b=b, bank=bank):
                    ap3, off = dst3d
                    cp = h.tensor_copy if cpeng == "V" else h.copy
                    cp(out=ap3[:, 4 * b:4 * b + 4, off:off + 128],
                       in_=trp[:, bank * 1024: bank * 1024 + 512]
                       .rearrange("p (j f) -> p j f", f=128))
            else:
                def cpb(h, b=b, bank=bank):
                    cp = h.tensor_copy if cpeng == "V" else h.copy
                    for j in range(4):
                        c = 4 * b + j
                        o = coloff_fn(c)
                        cp(out=dst[:, o:o + 128],
                           in_=trp[:, bank * 1024 + j * 128: bank * 1024 + (j + 1) * 128])
            wn = dstw(b) if callable(dstw) else dstw
            sc.step(cpeng, cpb, reads=[f"trp{bank}"], writes=[wn], tag=f"{tag}.cp{b}")

    for T in range(NT):
        load_x(T, T, f"ph0b.x{T}")
        sc.step("V", (lambda v, T=T: emit_xn(v, T, T)),
                reads=[f"xstg{T % 2}", "mn_all", "rs_all"], writes=["xn_st"],
                tag=f"ph0b.xn{T}")
        sc.step("A", (lambda a: a.activation(out=xnbf[:, :], in_=xn_st[:, :],
                                             func=AF.Copy)),
                reads=["xn_st"], writes=["xnbf"],
                tag=f"ph0b.cast{T}")
        tr16(xnbf, xn_fm, "xn_fm", None, f"ph0b.t{T}", ["xnbf"],
             dst3d=(xn_fm[:, :].rearrange("p (c n) -> p c n", n=N), T * 128),
             cpeng="A")

    q_fm = salloc("q_fm", [128, H * N], BF16, side="right")
    k_fm = salloc("k_fm", [128, N], BF16, side="right")
    vstage = salloc("vstage", [128, N], BF16, side="right")
    v_tok = salloc("v_tok", [128, N], BF16, side="right")
    wpan = salloc("wpan", [128, 2 * E], BF16)
    qk_ps = palloc("qk_ps", [128, 2 * 512])

    for po in range(QC):
        sc.step("G", (lambda g, sem, po=po:
                      g.dma_start(out=wpan[:, (po % 2) * E:(po % 2 + 1) * E]
                                  .rearrange("p (c f) -> p c f", f=128),
                                  in_=wqkv_h[:, po * 128:(po + 1) * 128]
                                  .rearrange("(c p) f -> p c f", p=128))
                      .then_inc(sem, 16)),
                writes=[f"wpan{po % 2}"], ndma=1, tag=f"ph1.w{po}")
        for sp in range(4):
            idx = po * 4 + sp

            def mm16(t, po=po, sp=sp, idx=idx):
                last = None
                for kc in range(EC):
                    last = t.matmul(
                        qk_ps[:, (idx % 2) * 512:(idx % 2 + 1) * 512],
                        wpan[:, (po % 2) * E + kc * 128:(po % 2) * E + (kc + 1) * 128],
                        xn_fm[:, kc * N + sp * 512: kc * N + (sp + 1) * 512],
                        start=(kc == 0), stop=(kc == EC - 1))
                return last
            sc.step("T", mm16, reads=[f"wpan{po % 2}", "xn_fm"],
                    writes=[f"qk_ps{idx % 2}"], tag=f"ph1.mm{po}.{sp}")

            def evac(a, po=po, sp=sp, idx=idx):
                src = qk_ps[:, (idx % 2) * 512:(idx % 2 + 1) * 512]
                if po < 16:
                    dst = q_fm[:, po * N + sp * 512: po * N + (sp + 1) * 512]
                elif po == 16:
                    dst = k_fm[:, sp * 512:(sp + 1) * 512]
                else:
                    dst = vstage[:, sp * 512:(sp + 1) * 512]
                a.activation(out=dst, in_=src, func=AF.Identity, bias=bq_sb[:, po:po + 1])
            wres = "q_fm" if po < 16 else ("k_fm" if po == 16 else "vstage")
            sc.step("A", evac, reads=[f"qk_ps{idx % 2}", "bq_sb"], writes=[wres],
                    tag=f"ph1.ev{po}.{sp}")

    tr16(vstage, v_tok, "v_tok", (lambda c: c * 128), "ph1v", ["vstage"])

    sc.barrier("b12")
    free("qk_ps", "wpan", "xnbf", "xn_fm", "junk")

    attn_fm = salloc("attn_fm", [128, EC * N], BF16)
    probs_sb = salloc("probs_sb", [128, 2 * N], F32)
    probs_bf = salloc("probs_bf", [128, 2 * N], BF16)
    probsT = salloc("probsT", [128, 2 * N], BF16)
    rm2 = salloc("rm2", [128, 2], F32)
    nb2 = salloc("nb2", [128, 2], F32)
    rsum2 = salloc("rsum2", [128, 2], F32)
    rinv2 = salloc("rinv2", [128, 2], F32)
    sc_ps = palloc("sc_ps", [128, N])
    av_ps = palloc("av_ps", [128, 128])

    for T in range(NT):
        L = (T + 1) * 128
        nb4 = (T + 4) // 4
        for h in range(H):
            hs = h % 2

            def mm_sc(t, T=T, h=h, L=L, nb4=nb4):
                last = None
                for c in range(nb4):
                    w = min(512, L - c * 512)
                    last = t.matmul(sc_ps[:, c * 512:c * 512 + w],
                                    q_fm[:, h * N + T * 128: h * N + (T + 1) * 128],
                                    k_fm[:, c * 512:c * 512 + w],
                                    start=True, stop=True)
                return last
            sc.step("T", mm_sc, reads=["q_fm", "k_fm"], writes=["sc_ps"],
                    tag=f"ph2.sc{T}.{h}")

            def pre(v, T=T, L=L, hs=hs):
                v.tensor_add(sc_ps[:, T * 128:(T + 1) * 128],
                             sc_ps[:, T * 128:(T + 1) * 128], cmask[:, :])
                v.drain()
                v.reduce_max(rm2[:, hs:hs + 1], sc_ps[:, 0:L], axis=AX.X)
                v.drain()
                v.tensor_scalar_mul(nb2[:, hs:hs + 1], rm2[:, hs:hs + 1], -SM)
            sc.step("V", pre, reads=["sc_ps", "cmask"],
                    writes=["sc_ps", f"rm{hs}"], tag=f"ph2.pre{T}.{h}")

            def expo(a, L=L, hs=hs):
                a.activation(out=probs_sb[:, hs * N: hs * N + L], in_=sc_ps[:, 0:L],
                             func=AF.Exp, scale=SM, bias=nb2[:, hs:hs + 1],
                             accum_out=rsum2[:, hs:hs + 1])
            sc.step("A", expo, reads=["sc_ps", f"rm{hs}"],
                    writes=[f"probs{hs}", f"rsum{hs}"], tag=f"ph2.exp{T}.{h}")

            def recip(v, hs=hs):
                v.reciprocal(rinv2[:, hs:hs + 1], rsum2[:, hs:hs + 1])
            sc.step("V", recip, reads=[f"rsum{hs}"], writes=[f"rinv{hs}"],
                    tag=f"ph2.rc{T}.{h}")

            def norm(a, L=L, hs=hs):
                a.activation(out=probs_bf[:, hs * N: hs * N + L],
                             in_=probs_sb[:, hs * N: hs * N + L],
                             func=AF.Identity, scale=rinv2[:, hs:hs + 1])
            sc.step("A", norm, reads=[f"probs{hs}", f"rinv{hs}"],
                    writes=[f"probsbf{hs}"], tag=f"ph2.cast{T}.{h}")

            for b in range(nb4):
                ntr = min(4, (T + 1) - 4 * b)

                bank = b % 2

                def trb(t, b=b, ntr=ntr, bank=bank, hs=hs):
                    last = None
                    for j in range(ntr):
                        kc = 4 * b + j
                        last = t.transpose(trp[:, bank * 1024 + j * 128: bank * 1024 + (j + 1) * 128],
                                           probs_bf[:, hs * N + kc * 128: hs * N + (kc + 1) * 128],
                                           ident[:, :])
                    return last
                sc.step("T", trb, reads=[f"probsbf{hs}", "ident"], writes=[f"trp{bank}"],
                        tag=f"ph2.tr{T}.{h}.{b}")

                def cpb(v, b=b, ntr=ntr, bank=bank, hs=hs):
                    v.tensor_copy(out=probsT[:, hs * N + 4 * b * 128: hs * N + (4 * b + ntr) * 128],
                                  in_=trp[:, bank * 1024: bank * 1024 + ntr * 128])
                sc.step("V", cpb, reads=[f"trp{bank}"], writes=[f"probsT{hs}"],
                        tag=f"ph2.cp{T}.{h}.{b}")

            def mm_av(t, T=T, hs=hs):
                last = None
                for kc in range(T + 1):
                    last = t.matmul(av_ps[:, :], v_tok[:, kc * 128:(kc + 1) * 128],
                                    probsT[:, hs * N + kc * 128: hs * N + (kc + 1) * 128],
                                    start=(kc == 0), stop=(kc == T))
                return last
            sc.step("T", mm_av, reads=["v_tok", f"probsT{hs}"], writes=["av_ps"],
                    tag=f"ph2.av{T}.{h}")

            def ev_av(a, T=T, h=h):
                a.activation(out=attn_fm[:, h * N + T * 128: h * N + (T + 1) * 128],
                             in_=av_ps[:, :], func=AF.Copy)
            sc.step("A", ev_av, reads=["av_ps"], writes=["attn_fm"],
                    tag=f"ph2.ev{T}.{h}")

    sc.barrier("b23")
    free("av_ps", "sc_ps", "rinv2", "rsum2", "nb2", "rm2", "probsT", "probs_bf",
         "probs_sb", "v_tok", "vstage", "k_fm", "q_fm")

    wfc_bf = salloc("wfc_bf", [128, EC * E], BF16, side="right")
    bfc_row = salloc("bfc_row", [1, E], F32)
    ones1 = salloc("ones1", [1, 128], F32)
    bfc_bc = salloc("bfc_bc", [128, E], F32)
    ares_st = salloc("ares_st", [128, 2 * E], F32)
    fc_ps = palloc("fc_ps", [128, 2 * 512])

    sc.step("G", (lambda g, sem: g.dma_start(out=bfc_row[:, :],
                                        in_=bfc_h[:].rearrange("(o f) -> o f", o=1))
                  .then_inc(sem, 16)), writes=["bfc_row"], ndma=1, tag="ph3.b")
    sc.step("V", lambda v: v.memset(ones1[:, :], 1.0), writes=["ones1"], tag="ph3.o")
    for es in range(4):
        sc.step("T", (lambda t, es=es:
                      t.matmul(fc_ps[:, (es % 2) * 512:(es % 2 + 1) * 512],
                               ones1[:, :], bfc_row[:, es * 512:(es + 1) * 512],
                               start=True, stop=True)),
                reads=["ones1", "bfc_row"], writes=[f"fc_ps{es % 2}"],
                tag=f"ph3.bm{es}")
        sc.step("A", (lambda a, es=es:
                      a.activation(out=bfc_bc[:, es * 512:(es + 1) * 512],
                                   in_=fc_ps[:, (es % 2) * 512:(es % 2 + 1) * 512],
                                   func=AF.Copy)),
                reads=[f"fc_ps{es % 2}"], writes=["bfc_bc"], tag=f"ph3.be{es}")

    for kc in range(EC):
        sc.step("G", (lambda g, sem, kc=kc:
                      g.dma_start(out=wfc_bf[:, kc * E:(kc + 1) * E],
                                  in_=wfc_h[kc * 128:(kc + 1) * 128, :])
                      .then_inc(sem, 16)),
                writes=["wfc_bf"], ndma=1, tag=f"ph3.w{kc}")

    for T in range(NT):
        load_x(T, T, f"ph3.x{T}")
        for es in range(4):
            idx = T * 4 + es

            def mmfc(t, T=T, es=es, idx=idx):
                last = None
                for kc in range(EC):
                    last = t.matmul(
                        fc_ps[:, (idx % 2) * 512:(idx % 2 + 1) * 512],
                        attn_fm[:, kc * N + T * 128: kc * N + (T + 1) * 128],
                        wfc_bf[:, kc * E + es * 512: kc * E + (es + 1) * 512],
                        start=(kc == 0), stop=(kc == EC - 1))
                return last
            sc.step("T", mmfc, reads=["attn_fm", "wfc_bf"], writes=[f"fc_ps{idx % 2}"],
                    tag=f"ph3.mm{T}.{es}")

            def evfc(v, T=T, es=es, idx=idx):
                dst = ares_st[:, (T % 2) * E + es * 512:(T % 2) * E + (es + 1) * 512]
                v.scalar_tensor_tensor(
                    out=dst, in0=fc_ps[:, (idx % 2) * 512:(idx % 2 + 1) * 512],
                    scalar=1.0, in1=bfc_bc[:, es * 512:(es + 1) * 512],
                    op0=ALU.mult, op1=ALU.add)
                v.drain()
                v.tensor_add(dst, dst,
                             xstg[:, (T % 2) * E + es * 512:(T % 2) * E + (es + 1) * 512])
            sc.step("V", evfc, reads=[f"fc_ps{idx % 2}", "bfc_bc", f"xstg{T % 2}"],
                    writes=[f"ares{T % 2}"], tag=f"ph3.ev{T}.{es}")

        sc.step("G", (lambda g, sem, T=T:
                      g.dma_start(out=ares_h[T * 128:(T + 1) * 128, :],
                                  in_=ares_st[:, (T % 2) * E:(T % 2 + 1) * E])
                      .then_inc(sem, 16)),
                reads=[f"ares{T % 2}"], writes=[f"aresh{T}"], ndma=1, tag=f"ph3.s{T}")

    sc.barrier("b34")
    free("fc_ps", "ares_st", "bfc_bc", "ones1", "bfc_row", "wfc_bf", "attn_fm",
         "xstg")

    QN = 8
    TQ = NT // QN
    NG = N // QN
    xq_fm = salloc("xq_fm", [128, EC * NG], BF16)
    res = salloc("res", [128, TQ * E], F32)
    wgT = salloc("wgT", [128, EC * OSTRIP], BF16)
    wuT = salloc("wuT", [128, EC * OSTRIP], BF16)
    wdT = salloc("wdT", [128, 2 * 4 * E], BF16)
    ring = salloc("ring", [128, 4 * E], F32)
    qbf = salloc("qbf", [128, 2 * E], BF16)
    interall = salloc("interall", [128, 2 * TQ * 512], F32)
    sg_t = salloc("sg_t", [128, 2 * 512], F32)
    qs0A = salloc("qs0A", [128, 1024], F32)
    qs0B = salloc("qs0B", [128, 1024], F32)
    qs0C = salloc("qs0C", [128, 1024], F32)
    qs1A = salloc("qs1A", [128, 1024], F32)
    qs1B = salloc("qs1B", [128, 1024], F32)
    qs1C = salloc("qs1C", [128, 1024], F32)
    QSETS = [(qs0A, qs0B, qs0C), (qs1A, qs1B, qs1C)]
    IVO = {"g": 0, "u": 256, "d": 512, "i0": 768, "i1": 800}
    iv4 = salloc("iv4", [128, 832], F32)
    sc4 = salloc("sc4", [128, 832], F32)
    am4 = salloc("am4", [128, 832], F32)
    chA = salloc("chA", [128, 256], F32)
    chD = salloc("chD", [128, 64], F32)
    intq4 = salloc("intq4", [128, TQ * 512], BF16)
    intT2 = salloc("intT2", [128, 2 * 512], BF16)
    xn_stB = salloc("xn_stB", [128, E], F32)
    _QP = [0]

    def q_body(jobs, fmt, tag):
        tinv = 8.0 if fmt == "act" else 2.0
        mn = 7.5 if fmt == "act" else 6.0
        t2 = tinv * tinv
        units = []
        for (srcf, srcr, dstf, dstr, ivo, ivr, extra) in jobs:
            F = srcf.shape[-1]
            for hh in range(0, F, 1024):
                Fh = min(1024, F - hh)
                p = _QP[0] % 2
                _QP[0] += 1
                units.append((srcf[:, hh:hh + Fh], list(srcr),
                              dstf[:, hh:hh + Fh], list(dstr),
                              ivo + hh // 32, ivr, list(extra), Fh, p))

        def e_p1(u, k):
            srch, srcr, dsth, dstr, ivo, ivr, extra, Fh, p = u
            nb = Fh // 32
            qA = QSETS[p][0]
            ivb = iv4[:, ivo:ivo + nb].rearrange("p (b o) -> p b o", o=1) \
                .broadcast_to([128, nb, 32])

            def f(g, A3=qA[:, 0:Fh].rearrange("p (b k) -> p b k", k=32),
                  s3=srch.rearrange("p (b k) -> p b k", k=32), ivb=ivb):
                g.tensor_tensor(out=A3, in0=s3, in1=ivb, op=ALU.mult)
            sc.step("G", f, reads=srcr + [f"iv4{ivr}"], writes=[f"qs{p}"],
                    tag=f"{tag}.p1.{k}")

        def e_v1(u, k):
            srch, srcr, dsth, dstr, ivo, ivr, extra, Fh, p = u
            qA, qB, _ = QSETS[p]

            def f(v, A=qA[:, 0:Fh], B=qB[:, 0:Fh]):
                v._custom_dve(_MXQ["MXQ_IK"], out=B, in0=A, s0=4.0 * t2,
                              s1=16.0 * t2, imm2=0.5)
            sc.step("V", f, reads=[f"qs{p}"], writes=[f"qs{p}"],
                    tag=f"{tag}.v1.{k}")

        def e_p2(u, k):
            srch, srcr, dsth, dstr, ivo, ivr, extra, Fh, p = u
            qA, qB, qC = QSETS[p]

            def f(g, A=qA[:, 0:Fh], B=qB[:, 0:Fh], Cb=qC[:, 0:Fh]):
                g.tensor_mul(Cb, A, B)
            sc.step("G", f, reads=[f"qs{p}"], writes=[f"qs{p}"],
                    tag=f"{tag}.p2.{k}")

        def e_v23(u, k):
            srch, srcr, dsth, dstr, ivo, ivr, extra, Fh, p = u
            nb = Fh // 32
            qA, qB, qC = QSETS[p]
            scb = sc4[:, ivo:ivo + nb].rearrange("p (b o) -> p b o", o=1) \
                .broadcast_to([128, nb, 32])

            def f(v, A=qA[:, 0:Fh], B=qB[:, 0:Fh], Cb=qC[:, 0:Fh],
                  d3=dsth.rearrange("p (b k) -> p b k", k=32),
                  c3=qC[:, 0:Fh].rearrange("p (b k) -> p b k", k=32), scb=scb):
                if fmt == "act":
                    v._custom_dve(_MXQ["MXQ_EX3"], out=B, in0=Cb, in1=sh16[:, :],
                                  s0=16.0, s1=8.0)
                    v.drain()
                    v._custom_dve(_MXQ["MXQ_EX3"], out=B, in0=B, in1=shift0[:, :],
                                  s0=4.0, s1=2.0)
                else:
                    v._custom_dve(_MXQ["MXQ_EX3"], out=B, in0=Cb, in1=sh4[:, :],
                                  s0=4.0, s1=2.0)
                v.drain()
                v._custom_dve(_MXQ["MXQ_TIE"], out=B, in0=B, in1=Cb, s1=0.5)
                v.drain()
                v._custom_dve(_MXQ["MXQ_UNF"], out=Cb, in0=A, in1=B,
                              s0=4.0 * t2, s1=16.0 * t2, imm2=mn * tinv)
                v.drain()
                v.scalar_tensor_tensor(out=d3, in0=c3, scalar=-mn * tinv,
                                       op0=ALU.max, in1=scb, op1=ALU.mult)
            sc.step("V", f, reads=[f"qs{p}", "shifts", f"sc4{ivr}"],
                    writes=[f"qs{p}"] + dstr + extra, tag=f"{tag}.v23.{k}")

        for b0 in range(0, len(units), 2):
            pr = list(enumerate(units[b0:b0 + 2], b0))
            for k, u in pr:
                e_p1(u, k)
            for k, u in pr:
                e_v1(u, k)
            for k, u in pr:
                e_p2(u, k)
            for k, u in pr:
                e_v23(u, k)

    g_ps = palloc("g_ps", [128, 2 * 512])
    u_ps = palloc("u_ps", [128, 2 * 512])
    dn_ps = palloc("dn_ps", [128, 2 * 512])

    def rg(i):
        return ring[:, i * E:(i + 1) * E]

    QBS = [[f"qbfs{4 * h + kk}" for kk in range(4)] for h in range(2)]

    def wdTs(p):
        return wdT[:, p * 4 * E:(p + 1) * 4 * E]

    def wdT3(p):
        return wdTs(p).rearrange("p (oc es e) -> p es oc e", es=4, e=512)

    for qq in range(QN):
        T0 = qq * TQ
        for i in range(TQ):
            T = T0 + i
            lnbuf = xn_st if i % 2 == 0 else xn_stB
            sc.step("S", (lambda s, sem, T=T, i=i:
                          s.dma_start(out=rg(i % 2), in_=x_h[T * 128:(T + 1) * 128, :])
                          .then_inc(sem, 16)),
                    writes=[f"ring{i % 2}"], ndma=1, tag=f"p4a{qq}.x{i}")

            def am(v, T=T, i=i, lnbuf=lnbuf):
                v.tensor_scalar(out=lnbuf[:, :], in0=rg(i % 2),
                                scalar1=mn_all[:, T:T + 1], scalar2=rs_all[:, T:T + 1],
                                op0=ALU.subtract, op1=ALU.mult)
                v.drain()
                v.tensor_reduce(out=am4[:, i * 64:(i + 1) * 64],
                                in_=lnbuf[:, :].rearrange("p (b k) -> p b k", k=32),
                                axis=AX.X, op=ALU.max, apply_absolute_value=True)
            sc.step("V", am, reads=[f"ring{i % 2}", "mn_all", "rs_all"],
                    writes=[f"xnst{i % 2}", "am4g"], tag=f"p4a{qq}.am{i}")

        sc.step("V", (lambda v: emit_scale_chain(v, am4[:, 0:TQ * 64],
                                                 chA[:, 0:TQ * 64],
                                                 iv4[:, 0:TQ * 64], sc4[:, 0:TQ * 64],
                                                 24, CH_STD, post=8.0)),
                reads=["am4g"], writes=["iv4g", "sc4g", "chsc"],
                tag=f"p4a{qq}.ch")

        jobs = []
        for i in range(TQ):
            T = T0 + i
            lnbuf = xn_st if i % 2 == 0 else xn_stB
            sc.step("S", (lambda s, sem, T=T, i=i:
                          s.dma_start(out=rg(i % 2), in_=x_h[T * 128:(T + 1) * 128, :])
                          .then_inc(sem, 16)),
                    writes=[f"ring{i % 2}"], ndma=1, tag=f"p4b{qq}.x{i}")

            def qx(v, T=T, i=i, lnbuf=lnbuf):
                v.tensor_scalar(out=lnbuf[:, :], in0=rg(i % 2),
                                scalar1=mn_all[:, T:T + 1],
                                scalar2=rs_all[:, T:T + 1],
                                op0=ALU.subtract, op1=ALU.mult)
            sc.step("V", qx, reads=[f"ring{i % 2}", "mn_all", "rs_all"],
                    writes=[f"xnst{i % 2}"], tag=f"p4b{qq}.ln{i}")
            jobs.append((lnbuf, [f"xnst{i % 2}"],
                         qbf[:, (i % 2) * E:(i % 2 + 1) * E], QBS[i % 2],
                         i * 64, "g", []))
        q_body(jobs, "act", f"p4b{qq}.q")
        for i in range(TQ):
            tr16(qbf[:, (i % 2) * E:(i % 2 + 1) * E], xq_fm, "xq_fm", None,
                 f"p4b{qq}.t{i}", QBS[i % 2],
                 dst3d=(xq_fm[:, :].rearrange("p (c n) -> p c n", n=NG),
                        i * 128), cpeng="A")
        for i in range(TQ):
            sc.step("V", (lambda v, i=i: v.memset(res[:, i * E:(i + 1) * E], 0.0)),
                    writes=[f"res{i}"], tag=f"p4b{qq}.z{i}")

        def stage_weights(s_i):
            o0 = s_i * OSTRIP
            if qq == 0:
                for m in range(2):
                    srcw = wg_h if m == 0 else wu_h
                    dstc = wgt_h if m == 0 else wut_h
                    dstT = wgT if m == 0 else wuT
                    pfx = "wgTq" if m == 0 else "wuTq"
                    reg = "g" if m == 0 else "u"
                    v0 = IVO[reg]
                    for it in range(4):
                        sc.step("S", (lambda s, sem, srcw=srcw, o0=o0, it=it:
                                      s.dma_start(out=rg(it),
                                                  in_=srcw[o0 + it * 128:o0 + (it + 1) * 128, :])
                                      .then_inc(sem, 16)),
                                writes=[f"ring{it}"], ndma=1, tag=f"w{m}l{s_i}.{it}")
                        sc.step("V", (lambda v, it=it, v0=v0:
                                      v.tensor_reduce(out=am4[:, v0 + it * 64:v0 + (it + 1) * 64],
                                                      in_=rg(it).rearrange("p (b k) -> p b k", k=32),
                                                      axis=AX.X, op=ALU.max,
                                                      apply_absolute_value=True)),
                                reads=[f"ring{it}"], writes=[f"am4{reg}"], tag=f"w{m}a{s_i}.{it}")
                    sc.step("V", (lambda v, v0=v0: emit_scale_chain(
                                v, am4[:, v0:v0 + 256], chA[:, :],
                                iv4[:, v0:v0 + 256], sc4[:, v0:v0 + 256],
                                24, CH_STD, post=2.0)),
                            reads=[f"am4{reg}"], writes=[f"iv4{reg}", f"sc4{reg}", "chsc"],
                            tag=f"w{m}c{s_i}")
                    for itp in (0, 2):
                        q_body([(rg(it), [f"ring{it}"],
                                 qbf[:, (it % 2) * E:(it % 2 + 1) * E], QBS[it % 2],
                                 v0 + it * 64, reg, []) for it in (itp, itp + 1)],
                               "wgt", f"w{m}q{s_i}.{itp}")
                        for it in (itp, itp + 1):
                            tr16(qbf[:, (it % 2) * E:(it % 2 + 1) * E], dstT,
                                 (lambda b, pfx=pfx: f"{pfx}{b}"),
                                 None, f"w{m}t{s_i}.{it}", QBS[it % 2],
                                 dst3d=(dstT[:, :].rearrange("p (c n) -> p c n", n=OSTRIP),
                                        it * 128), cpeng="A")
                    for hcq in range(4):
                        sc.step("G", (lambda g, sem, dstc=dstc, s_i=s_i, hcq=hcq, dstT=dstT:
                                      g.dma_start(out=dstc[s_i, :, hcq * E:(hcq + 1) * E],
                                                  in_=dstT[:, hcq * E:(hcq + 1) * E])
                                      .then_inc(sem, 16)),
                                reads=[f"{pfx}{hcq}"], writes=[f"c{m}.{s_i}.{hcq}"],
                                ndma=1, tag=f"w{m}cw{s_i}.{hcq}")
                for j in range(16):
                    sc.step("S", (lambda s, sem, o0=o0, j=j:
                                  s.dma_start(out=ring[:, j * 512:(j + 1) * 512],
                                              in_=wd_h[j * 128:(j + 1) * 128, o0:o0 + OSTRIP])
                                  .then_inc(sem, 16)),
                            writes=[f"ring{j // 4}"], ndma=1, tag=f"wdl{s_i}.{j}")
                    sc.step("V", (lambda v, j=j:
                                  v.tensor_reduce(out=am4[:, 512 + j * 16:512 + (j + 1) * 16],
                                                  in_=ring[:, j * 512:(j + 1) * 512]
                                                  .rearrange("p (b k) -> p b k", k=32),
                                                  axis=AX.X, op=ALU.max,
                                                  apply_absolute_value=True)),
                            reads=[f"ring{j // 4}"], writes=["am4d"], tag=f"wda{s_i}.{j}")
                sc.step("V", (lambda v: emit_scale_chain(v, am4[:, 512:768], chA[:, :],
                                                         iv4[:, 512:768],
                                                         sc4[:, 512:768],
                                                         24, CH_STD, post=2.0)),
                        reads=["am4d"], writes=["iv4d", "sc4d", "chsc"],
                        tag=f"wdc{s_i}")
                for jb in (0, 8):
                    q_body([(ring[:, j * 512:(j + 2) * 512],
                             [f"ring{j // 4}", f"ring{(j + 1) // 4}"],
                             qbf[:, (j % 8) * 512:(j % 8 + 2) * 512],
                             [f"qbfs{j % 8}", f"qbfs{j % 8 + 1}"],
                             512 + j * 16, "d",
                             [f"wdq{s_i}.{j}", f"wdq{s_i}.{j + 1}"])
                            for j in range(jb, jb + 8, 2)],
                           "wgt", f"wdq{s_i}.{jb}")
                    for j in range(jb, jb + 8):
                        bank = j % 2

                        def trd(t, j=j, bank=bank):
                            last = None
                            for oc in range(4):
                                last = t.transpose(
                                    trp[:, bank * 1024 + oc * 128: bank * 1024 + (oc + 1) * 128],
                                    qbf[:, (j % 8) * 512 + oc * 128:(j % 8) * 512 + (oc + 1) * 128],
                                    ident[:, :])
                            return last
                        sc.step("T", trd, reads=[f"wdq{s_i}.{j}", f"qbfs{j % 8}",
                                                 "ident"], writes=[f"trp{bank}"],
                                tag=f"wdt{s_i}.{j}")

                        def cpd(a, j=j, bank=bank, p=s_i % 2):
                            a.copy(out=wdTs(p).rearrange("p (oc e) -> p oc e", e=E)
                                   [:, :, j * 128:(j + 1) * 128],
                                   in_=trp[:, bank * 1024: bank * 1024 + 512]
                                   .rearrange("p (oc f) -> p oc f", f=128))
                        sc.step("A", cpd, reads=[f"trp{bank}"],
                                writes=[f"wdTe{s_i % 2}.{j // 4}"],
                                tag=f"wdcp{s_i}.{j}")
                for es in range(4):
                    sc.step("G", (lambda g, sem, s_i=s_i, es=es:
                                  g.dma_start(out=wdt_h[s_i, es, :, :],
                                              in_=wdT3(s_i % 2)[:, es])
                                  .then_inc(sem, 16)),
                            reads=[f"wdTe{s_i % 2}.{es}"], writes=[f"cd.{s_i}.{es}"],
                            ndma=1, tag=f"wdcw{s_i}.{es}")
            else:
                for m in range(2):
                    srcc = wgt_h if m == 0 else wut_h
                    dstT = wgT if m == 0 else wuT
                    pfx = "wgTq" if m == 0 else "wuTq"
                    for hcq in range(4):
                        sc.step("S", (lambda s, sem, srcc=srcc, s_i=s_i, hcq=hcq, dstT=dstT:
                                      s.dma_start(out=dstT[:, hcq * E:(hcq + 1) * E],
                                                  in_=srcc[s_i, :, hcq * E:(hcq + 1) * E])
                                      .then_inc(sem, 16)),
                                reads=[f"c{m}.{s_i}.{hcq}"], writes=[f"{pfx}{hcq}"],
                                ndma=1, tag=f"w{m}lb{s_i}.{hcq}")
                for es in range(4):
                    sc.step("S", (lambda s, sem, s_i=s_i, es=es:
                                  s.dma_start(out=wdT3(s_i % 2)[:, es],
                                              in_=wdt_h[s_i, es, :, :])
                                  .then_inc(sem, 16)),
                            reads=[f"cd.{s_i}.{es}"], writes=[f"wdTe{s_i % 2}.{es}"],
                            ndma=1, tag=f"wdlb{s_i}.{es}")

        def pass1_T(s_i):
            for m in range(2):
                dstT = wgT if m == 0 else wuT
                pfx = "wgTq" if m == 0 else "wuTq"
                ps = g_ps if m == 0 else u_ps
                pn = "g_ps" if m == 0 else "u_ps"
                for hcq in range(4):
                    def mmq(t, hcq=hcq, dstT=dstT, ps=ps):
                        last = None
                        for tt in range(TQ):
                            for hc in range(4 * hcq, 4 * hcq + 4):
                                last = t.matmul(
                                    ps[:, tt * 512:(tt + 1) * 512],
                                    xq_fm[:, hc * NG + tt * 128: hc * NG + (tt + 1) * 128],
                                    dstT[:, hc * OSTRIP:(hc + 1) * OSTRIP],
                                    start=(hc == 0), stop=(hc == EC - 1))
                        return last
                    sc.step("T", mmq, reads=["xq_fm", f"{pfx}{hcq}"], writes=[pn],
                            tag=f"ff{m}{qq}.{s_i}.{hcq}")
            for tt in range(TQ):
                sc.step("A", (lambda a, tt=tt:
                              a.activation(out=sg_t[:, tt * 512:(tt + 1) * 512],
                                           in_=g_ps[:, tt * 512:(tt + 1) * 512],
                                           func=AF.Silu)),
                        reads=["g_ps"], writes=[f"sg{tt}"], tag=f"ffs{qq}.{s_i}.{tt}")

        def pass1_V(s_i):
            p = s_i % 2
            i0 = IVO[f"i{p}"]
            for tt in range(TQ):
                def mul(v, tt=tt, p=p, i0=i0):
                    o0i = (p * TQ + tt) * 512
                    v.tensor_mul(interall[:, o0i:o0i + 512],
                                 sg_t[:, tt * 512:(tt + 1) * 512],
                                 u_ps[:, tt * 512:(tt + 1) * 512])
                    v.drain()
                    v.tensor_reduce(out=am4[:, i0 + tt * 16: i0 + (tt + 1) * 16],
                                    in_=interall[:, o0i:o0i + 512]
                                    .rearrange("p (b k) -> p b k", k=32),
                                    axis=AX.X, op=ALU.max, apply_absolute_value=True)
                sc.step("V", mul, reads=[f"sg{tt}", "u_ps"],
                        writes=[f"inter{p}.{tt}", f"am4i{p}"], tag=f"ffm{qq}.{s_i}.{tt}")

            sc.step("V", (lambda v, i0=i0: emit_scale_chain(
                        v, am4[:, i0:i0 + TQ * 16], chD[:, 0:TQ * 16],
                        iv4[:, i0:i0 + TQ * 16], sc4[:, i0:i0 + TQ * 16],
                        43, CH_INT, post=8.0)),
                    reads=[f"am4i{p}"], writes=[f"iv4i{p}", f"sc4i{p}", "chsci"],
                    tag=f"ffc{qq}.{s_i}")

        def pass2(s_i):
            p = s_i % 2
            i0 = IVO[f"i{p}"]
            q_body([(interall[:, (p * TQ + tt) * 512:(p * TQ + tt + 1) * 512],
                     [f"inter{p}.{tt}"],
                     intq4[:, tt * 512:(tt + 1) * 512], [f"intq{tt}"],
                     i0 + tt * 16, f"i{p}", []) for tt in range(TQ)],
                   "act", f"ffq{qq}.{s_i}")
            for tt in range(TQ):
                bank = tt % 2

                def tri(t, tt=tt, bank=bank):
                    last = None
                    for oc in range(4):
                        last = t.transpose(trp[:, bank * 1024 + oc * 128: bank * 1024 + (oc + 1) * 128],
                                           intq4[:, tt * 512 + oc * 128: tt * 512 + (oc + 1) * 128],
                                           ident[:, :])
                    return last
                sc.step("T", tri, reads=[f"intq{tt}", "ident"], writes=[f"trp{bank}"],
                        tag=f"fft{qq}.{s_i}.{tt}")
                sc.step("A", (lambda a, tt=tt, bank=bank:
                              a.copy(out=intT2[:, tt * 512:(tt + 1) * 512],
                                     in_=trp[:, bank * 1024: bank * 1024 + 512])),
                        reads=[f"trp{bank}"], writes=[f"intT{tt}"],
                        tag=f"ffcp{qq}.{s_i}.{tt}")

            for es in range(4):
                for tt in range(TQ):
                    def mmd(t, es=es, tt=tt, p=p):
                        last = None
                        for oc in range(4):
                            last = t.matmul(
                                dn_ps[:, (tt % 2) * 512:(tt % 2 + 1) * 512],
                                intT2[:, tt * 512 + oc * 128: tt * 512 + (oc + 1) * 128],
                                wdTs(p)[:, oc * E + es * 512: oc * E + (es + 1) * 512],
                                start=(oc == 0), stop=(oc == 3))
                        return last
                    sc.step("T", mmd, reads=[f"intT{tt}", f"wdTe{p}.{es}"],
                            writes=[f"dn_ps{tt % 2}"],
                            tag=f"ffd{qq}.{s_i}.{es}.{tt}")

                    def acc(v, tt=tt, es=es):
                        dst = res[:, tt * E + es * 512: tt * E + (es + 1) * 512]
                        v.tensor_add(dst, dst,
                                     dn_ps[:, (tt % 2) * 512:(tt % 2 + 1) * 512])
                    sc.step("V", acc, reads=[f"dn_ps{tt % 2}"], writes=[f"res{tt}"],
                            tag=f"ffa{qq}.{s_i}.{es}.{tt}")

        for s_i in range(NSTRIP + 1):
            if s_i < NSTRIP:
                stage_weights(s_i)
                pass1_T(s_i)
            if s_i > 0:
                pass2(s_i - 1)
            if s_i < NSTRIP:
                pass1_V(s_i)

        for tt in range(TQ):
            T = T0 + tt
            sc.step("G", (lambda g, sem, T=T: g.dma_start(out=rg(0),
                                                     in_=ares_h[T * 128:(T + 1) * 128, :])
                          .then_inc(sem, 16)),
                    reads=[f"aresh{T}"], writes=["ring0"], ndma=1, tag=f"fin{qq}.l{tt}")
            sc.step("V", (lambda v, tt=tt: v.tensor_add(rg(1), rg(0),
                                                        res[:, tt * E:(tt + 1) * E])),
                    reads=["ring0", f"res{tt}"], writes=["ring1"], tag=f"fin{qq}.a{tt}")
            sc.step("G", (lambda g, sem, T=T: g.dma_start(out=out_h[T * 128:(T + 1) * 128, :],
                                                     in_=rg(1))
                          .then_inc(sem, 16)),
                    reads=["ring1"], writes=[], ndma=1, tag=f"fin{qq}.s{tt}")
        if qq < QN - 1:
            sc.barrier(f"bq{qq}")

    blk = nc.Block()
    block = blk.__enter__()
    sc.emit(nc, block, SEMS)
    blk.__exit__(None, None, None)
    for g in semg:
        g.__exit__(None, None, None)
    for nm in reversed(list(live)):
        live.pop(nm).__exit__(None, None, None)
    mybir.codegen_inst_isa_subclasses(nc)
    return nc


_NC_CACHE = []


def kernel(L0_attn_x, L0_attn_w_qkv, L0_attn_b_qkv, L0_attn_w_fc, L0_attn_b_fc,
           L0_ffn_w_gate, L0_ffn_w_up, L0_ffn_w_down):
    import ml_dtypes
    if not _NC_CACHE:
        _NC_CACHE.append(build_program())
    nc = _NC_CACHE[0]
    x = np.ascontiguousarray(np.asarray(L0_attn_x, dtype=np.float32))
    wqkv = np.ascontiguousarray(np.asarray(L0_attn_w_qkv, dtype=np.float32))
    bqkv = np.ascontiguousarray(np.asarray(L0_attn_b_qkv, dtype=np.float32))
    wfc = np.ascontiguousarray(np.asarray(L0_attn_w_fc, dtype=np.float32))
    bfc = np.ascontiguousarray(np.asarray(L0_attn_b_fc, dtype=np.float32))
    wg = np.ascontiguousarray(np.asarray(L0_ffn_w_gate, dtype=np.float32))
    wu = np.ascontiguousarray(np.asarray(L0_ffn_w_up, dtype=np.float32))
    wd = np.ascontiguousarray(np.asarray(L0_ffn_w_down, dtype=np.float32))
    ident = np.eye(128, dtype=ml_dtypes.bfloat16)
    cmask = np.triu(np.full((128, 128), -1.0e30, np.float32), k=1)
    in_maps = []
    for b in range(8):
        in_maps.append({
            "x": np.ascontiguousarray(x[b]),
            "w_qkv": wqkv, "b_qkv": bqkv, "w_fc": wfc, "b_fc": bfc,
            "w_gate": wg, "w_up": wu, "w_down": wd,
            "ident": ident, "cmask": cmask,
        })
    res = run_bass_kernel_spmd(nc, in_maps, core_ids=list(range(8)))
    out = np.stack([np.asarray(r["out"], dtype=np.float32) for r in res.results], axis=0)
    return out
